# version 01
# baseline (speedup 1.0000x reference)
import dataclasses
import math
from contextlib import ExitStack

import numpy as np
import concourse.bass as bass
import concourse.mybir as mybir
from concourse.bass_utils import run_bass_kernel_spmd

F32 = mybir.dt.float32
BF16 = mybir.dt.bfloat16
AF = mybir.ActivationFunctionType
ALU = mybir.AluOpType
AX = mybir.AxisListType

B, T, E, H, DH = 4, 4096, 1024, 16, 64
NEXP, HS = 32, 512
NT, NG, SGRP = 16, 4, 8
EPS = 1e-5
NEGBIG = -1e30
DEBUG_STOP_AT_X1 = False
DEBUG_FFN_RAW = False
DEBUG_FFN_W = False
DEBUG_NBLK = 4
DEBUG_NO_SHARED = False


def mkap(handle, off, pattern):
    a = handle[:, :] if len(handle.shape) == 2 else handle[:]
    return dataclasses.replace(a, offset=a.offset + off, ap=pattern)


class Sched:
    ENGS = ("s", "g", "t", "a", "v")

    def __init__(self, nc):
        self.nc = nc
        self.q = {e: [] for e in self.ENGS}
        self.sems = {}
        self.cnt = {}
        self.floor = {}

    def sem(self, ctx, name):
        self.sems[name] = ctx.enter_context(self.nc.semaphore(name))
        self.cnt[name] = 0

    def set_floor(self):
        self.floor = dict(self.cnt)

    def emit(self, eng, fn, waits=(), inc=None, amt=1):
        ws = dict(self.floor)
        for w in waits:
            if w is None:
                continue
            sname, val = w
            assert val <= self.cnt[sname], (sname, val, self.cnt[sname])
            if ws.get(sname, 0) < val:
                ws[sname] = val
        evt = None
        if inc is not None:
            self.cnt[inc] += amt
            evt = (inc, self.cnt[inc])
        self.q[eng].append((fn, tuple(ws.items()), (inc, amt) if inc else None))
        return evt

    def dma(self, eng, out, in_, waits=(), sem=None):
        sem = sem or {"s": "dS", "a": "dA", "g": "dG"}[eng]
        return self.emit(eng, lambda h, o=out, i=in_: h.dma_start(out=o, in_=i),
                         waits=waits, inc=sem, amt=16)

    def drain(self, eng):
        self.emit(eng, lambda h: h.drain())

    def replay(self, block):
        decos = {"s": block.sync, "g": block.gpsimd, "t": block.tensor,
                 "a": block.scalar, "v": block.vector}
        for ek in self.ENGS:
            ops = self.q[ek]
            if not ops:
                continue
            def body(h, ops=ops):
                hwm = {}
                for fn, waits, inc in ops:
                    for sname, val in waits:
                        if hwm.get(sname, 0) < val:
                            h.wait_ge(self.sems[sname], val)
                            hwm[sname] = val
                    ins = fn(h)
                    if inc is not None and ins is not None:
                        ins.then_inc(self.sems[inc[0]], inc[1])
            decos[ek](body)


def build_program():
    nc = bass.Bass("TRN2", target_bir_lowering=False)
    dt = nc.dram_tensor
    xseq = dt("xseq", [T, E], F32, kind="ExternalInput").ap()
    xown = dt("xown", [NT * 128, E], F32, kind="ExternalInput").ap()
    maskp_d = dt("maskp", [128, 256], F32, kind="ExternalInput").ap()
    bias_d = dt("biastab", [128, 512], F32, kind="ExternalInput").ap()
    ident_d = dt("identc", [128, 128], F32, kind="ExternalInput").ap()
    ishift_d = dt("ishift", [64, 256], F32, kind="ExternalInput").ap()
    ones_d = dt("onesc", [128, 128], F32, kind="ExternalInput").ap()
    scal_d = dt("scal", [128, 4], F32, kind="ExternalInput").ap()
    wq_h = dt("wq", [128, 8 * 1024], F32, kind="ExternalInput").ap()
    wk_h = dt("wk", [128, 8 * 1024], F32, kind="ExternalInput").ap()
    wv_h = dt("wv", [128, 8 * 1024], F32, kind="ExternalInput").ap()
    wp_h = dt("wp", [128, 8 * 1024], F32, kind="ExternalInput").ap()
    bqkv_h = dt("bqkv", [128, 8 * 128], F32, kind="ExternalInput").ap()
    bq_h = dt("bq", [128, 8 * 8], F32, kind="ExternalInput").ap()
    at_h = dt("at", [128, 1024], F32, kind="ExternalInput").ap()
    rw_h = dt("rw", [128, 8 * 32], F32, kind="ExternalInput").ap()
    rb_h = dt("rb", [1, 32], F32, kind="ExternalInput").ap()
    gw_h = dt("gw", [NEXP, 128, 1024], F32, kind="ExternalInput").ap()
    uw_h = dt("uw", [NEXP, 128, 1024], F32, kind="ExternalInput").ap()
    dw_h = dt("dw", [NEXP, 128, 1024], F32, kind="ExternalInput").ap()
    sg_h = dt("sg", [128, 8 * 512], F32, kind="ExternalInput").ap()
    su_h = dt("su", [128, 8 * 512], F32, kind="ExternalInput").ap()
    sd_h = dt("sd", [128, 4 * 1024], F32, kind="ExternalInput").ap()
    out_d = dt("out", [NT * 128, E], F32, kind="ExternalOutput").ap()
    l1s_d = dt("l1scr", [SGRP, 128, 4096], BF16, kind="Internal").ap()
    l1o_d = dt("l1oscr", [NG, 128, 4096], BF16, kind="Internal").ap()
    x1_d = dt("x1scr", [NT, 128, 1024], F32, kind="Internal").ap()
    wg_d = dt("wgscr", [2, 32, 1024], BF16, kind="Internal").ap()

    octx = ExitStack()
    S = Sched(nc)
    for s in ("dS", "dA", "dG", "P", "A", "V", "O"):
        S.sem(octx, s)

    def SB(ctx, n, shp, d=F32):
        return ctx.enter_context(nc.sbuf_tensor(n, shp, d))

    ident_b = SB(octx, "ident_b", [128, 128], BF16)
    ishift_b = SB(octx, "ishift_b", [64, 256], BF16)
    ones_f = SB(octx, "ones_f", [128, 128], F32)
    ones_b = SB(octx, "ones_b", [128, 128], BF16)
    maskp_s = SB(octx, "maskp_s", [128, 256], F32)
    bias_s = SB(octx, "bias_s", [128, 512], F32)
    rb_s = SB(octx, "rb_s", [1, 32], F32)
    scal_s = SB(octx, "scal_s", [128, 4], F32)
    at_b = SB(octx, "at_b", [128, 1024], BF16)
    bqkv_b = SB(octx, "bqkv_b", [128, 8 * 128], BF16)
    bq_b = SB(octx, "bq_b", [128, 8 * 8], BF16)
    xt = [SB(octx, "xt0", [128, 1024]), SB(octx, "xt1", [128, 1024])]
    lnb = [SB(octx, "lnb0", [128, 1024], BF16), SB(octx, "lnb1", [128, 1024], BF16)]
    tmp_f = SB(octx, "tmpf", [128, 1024])
    lnscr = [dict(bn6=SB(octx, f"bn6_{i}", [128, 12]), mv=SB(octx, f"mv_{i}", [128, 2]),
                  sd=SB(octx, f"sd_{i}", [128, 1]), rs=SB(octx, f"rs_{i}", [128, 1]),
                  busy_v=None, busy_a=None) for i in range(2)]
    pA = octx.enter_context(nc.psum_tensor("pA", [128, 1024], F32))
    pB = octx.enter_context(nc.psum_tensor("pB", [128, 1024], F32))
    pC = octx.enter_context(nc.psum_tensor("pC", [128, 1024], F32))
    pT = octx.enter_context(nc.psum_tensor("pT", [128, 2048], BF16))

    ev_c = {}
    for key, dst, src in [("ident", ident_b, ident_d), ("ishift", ishift_b, ishift_d),
                          ("onesb", ones_b, ones_d), ("at", at_b, at_h),
                          ("bqkv", bqkv_b, bqkv_h), ("bq", bq_b, bq_h)]:
        ev_c[key] = S.dma("g", dst[:, :], src)
    for key, dst, src in [("ones", ones_f, ones_d), ("maskp", maskp_s, maskp_d),
                          ("bias", bias_s, bias_d), ("rb", rb_s, rb_h),
                          ("scal", scal_s, scal_d)]:
        ev_c[key] = S.dma("s", dst[:, :], src)

    xt_free = [None, None]
    lnb_free = [None, None]
    pT_free = [None, None]
    ln_i = [0]

    def emit_ln(x_tensor, x_off, out_ap, x_evt, out_wait=()):
        x_ap = x_tensor[:, x_off:x_off + 1024]
        sc = lnscr[ln_i[0] % 2]
        ln_i[0] += 1
        S.emit("v", lambda h, a=sc["bn6"], b=x_tensor, o=x_off: h.bn_stats(
            a[:, 0:6], b[:, o:o + 512]),
            waits=[x_evt, sc["busy_v"], sc["busy_a"]])
        S.emit("v", lambda h, a=sc["bn6"], b=x_tensor, o=x_off: h.bn_stats(
            a[:, 6:12], b[:, o + 512:o + 1024]))
        S.drain("v")
        ev_mv = S.emit("v", lambda h, a=sc["mv"], b=sc["bn6"]: h.bn_aggr(a[:, :], b[:, :]), inc="V")
        ev_sd = S.emit("a", lambda h, a=sc["sd"], b=sc["mv"]: h.activation(
            out=a[:, :], in_=b[:, 1:2], func=AF.Sqrt, bias=scal_s[:, 0:1]),
            waits=[ev_mv, ev_c["scal"]], inc="A")
        S.emit("v", lambda h, a=sc["rs"], b=sc["sd"]: h.reciprocal(a[:, :], b[:, :]),
               waits=[ev_sd])
        S.drain("v")
        ev = S.emit("v", lambda h, o=out_ap, x=x_ap, m=sc["mv"], r=sc["rs"]: h.tensor_scalar(
            out=o, in0=x, scalar1=m[:, 0:1], scalar2=r[:, 0:1],
            op0=ALU.subtract, op1=ALU.mult), waits=list(out_wait), inc="V")
        sc["busy_v"] = ev
        sc["busy_a"] = ev_sd
        return ev

    def emit_ln_transpose(src_dram_row, dst_tensor, dst_off):
        i = ln_i[0] % 2
        ev_x = S.dma("a", xt[i][:, :], src_dram_row, waits=[xt_free[i]])
        ev_ln = emit_ln(xt[i], 0, lnb[i][:, :], ev_x, out_wait=[lnb_free[i]])
        xt_free[i] = ev_ln
        th = ln_i[0] % 2
        ev_tr = None
        for c in range(8):
            w = [ev_ln, ev_c["ident"]] + ([pT_free[th]] if c == 0 else [])
            ev_tr = S.emit("t", lambda h, c=c, i=i, th=th: h.transpose(
                pT[:, th * 1024 + c * 128: th * 1024 + (c + 1) * 128],
                in_=lnb[i][:, c * 128:(c + 1) * 128], identity=ident_b[:, :]),
                waits=w, inc="P")
        lnb_free[i] = ev_tr
        ev_cp = S.emit("a", lambda h, th=th, dst=mkap(dst_tensor, dst_off,
                       [[dst_tensor.shape[1], 128],
                        [dst_tensor.shape[1] // 8, 8], [1, 128]]): h.activation(
            out=dst, in_=pT[:, th * 1024:(th + 1) * 1024].rearrange("p (c n) -> p c n", n=128),
            func=AF.Copy), waits=[ev_tr], inc="A")
        pT_free[th] = ev_cp
        return ev_cp

    ab_ctx = ExitStack()
    O_sb = SB(ab_ctx, "O_sb", [128, 8 * 2048], BF16)
    a_ctx = ExitStack()
    t1kv = SB(a_ctx, "t1kv", [128, SGRP * 512], BF16)
    t1q = SB(a_ctx, "t1q", [8, NG * 512], BF16)
    l1t = [SB(a_ctx, "l1t0", [128, 4096], BF16), SB(a_ctx, "l1t1", [128, 4096], BF16)]
    wq_s = SB(a_ctx, "wq_s", [128, 8 * 512], BF16)
    wk_s = SB(a_ctx, "wk_s", [128, 8 * 512], BF16)
    wv_s = SB(a_ctx, "wv_s", [128, 8 * 512], BF16)
    K_sb = SB(a_ctx, "K_sb", [128, 4 * 4096], BF16)
    V_sb = SB(a_ctx, "V_sb", [128, 8 * 32 * 66], BF16)
    Q_sb = SB(a_ctx, "Q_sb", [128, 4 * 2048], BF16)
    Pt = [SB(a_ctx, "Pt0", [128, 512], BF16), SB(a_ctx, "Pt1", [128, 512], BF16)]
    rec_s = SB(a_ctx, "rec_s", [128, 512])
    bc_s = SB(a_ctx, "bc_s", [64, 512])
    On_s = SB(a_ctx, "On_s", [64, 512], BF16)

    l1t_free = [None, None]
    ev_t1_seq = [None] * SGRP
    ev_t1_own = [None] * NG
    pfree = {k: None for k in ("A0", "A1", "B0", "B1", "C0", "C1")}
    PREG = {"A0": (pA, 0), "A1": (pA, 512), "B0": (pB, 0), "B1": (pB, 512),
            "C0": (pC, 0), "C1": (pC, 512)}

    ev_vones = S.emit("v", lambda h: h.memset(
        V_sb[:, :].rearrange("p (s n) -> p s n", n=66)[:, :, 64:66], 1.0), inc="V")

    def emit_weight_half(hh, extra_wait):
        ev = None
        for dst, src in ((wq_s, wq_h), (wk_s, wk_h), (wv_s, wv_h)):
            for c in range(8):
                ev = S.dma("g", dst[:, c * 512:(c + 1) * 512],
                           src[:, c * 1024 + hh * 512: c * 1024 + hh * 512 + 512],
                           waits=extra_wait)
        return ev

    def emit_qkv_half(hh, ev_w, first):
        kv_last, q_last = None, None
        acc_regs = ["A0", "A1", "B0", "B1"]
        for g in range(SGRP):
            gb = g % 2
            if first:
                subev = []
                for s in range(4):
                    subev.append(emit_ln_transpose(
                        xseq[(g * 4 + s) * 128:(g * 4 + s + 1) * 128, :],
                        l1t[gb], s * 128))
                ev_l1 = subev[-1]
                S.dma("s", l1s_d[g], l1t[gb][:, :], waits=subev)
                for ic in range(8):
                    w = [ev_l1, ev_c["bqkv"]] + ([pfree["C0"]] if ic == 0 else [])
                    S.emit("t", lambda h, ic=ic, gb=gb: h.matmul(
                        pC[:, 0:512], lhsT=bqkv_b[:, ic * 128:(ic + 1) * 128],
                        rhs=l1t[gb][:, ic * 512:(ic + 1) * 512],
                        start=(ic == 0), stop=(ic == 7)), waits=w,
                        inc="P" if ic == 7 else None)
                ev_t1p = ("P", S.cnt["P"])
                S.emit("a", lambda h, g=g: h.activation(
                    out=t1kv[32:40, g * 512:(g + 1) * 512], in_=pC[32:40, 0:512],
                    func=AF.Copy, scale=scal_s[32:40, 1:2]),
                    waits=[ev_t1p, ev_c["scal"]], inc="A")
                ev_t1_seq[g] = S.emit("a", lambda h, g=g: h.activation(
                    out=t1kv[64:72, g * 512:(g + 1) * 512], in_=pC[64:72, 0:512],
                    func=AF.Copy, scale=scal_s[64:72, 1:2]), inc="A")
                pfree["C0"] = ev_t1_seq[g]
            else:
                ev_l1 = S.dma("s", l1t[gb][:, :], l1s_d[g], waits=[l1t_free[gb]])
            for c in range(4):
                pp, off = PREG[acc_regs[c]]
                for ic in range(8):
                    w = [ev_l1, ev_w] + ([pfree[acc_regs[c]]] if ic == 0 else [])
                    S.emit("t", lambda h, c=c, ic=ic, gb=gb, pp=pp, off=off: h.matmul(
                        pp[:, off:off + 512],
                        lhsT=wk_s[:, ic * 512 + c * 128: ic * 512 + (c + 1) * 128],
                        rhs=l1t[gb][:, ic * 512:(ic + 1) * 512],
                        start=(ic == 0), stop=False), waits=w)
                S.emit("t", lambda h, c=c, g=g, hh=hh, pp=pp, off=off: h.matmul(
                    pp[:, off:off + 512],
                    lhsT=at_b[32:40, hh * 512 + c * 128: hh * 512 + (c + 1) * 128],
                    rhs=t1kv[32:40, g * 512:(g + 1) * 512],
                    start=False, stop=True),
                    waits=[ev_t1_seq[g], ev_c["at"]], inc="P")
                ev_kp = ("P", S.cnt["P"])
                kv_last = S.emit("a", lambda h, c=c, g=g, pp=pp, off=off: h.activation(
                    out=K_sb[:, c * 4096 + g * 512: c * 4096 + (g + 1) * 512],
                    in_=pp[:, off:off + 512], func=AF.Copy), waits=[ev_kp], inc="A")
                pfree[acc_regs[c]] = kv_last
            for s in range(4):
                kt = g * 4 + s
                vr = "C0" if s % 2 == 0 else "C1"
                pp, off = PREG[vr]
                for ic in range(8):
                    w = [ev_l1, ev_w] + ([pfree[vr]] if ic == 0 else [])
                    S.emit("t", lambda h, s=s, ic=ic, gb=gb, pp=pp, off=off: h.matmul(
                        pp[:, off:off + 512],
                        lhsT=l1t[gb][:, ic * 512 + s * 128: ic * 512 + (s + 1) * 128],
                        rhs=wv_s[:, ic * 512:(ic + 1) * 512],
                        start=(ic == 0), stop=False), waits=w)
                S.emit("t", lambda h, g=g, s=s, hh=hh, pp=pp, off=off: h.matmul(
                    pp[:, off:off + 512],
                    lhsT=t1kv[64:72, (g * 4 + s) * 128:(g * 4 + s + 1) * 128],
                    rhs=at_b[64:72, hh * 512:(hh + 1) * 512],
                    start=False, stop=True),
                    waits=[ev_t1_seq[g], ev_c["at"]], inc="P")
                ev_vp = ("P", S.cnt["P"])
                dst = mkap(V_sb, kt * 66, [[8 * 32 * 66, 128], [32 * 66, 8], [1, 64]])
                kv_last = S.emit("a", lambda h, pp=pp, off=off, dst=dst: h.activation(
                    out=dst, in_=pp[:, off:off + 512].rearrange("p (c n) -> p c n", n=64),
                    func=AF.Copy), waits=[ev_vp, ev_vones], inc="A")
                pfree[vr] = kv_last
            l1t_free[gb] = ("P", S.cnt["P"])
        for og in range(NG):
            gb = og % 2
            if first:
                subev = []
                for s in range(4):
                    subev.append(emit_ln_transpose(
                        xown[(og * 4 + s) * 128:(og * 4 + s + 1) * 128, :],
                        l1t[gb], s * 128))
                ev_l1 = subev[-1]
                S.dma("s", l1o_d[og], l1t[gb][:, :], waits=subev)
                for ic in range(8):
                    w = [ev_l1, ev_c["bq"]] + ([pfree["C0"]] if ic == 0 else [])
                    S.emit("t", lambda h, ic=ic, gb=gb: h.matmul(
                        pC[0:8, 0:512], lhsT=bq_b[:, ic * 8:(ic + 1) * 8],
                        rhs=l1t[gb][:, ic * 512:(ic + 1) * 512],
                        start=(ic == 0), stop=(ic == 7)), waits=w,
                        inc="P" if ic == 7 else None)
                ev_qt1p = ("P", S.cnt["P"])
                ev_t1_own[og] = S.emit("a", lambda h, og=og: h.activation(
                    out=t1q[0:8, og * 512:(og + 1) * 512], in_=pC[0:8, 0:512],
                    func=AF.Copy, scale=scal_s[0:8, 1:2]),
                    waits=[ev_qt1p, ev_c["scal"]], inc="A")
                pfree["C0"] = ev_t1_own[og]
            else:
                ev_l1 = S.dma("s", l1t[gb][:, :], l1o_d[og], waits=[l1t_free[gb]])
            for c in range(4):
                pp, off = PREG[acc_regs[c]]
                for ic in range(8):
                    w = [ev_l1, ev_w] + ([pfree[acc_regs[c]]] if ic == 0 else [])
                    S.emit("t", lambda h, c=c, ic=ic, gb=gb, pp=pp, off=off: h.matmul(
                        pp[:, off:off + 512],
                        lhsT=wq_s[:, ic * 512 + c * 128: ic * 512 + (c + 1) * 128],
                        rhs=l1t[gb][:, ic * 512:(ic + 1) * 512],
                        start=(ic == 0), stop=False), waits=w)
                S.emit("t", lambda h, c=c, og=og, hh=hh, pp=pp, off=off: h.matmul(
                    pp[:, off:off + 512],
                    lhsT=at_b[0:8, hh * 512 + c * 128: hh * 512 + (c + 1) * 128],
                    rhs=t1q[0:8, og * 512:(og + 1) * 512],
                    start=False, stop=True),
                    waits=[ev_t1_own[og], ev_c["at"]], inc="P")
                ev_qp = ("P", S.cnt["P"])
                q_last = S.emit("a", lambda h, c=c, og=og, pp=pp, off=off: h.activation(
                    out=Q_sb[:, c * 2048 + og * 512: c * 2048 + (og + 1) * 512],
                    in_=pp[:, off:off + 512], func=AF.Copy), waits=[ev_qp], inc="A")
                pfree[acc_regs[c]] = q_last
            l1t_free[gb] = ("P", S.cnt["P"])
        return kv_last, q_last

    def emit_attn_half(hh, ev_ready):
        Pt_free = [None, None]
        rec_free, bc_free, On_free = None, None, None
        ev_oc = None
        for gq in range(NG):
            for hl in range(8):
                h_g = hh * 8 + hl
                cl, hp = hl // 2, hl % 2
                o_reg = "A0" if hp == 0 else "A1"
                nkt = 8 * gq + 8
                ev_pv = None
                for kt in range(nkt):
                    cs = 128 * max(0, kt // 2 - 4 * gq)
                    sr = "C0" if kt % 2 == 0 else "C1"
                    sbk = kt % 2
                    w = [ev_ready, pfree[sr]]
                    ev_s = S.emit("t", lambda h, hp=hp, cl=cl, kt=kt, gq=gq, cs=cs, sbk=sbk: h.matmul(
                        pC[:, sbk * 512 + cs:(sbk + 1) * 512],
                        lhsT=K_sb[hp * 64:(hp + 1) * 64,
                                  cl * 4096 + kt * 128: cl * 4096 + (kt + 1) * 128],
                        rhs=Q_sb[hp * 64:(hp + 1) * 64,
                                 cl * 2048 + gq * 512 + cs: cl * 2048 + (gq + 1) * 512],
                        start=True, stop=True), waits=w, inc="P")
                    ev_m = None
                    if kt >= 8 * gq:
                        jm = kt // 2 - 4 * gq
                        ev_m = S.emit("v", lambda h, sbk=sbk, jm=jm, kt=kt: h.tensor_tensor(
                            out=pC[:, sbk * 512 + jm * 128: sbk * 512 + (jm + 1) * 128],
                            in0=pC[:, sbk * 512 + jm * 128: sbk * 512 + (jm + 1) * 128],
                            in1=maskp_s[:, (kt % 2) * 128:(kt % 2 + 1) * 128],
                            op=ALU.add), waits=[ev_s, ev_c["maskp"]], inc="V")
                    ev_e = S.emit("a", lambda h, sbk=sbk, cs=cs, h_g=h_g, kt=kt: h.activation(
                        out=Pt[sbk][:, cs:512], in_=pC[:, sbk * 512 + cs:(sbk + 1) * 512],
                        func=AF.Exp, bias=bias_s[:, h_g * 32 + kt: h_g * 32 + kt + 1],
                        scale=scal_s[:, 2:3]),
                        waits=[ev_s, ev_m, ev_c["bias"], Pt_free[sbk]], inc="A")
                    pfree[sr] = ev_e
                    w2 = [ev_e] + ([pfree[o_reg]] if kt == 0 else [])
                    ev_pv = S.emit("t", lambda h, hl=hl, kt=kt, cs=cs, sbk=sbk, hp=hp, nkt=nkt: h.matmul(
                        pA[0:65, hp * 512 + cs:(hp + 1) * 512],
                        lhsT=V_sb[:, (hl * 32 + kt) * 66: (hl * 32 + kt) * 66 + 65],
                        rhs=Pt[sbk][:, cs:512],
                        start=(kt == 0), stop=(kt == nkt - 1)), waits=w2, inc="P")
                    Pt_free[sbk] = ev_pv
                ev_rec = S.emit("v", lambda h, hp=hp: h.reciprocal(
                    rec_s[64:65, :], pA[64:65, hp * 512:(hp + 1) * 512]),
                    waits=[ev_pv, rec_free], inc="V")
                ev_bc = S.emit("t", lambda h: h.matmul(
                    pB[:, 512:1024], lhsT=ones_f[64:65, 0:128], rhs=rec_s[64:65, 0:512],
                    start=True, stop=True),
                    waits=[ev_rec, ev_c["ones"], pfree["B1"]], inc="P")
                rec_free = ev_bc
                ev_bce = S.emit("a", lambda h: h.activation(
                    out=bc_s[:, :], in_=pB[0:64, 512:1024], func=AF.Copy),
                    waits=[ev_bc, bc_free], inc="A")
                pfree["B1"] = ev_bce
                ev_on = S.emit("v", lambda h, hp=hp: h.tensor_tensor(
                    out=On_s[:, :], in0=pA[0:64, hp * 512:(hp + 1) * 512],
                    in1=bc_s[:, :], op=ALU.mult), waits=[ev_bce, On_free], inc="V")
                bc_free = ev_on
                pfree[o_reg] = ev_on
                w3 = [ev_on, ev_c["ishift"]] + ([pfree["B0"]] if hp == 0 else [])
                ev_sh = S.emit("t", lambda h, hp=hp: h.matmul(
                    pB[:, 0:512], lhsT=ishift_b[0:64, hp * 128:(hp + 1) * 128],
                    rhs=On_s[0:64, 0:512], start=(hp == 0), stop=(hp == 1)),
                    waits=w3, inc="P")
                On_free = ev_sh
                if hp == 1:
                    ev_oc = S.emit("a", lambda h, hh=hh, cl=cl, gq=gq: h.activation(
                        out=O_sb[:, (hh * 4 + cl) * 2048 + gq * 512:
                                 (hh * 4 + cl) * 2048 + (gq + 1) * 512],
                        in_=pB[:, 0:512], func=AF.Copy), waits=[ev_sh], inc="A")
                    pfree["B0"] = ev_oc
        return ev_oc

    ev_w0 = emit_weight_half(0, [])
    kv0, q0 = emit_qkv_half(0, ev_w0, first=True)
    ev_oc0 = emit_attn_half(0, q0)
    ev_w1 = emit_weight_half(1, [("P", S.cnt["P"])])
    kv1, q1 = emit_qkv_half(1, ev_w1, first=False)
    ev_oc1 = emit_attn_half(1, q1)

    a_ctx.close()
    S.set_floor()
    b_ctx = ExitStack()
    wp_s = SB(b_ctx, "wp_s", [128, 8 * 1024], BF16)
    at_f = [SB(b_ctx, "atf0", [128, 1024]), SB(b_ctx, "atf1", [128, 1024])]
    xo_f = [SB(b_ctx, "xof0", [128, 1024]), SB(b_ctx, "xof1", [128, 1024])]
    x1t = [SB(b_ctx, "x1t0", [128, 1024]), SB(b_ctx, "x1t1", [128, 1024])]
    ev_wp = None
    for ic in range(8):
        ev_wp = S.dma("g", wp_s[:, ic * 1024:(ic + 1) * 1024],
                      wp_h[:, ic * 1024:(ic + 1) * 1024])
    pacc_free = [None, None]
    at_free = [None, None]
    xo_free = [None, None]
    x1t_free = [None, None]
    ev_x1 = [None] * NT
    for j in range(NT):
        pb = j % 2
        pp = pA if pb == 0 else pB
        ev_xo = S.dma("a", xo_f[pb][:, :], xown[j * 128:(j + 1) * 128, :],
                      waits=[xo_free[pb]])
        for ic in range(8):
            for hf in range(2):
                w = [ev_oc1, ev_wp] + ([pacc_free[pb]] if (ic == 0 and hf == 0) else [])
                S.emit("t", lambda h, ic=ic, hf=hf, j=j, pp=pp: h.matmul(
                    pp[:, hf * 512:(hf + 1) * 512],
                    lhsT=O_sb[:, ic * 2048 + j * 128:ic * 2048 + (j + 1) * 128],
                    rhs=wp_s[:, ic * 1024 + hf * 512: ic * 1024 + (hf + 1) * 512],
                    start=(ic == 0), stop=(ic == 7)),
                    waits=w, inc="P" if (ic == 7 and hf == 1) else None)
        ev_wpp = ("P", S.cnt["P"])
        ev_ae = S.emit("a", lambda h, pb=pb, pp=pp: h.activation(
            out=at_f[pb][:, :], in_=pp[:, :], func=AF.Copy),
            waits=[ev_wpp, at_free[pb]], inc="A")
        pacc_free[pb] = ev_ae
        ev_ln = emit_ln(at_f[pb], 0, tmp_f[:, :], ev_ae)
        at_free[pb] = ev_ln
        S.drain("v")
        ev_x1t = S.emit("v", lambda h, pb=pb: h.tensor_tensor(
            out=x1t[pb][:, :], in0=tmp_f[:, :], in1=xo_f[pb][:, :], op=ALU.add),
            waits=[ev_xo, x1t_free[pb]], inc="V")
        xo_free[pb] = ev_x1t
        ev_x1[j] = S.dma("s", x1_d[j], x1t[pb][:, :], waits=[ev_x1t])
        x1t_free[pb] = ev_x1[j]

    if DEBUG_STOP_AT_X1:
        for j in range(NT):
            S.dma("s", out_d[j * 128:(j + 1) * 128, :], x1_d[j],
                  waits=[ev_x1[j]], sem="O")
        S.emit("s", lambda h: None, waits=[("O", S.cnt["O"])])
        b_ctx.close(); ab_ctx.close()
        with octx, nc.Block() as block:
            S.replay(block)
        return nc

    b_ctx.close()
    ab_ctx.close()
    S.set_floor()
    c_ctx = ExitStack()
    ln2T = SB(c_ctx, "ln2T", [128, 8 * 1024], BF16)
    rw_s = SB(c_ctx, "rw_s", [128, 8 * 32], BF16)
    sgw_s = SB(c_ctx, "sgw_s", [128, 8 * 512], BF16)
    suw_s = SB(c_ctx, "suw_s", [128, 8 * 512], BF16)
    sdw_s = SB(c_ctx, "sdw_s", [128, 4 * 1024], BF16)
    gw_s = [SB(c_ctx, "gw_s0", [128, 1024], BF16), SB(c_ctx, "gw_s1", [128, 1024], BF16)]
    uw_s = [SB(c_ctx, "uw_s0", [128, 1024], BF16), SB(c_ctx, "uw_s1", [128, 1024], BF16)]
    dw_s = [SB(c_ctx, f"dw_s{i}", [128, 1024], BF16) for i in range(8)]
    sc_f = SB(c_ctx, "sc_f", [128, 32])
    cur_f = SB(c_ctx, "cur_f", [128, 32])
    msk_f = SB(c_ctx, "msk_f", [128, 32])
    m_f = SB(c_ctx, "m_f", [128, 1])
    wu_f = SB(c_ctx, "wu_f", [128, 32])
    w_b = SB(c_ctx, "w_b", [128, 32], BF16)
    wfm = SB(c_ctx, "wfm", [32, 1024], BF16)
    wrow = [SB(c_ctx, "wrow0", [1, 1024], BF16), SB(c_ctx, "wrow1", [1, 1024], BF16)]
    sg_b = SB(c_ctx, "sg_b", [128, 1024], BF16)
    wb_b = SB(c_ctx, "wb_b", [128, 1024], BF16)
    h1_b = SB(c_ctx, "h1_b", [128, 1024], BF16)
    h_blk = SB(c_ctx, "h_blk", [128, 8 * 1024], BF16)
    hsh = SB(c_ctx, "hsh", [128, 4 * 1024], BF16)
    moe = SB(c_ctx, "moe", [128, 8 * 1024])
    out_f = [SB(c_ctx, "outf0", [128, 1024]), SB(c_ctx, "outf1", [128, 1024])]
    wdbg = SB(c_ctx, "wdbg", [128, 32])
    x1e = [SB(c_ctx, "x1e0", [128, 1024]), SB(c_ctx, "x1e1", [128, 1024])]

    ev_rw = S.dma("g", rw_s[:, :], rw_h)
    ev_sg = S.dma("g", sgw_s[:, :], sg_h)
    ev_su = S.dma("g", suw_s[:, :], su_h)
    ev_sd = S.dma("g", sdw_s[:, :], sd_h)

    def l2rhs(ic, hf):
        return mkap(ln2T, ic * 1024 + hf * 512, [[8192, 128], [1, 512]])

    ev_moe = [None] * 8
    w_b_free = [None]
    gw_free = [None, None]
    uw_free = [None, None]
    dw_free = [None] * 8
    wrow_free = [None, None]
    h_blk_free = [None] * 8
    sg_free, wb_free, h1_free = None, None, None
    x1e_free = [None, None]
    out_free = [None, None]
    topk_free = None
    for G in range(2):
        ev_l2_last = None
        ev_wfm = []
        pfree["C0"] = pfree.get("C0")
        for st in range(8):
            j = G * 8 + st
            pb = st % 2
            ev_x1l = S.dma("a", x1e[pb][:, :], x1_d[j], waits=[ev_x1[j], x1e_free[pb]])
            ev_ln = emit_ln(x1e[pb], 0, lnb[pb][:, :], ev_x1l,
                            out_wait=[lnb_free[pb]])
            x1e_free[pb] = ev_ln
            th = st % 2
            ev_tr = None
            for c in range(8):
                w = [ev_ln] + ([pT_free[th]] if c == 0 else [])
                ev_tr = S.emit("t", lambda h, c=c, pb=pb, th=th: h.transpose(
                    pT[:, th * 1024 + c * 128: th * 1024 + (c + 1) * 128],
                    in_=lnb[pb][:, c * 128:(c + 1) * 128], identity=ident_b[:, :]),
                    waits=w, inc="P")
            lnb_free[pb] = ev_tr
            ev_l2_last = S.emit("a", lambda h, th=th, dst=mkap(
                ln2T, st * 128, [[8192, 128], [1024, 8], [1, 128]]): h.activation(
                out=dst, in_=pT[:, th * 1024:(th + 1) * 1024].rearrange("p (c n) -> p c n", n=128),
                func=AF.Copy), waits=[ev_tr], inc="A")
            pT_free[th] = ev_l2_last
            for ic in range(8):
                w = [ev_l2_last, ev_rw] + ([pfree["C0"]] if ic == 0 else [])
                S.emit("t", lambda h, ic=ic, st=st: h.matmul(
                    pC[:, 0:32],
                    lhsT=mkap(ln2T, ic * 1024 + st * 128, [[8192, 128], [1, 128]]),
                    rhs=rw_s[:, ic * 32:(ic + 1) * 32],
                    start=(ic == 0), stop=False), waits=w)
            S.emit("t", lambda h: h.matmul(
                pC[:, 0:32], lhsT=ones_f[0:1, 0:128], rhs=rb_s[0:1, :],
                start=False, stop=True), waits=[ev_c["rb"], ev_c["ones"]], inc="P")
            ev_rt = ("P", S.cnt["P"])
            ev_nm = S.emit("v", lambda h: h.reduce_max(
                m_f[:, :], pC[:, 0:32], axis=AX.X, negate=True),
                waits=[ev_rt, topk_free], inc="V")
            ev_sc = S.emit("a", lambda h: h.activation(
                out=sc_f[:, :], in_=pC[:, 0:32], func=AF.Exp, bias=m_f[:, 0:1]),
                waits=[ev_nm], inc="A")
            pfree["C0"] = ev_sc
            S.emit("v", lambda h: h.tensor_copy(cur_f[:, :], sc_f[:, :]), waits=[ev_sc])
            for it in range(3):
                S.drain("v")
                S.emit("v", lambda h: h.reduce_max(m_f[:, :], cur_f[:, :], axis=AX.X),
                       waits=[ev_sc])
                S.drain("v")
                S.emit("v", lambda h: h.tensor_scalar(
                    out=msk_f[:, :], in0=cur_f[:, :], scalar1=m_f[:, 0:1],
                    scalar2=1e30, op0=ALU.is_ge, op1=ALU.mult))
                S.drain("v")
                S.emit("v", lambda h: h.tensor_sub(cur_f[:, :], cur_f[:, :], msk_f[:, :]))
            S.drain("v")
            S.emit("v", lambda h: h.reduce_max(m_f[:, :], cur_f[:, :], axis=AX.X))
            S.drain("v")
            S.emit("v", lambda h: h.tensor_scalar(
                out=msk_f[:, :], in0=sc_f[:, :], scalar1=m_f[:, 0:1],
                scalar2=None, op0=ALU.is_ge))
            S.drain("v")
            S.emit("v", lambda h: h.tensor_mul(wu_f[:, :], sc_f[:, :], msk_f[:, :]))
            S.drain("v")
            S.emit("v", lambda h: h.reduce_sum(m_f[:, :], wu_f[:, :], axis=AX.X))
            S.drain("v")
            S.emit("v", lambda h: h.reciprocal(m_f[:, :], m_f[:, :]))
            S.drain("v")
            ev_wb = S.emit("v", lambda h: h.tensor_scalar(
                out=w_b[:, :], in0=wu_f[:, :], scalar1=m_f[:, 0:1], scalar2=None,
                op0=ALU.mult), waits=[w_b_free[0]], inc="V")
            topk_free = ev_wb
            ev_wt = S.emit("t", lambda h, th=th: h.transpose(
                pT[0:32, th * 1024: th * 1024 + 128],
                in_=w_b[:, :], identity=ident_b[:, :]),
                waits=[ev_wb, pT_free[th]], inc="P")
            ev_wf = S.emit("a", lambda h, st=st, th=th: h.activation(
                out=wfm[0:32, st * 128:(st + 1) * 128],
                in_=pT[0:32, th * 1024: th * 1024 + 128],
                func=AF.Copy), waits=[ev_wt], inc="A")
            ev_wfm.append(ev_wf)
            pT_free[th] = ev_wf
            w_b_free[0] = ev_wt
            if DEBUG_FFN_W:
                ev_wd = S.emit("v", lambda h: h.tensor_copy(wdbg[:, :], w_b[:, :]),
                               waits=[ev_wb], inc="V")
                S.dma("s", out_d[j * 128:(j + 1) * 128, 0:32], wdbg[:, :],
                      waits=[ev_wd], sem="O")
        ev_wsc = S.dma("s", wg_d[G], wfm[:, :], waits=ev_wfm)
        if DEBUG_FFN_W:
            continue
        for blk in range(DEBUG_NBLK):
            ev_h_all = None
            for e8 in range(8):
                e = blk * 8 + e8
                eb = e % 2
                ev_gw = S.dma("g", gw_s[eb][:, :], gw_h[e],
                              waits=[gw_free[eb], ("dG", S.cnt["dG"])])
                ev_uw = S.dma("g", uw_s[eb][:, :], uw_h[e],
                              waits=[uw_free[eb], ("dG", S.cnt["dG"])])
                ev_dwl = S.dma("g", dw_s[e8][:, :], dw_h[e],
                               waits=[dw_free[e8], ("dG", S.cnt["dG"])])
                dw_free[e8] = ev_dwl
                ev_wr = S.dma("s", wrow[eb][0:1, :], wg_d[G][e:e + 1, :],
                              waits=[ev_wsc, wrow_free[eb], ("dS", S.cnt["dS"])])
                for ic in range(8):
                    for hf in range(2):
                        w = [ev_gw, ev_l2_last] + (
                            [pfree["A0"], pfree["A1"]] if (ic == 0 and hf == 0) else [])
                        S.emit("t", lambda h, ic=ic, hf=hf, eb=eb: h.matmul(
                            pA[:, hf * 512:(hf + 1) * 512],
                            lhsT=gw_s[eb][:, ic * 128:(ic + 1) * 128],
                            rhs=l2rhs(ic, hf), start=(ic == 0), stop=(ic == 7)),
                            waits=w, inc="P" if (ic == 7 and hf == 1) else None)
                ev_g = ("P", S.cnt["P"])
                gw_free[eb] = ev_g
                for ic in range(8):
                    for hf in range(2):
                        w = [ev_uw, ev_l2_last] + (
                            [pfree["B0"], pfree["B1"]] if (ic == 0 and hf == 0) else [])
                        S.emit("t", lambda h, ic=ic, hf=hf, eb=eb: h.matmul(
                            pB[:, hf * 512:(hf + 1) * 512],
                            lhsT=uw_s[eb][:, ic * 128:(ic + 1) * 128],
                            rhs=l2rhs(ic, hf), start=(ic == 0), stop=(ic == 7)),
                            waits=w, inc="P" if (ic == 7 and hf == 1) else None)
                ev_u = ("P", S.cnt["P"])
                uw_free[eb] = ev_u
                for hf in range(2):
                    w = [ev_wr, ev_c["onesb"]] + (
                        [pfree["C0"], pfree["C1"]] if hf == 0 else [])
                    S.emit("t", lambda h, hf=hf, eb=eb: h.matmul(
                        pC[:, hf * 512:(hf + 1) * 512], lhsT=ones_b[0:1, 0:128],
                        rhs=wrow[eb][0:1, hf * 512:(hf + 1) * 512],
                        start=True, stop=True), waits=w,
                        inc="P" if hf == 1 else None)
                ev_wbp = ("P", S.cnt["P"])
                wrow_free[eb] = ev_wbp
                ev_sgb = S.emit("a", lambda h: h.activation(
                    out=sg_b[:, :], in_=pA[:, :], func=AF.Silu),
                    waits=[ev_g, sg_free], inc="A")
                pfree["A0"] = pfree["A1"] = ev_sgb
                ev_wbe = S.emit("a", lambda h: h.activation(
                    out=wb_b[:, :], in_=pC[:, :], func=AF.Copy),
                    waits=[ev_wbp, wb_free], inc="A")
                pfree["C0"] = pfree["C1"] = ev_wbe
                ev_h1 = S.emit("v", lambda h: h.tensor_tensor(
                    out=h1_b[:, :], in0=sg_b[:, :], in1=pB[:, :], op=ALU.mult),
                    waits=[ev_sgb, ev_u, h1_free], inc="V")
                sg_free = ev_h1
                pfree["B0"] = pfree["B1"] = ev_h1
                ev_h = S.emit("v", lambda h, e8=e8: h.tensor_tensor(
                    out=h_blk[:, e8 * 1024:(e8 + 1) * 1024], in0=h1_b[:, :],
                    in1=wb_b[:, :], op=ALU.mult),
                    waits=[ev_h1, ev_wbe, h_blk_free[e8]], inc="V")
                h1_free = ev_h
                wb_free = ev_h
                ev_h_all = ev_h
            dn_regs = [("C0", "C1", pC), ("A0", "A1", pA)]
            for st in range(8):
                r0, r1, pp = dn_regs[st % 2]
                for e8 in range(8):
                    w = [ev_h_all, dw_free[e8]] + (
                        [pfree[r0], pfree[r1]] if e8 == 0 else [])
                    S.emit("t", lambda h, e8=e8, st=st, pp=pp: h.matmul(
                        pp[:, 0:512],
                        lhsT=h_blk[:, e8 * 1024 + st * 128: e8 * 1024 + (st + 1) * 128],
                        rhs=dw_s[e8][:, 0:512],
                        start=(e8 == 0), stop=False), waits=w)
                    S.emit("t", lambda h, e8=e8, st=st, pp=pp: h.matmul(
                        pp[:, 512:1024],
                        lhsT=h_blk[:, e8 * 1024 + st * 128: e8 * 1024 + (st + 1) * 128],
                        rhs=dw_s[e8][:, 512:1024],
                        start=(e8 == 0), stop=(e8 == 7)),
                        inc="P" if e8 == 7 else None)
                ev_dn = ("P", S.cnt["P"])
                if blk == 0:
                    ev_mo = S.emit("v", lambda h, st=st, pp=pp: h.tensor_copy(
                        moe[:, st * 1024:(st + 1) * 1024], pp[:, :]),
                        waits=[ev_dn, ev_moe[st]], inc="V")
                else:
                    ev_mo = S.emit("v", lambda h, st=st, pp=pp: h.tensor_tensor(
                        out=moe[:, st * 1024:(st + 1) * 1024],
                        in0=moe[:, st * 1024:(st + 1) * 1024], in1=pp[:, :],
                        op=ALU.add), waits=[ev_dn, ev_moe[st]], inc="V")
                ev_moe[st] = ev_mo
                pfree[r0] = pfree[r1] = ev_mo
            ev_dn_done = ("P", S.cnt["P"])
            for e8 in range(8):
                h_blk_free[e8] = ev_dn_done
                dw_free[e8] = ev_dn_done
        for hc in range(0 if DEBUG_NO_SHARED else 4):
            for ic in range(8):
                for hf in range(2):
                    w = [ev_sg, ev_l2_last] + (
                        [pfree["A0"], pfree["A1"]] if (ic == 0 and hf == 0) else [])
                    S.emit("t", lambda h, ic=ic, hf=hf, hc=hc: h.matmul(
                        pA[:, hf * 512:(hf + 1) * 512],
                        lhsT=sgw_s[:, ic * 512 + hc * 128: ic * 512 + (hc + 1) * 128],
                        rhs=l2rhs(ic, hf), start=(ic == 0), stop=(ic == 7)),
                        waits=w, inc="P" if (ic == 7 and hf == 1) else None)
            ev_g = ("P", S.cnt["P"])
            for ic in range(8):
                for hf in range(2):
                    w = [ev_su, ev_l2_last] + (
                        [pfree["B0"], pfree["B1"]] if (ic == 0 and hf == 0) else [])
                    S.emit("t", lambda h, ic=ic, hf=hf, hc=hc: h.matmul(
                        pB[:, hf * 512:(hf + 1) * 512],
                        lhsT=suw_s[:, ic * 512 + hc * 128: ic * 512 + (hc + 1) * 128],
                        rhs=l2rhs(ic, hf), start=(ic == 0), stop=(ic == 7)),
                        waits=w, inc="P" if (ic == 7 and hf == 1) else None)
            ev_u = ("P", S.cnt["P"])
            ev_sgb = S.emit("a", lambda h: h.activation(
                out=sg_b[:, :], in_=pA[:, :], func=AF.Silu),
                waits=[ev_g, sg_free], inc="A")
            pfree["A0"] = pfree["A1"] = ev_sgb
            ev_h = S.emit("v", lambda h, hc=hc: h.tensor_tensor(
                out=hsh[:, hc * 1024:(hc + 1) * 1024], in0=sg_b[:, :], in1=pB[:, :],
                op=ALU.mult), waits=[ev_sgb, ev_u], inc="V")
            sg_free = ev_h
            pfree["B0"] = pfree["B1"] = ev_h
        if DEBUG_NO_SHARED:
            for hc in []: pass
        ev_hsh = ("V", S.cnt["V"])
        dn_regs = [("C0", "C1", pC), ("A0", "A1", pA)]
        for st in range(0 if DEBUG_NO_SHARED else 8):
            r0, r1, pp = dn_regs[st % 2]
            for hc in range(4):
                w = [ev_hsh, ev_sd] + ([pfree[r0], pfree[r1]] if hc == 0 else [])
                S.emit("t", lambda h, hc=hc, st=st, pp=pp: h.matmul(
                    pp[:, 0:512],
                    lhsT=hsh[:, hc * 1024 + st * 128: hc * 1024 + (st + 1) * 128],
                    rhs=sdw_s[:, hc * 1024: hc * 1024 + 512],
                    start=(hc == 0), stop=False), waits=w)
                S.emit("t", lambda h, hc=hc, st=st, pp=pp: h.matmul(
                    pp[:, 512:1024],
                    lhsT=hsh[:, hc * 1024 + st * 128: hc * 1024 + (st + 1) * 128],
                    rhs=sdw_s[:, hc * 1024 + 512: (hc + 1) * 1024],
                    start=(hc == 0), stop=(hc == 3)),
                    inc="P" if hc == 3 else None)
            ev_dn = ("P", S.cnt["P"])
            ev_mo = S.emit("v", lambda h, st=st, pp=pp: h.tensor_tensor(
                out=moe[:, st * 1024:(st + 1) * 1024],
                in0=moe[:, st * 1024:(st + 1) * 1024], in1=pp[:, :], op=ALU.add),
                waits=[ev_dn, ev_moe[st]], inc="V")
            ev_moe[st] = ev_mo
            pfree[r0] = pfree[r1] = ev_mo
        for st in range(8):
            j = G * 8 + st
            pb = st % 2
            if DEBUG_FFN_RAW:
                out_free[pb] = S.dma("s", out_d[j * 128:(j + 1) * 128, :],
                                     moe[:, st * 1024:(st + 1) * 1024],
                                     waits=[ev_moe[st], out_free[pb]], sem="O")
                continue
            ev_x1l = S.dma("a", x1e[pb][:, :], x1_d[j], waits=[x1e_free[pb]])
            ev_ln = emit_ln(moe, st * 1024, tmp_f[:, :], ev_moe[st])
            ev_moe[st] = ev_ln
            S.drain("v")
            ev_o = S.emit("v", lambda h, pb=pb: h.tensor_tensor(
                out=out_f[pb][:, :], in0=tmp_f[:, :], in1=x1e[pb][:, :], op=ALU.add),
                waits=[ev_x1l, out_free[pb]], inc="V")
            x1e_free[pb] = ev_o
            out_free[pb] = S.dma("s", out_d[j * 128:(j + 1) * 128, :], out_f[pb][:, :],
                                 waits=[ev_o], sem="O")
    S.emit("s", lambda h: None, waits=[("O", S.cnt["O"])])

    c_ctx.close()
    with octx, nc.Block() as block:
        S.replay(block)
    return nc


_NC_CACHE = {}


def _chunked(w, cols):
    return np.ascontiguousarray(
        w.reshape(8, 128, cols).transpose(1, 0, 2).reshape(128, 8 * cols))


def kernel(L0_attn_x, L0_attn_Wq, L0_attn_Aq, L0_attn_Bq, L0_attn_Wk, L0_attn_Ak,
           L0_attn_Bk, L0_attn_Wv, L0_attn_Av, L0_attn_Bv, L0_attn_Wp,
           L0_ffn_router_w, L0_ffn_router_bias, L0_ffn_gate_w, L0_ffn_up_w,
           L0_ffn_down_w, L0_ffn_sgate_w, L0_ffn_sup_w, L0_ffn_sdown_w):
    f = lambda a: np.ascontiguousarray(np.asarray(a, dtype=np.float32))
    X = f(L0_attn_x)
    wq = _chunked(f(L0_attn_Wq).T, 1024)
    wk = _chunked(f(L0_attn_Wk).T, 1024)
    wv = _chunked(f(L0_attn_Wv).T, 1024)
    wp = _chunked(f(L0_attn_Wp).T, 1024)
    Mkv = np.zeros((1024, 128), np.float32)
    Mkv[:, 32:40] = f(L0_attn_Bk).T
    Mkv[:, 64:72] = f(L0_attn_Bv).T
    bqkv = _chunked(Mkv, 128)
    bq = _chunked(f(L0_attn_Bq).T, 8)
    aT = np.zeros((128, 1024), np.float32)
    aT[0:8] = f(L0_attn_Aq).T
    aT[32:40] = f(L0_attn_Ak).T
    aT[64:72] = f(L0_attn_Av).T
    rw = _chunked(f(L0_ffn_router_w).T, 32)
    rb = f(L0_ffn_router_bias).reshape(1, 32)
    gw = np.ascontiguousarray(
        f(L0_ffn_gate_w).reshape(32, 128, 8, 128).transpose(0, 3, 2, 1).reshape(32, 128, 1024))
    uw = np.ascontiguousarray(
        f(L0_ffn_up_w).reshape(32, 128, 8, 128).transpose(0, 3, 2, 1).reshape(32, 128, 1024))
    dw = np.ascontiguousarray(f(L0_ffn_down_w).transpose(0, 2, 1))
    sg = _chunked(f(L0_ffn_sgate_w).T, 512)
    su = _chunked(f(L0_ffn_sup_w).T, 512)
    sd = np.ascontiguousarray(
        f(L0_ffn_sdown_w).T.reshape(4, 128, 1024).transpose(1, 0, 2).reshape(128, 4096))
    ident = np.eye(128, dtype=np.float32)
    ones = np.ones((128, 128), np.float32)
    scal = np.zeros((128, 4), np.float32)
    scal[:, 0] = EPS
    scal[:, 1] = 2.0
    scal[:, 2] = 0.125
    ishift = np.zeros((64, 256), np.float32)
    for d in range(64):
        ishift[d, d] = 1.0
        ishift[d, 128 + 64 + d] = 1.0
    slopes = np.array([2.0 ** (-(i + 1) / 2.0) for i in range(H)], np.float64)
    biastab = np.zeros((128, 512), np.float32)
    p = np.arange(128, dtype=np.float64)
    for hh in range(H):
        for kt in range(32):
            biastab[:, hh * 32 + kt] = (-slopes[hh] * (kt * 128 + p)).astype(np.float32)
    tri = np.where(np.arange(128)[:, None] > np.arange(128)[None, :],
                   NEGBIG, 0.0).astype(np.float32)
    full = np.full((128, 128), NEGBIG, np.float32)
    zero = np.zeros((128, 128), np.float32)
    maskr = [np.concatenate([tri, full], axis=1), np.concatenate([zero, tri], axis=1)]

    if "nc" not in _NC_CACHE:
        _NC_CACHE["nc"] = build_program()
    nc = _NC_CACHE["nc"]

    in_maps = []
    for c in range(8):
        b, r = c // 2, c % 2
        xs = np.ascontiguousarray(X[b])
        own_rows = np.concatenate(
            [np.arange((2 * j + r) * 128, (2 * j + r + 1) * 128) for j in range(NT)])
        xo = np.ascontiguousarray(xs[own_rows])
        in_maps.append(dict(
            xseq=xs, xown=xo, maskp=maskr[r], biastab=biastab, identc=ident,
            ishift=ishift, onesc=ones, scal=scal, wq=wq, wk=wk, wv=wv, wp=wp, bqkv=bqkv,
            bq=bq, at=aT, rw=rw, rb=rb, gw=gw, uw=uw, dw=dw, sg=sg, su=su, sd=sd))
    res = run_bass_kernel_spmd(nc, in_maps, core_ids=list(range(8)))
    out = np.empty((B, T, E), np.float32)
    for c in range(8):
        b, r = c // 2, c % 2
        oc = res.results[c]["out"]
        for j in range(NT):
            out[b, (2 * j + r) * 128:(2 * j + r + 1) * 128, :] = oc[j * 128:(j + 1) * 128, :]
    return out


# version 02
# speedup vs baseline: 1.0725x; 1.0725x over previous
import dataclasses
import math
from contextlib import ExitStack

import numpy as np
import concourse.bass as bass
import concourse.mybir as mybir
from concourse.bass_utils import run_bass_kernel_spmd

F32 = mybir.dt.float32
BF16 = mybir.dt.bfloat16
AF = mybir.ActivationFunctionType
ALU = mybir.AluOpType
AX = mybir.AxisListType

B, T, E, H, DH = 4, 4096, 1024, 16, 64
NEXP, HS = 32, 512
NT, NG, SGRP = 16, 4, 8
EPS = 1e-5
NEGBIG = -1e30
DEBUG_STOP_AT_X1 = False
DEBUG_FFN_RAW = False
DEBUG_FFN_W = False
DEBUG_NBLK = 4
DEBUG_NO_SHARED = False


def mkap(handle, off, pattern):
    a = handle[:, :] if len(handle.shape) == 2 else handle[:]
    return dataclasses.replace(a, offset=a.offset + off, ap=pattern)


class Sched:
    ENGS = ("s", "g", "t", "a", "v")

    def __init__(self, nc):
        self.nc = nc
        self.q = {e: [] for e in self.ENGS}
        self.sems = {}
        self.cnt = {}
        self.floor = {}

    def sem(self, ctx, name):
        self.sems[name] = ctx.enter_context(self.nc.semaphore(name))
        self.cnt[name] = 0

    def set_floor(self):
        self.floor = dict(self.cnt)

    def emit(self, eng, fn, waits=(), inc=None, amt=1):
        ws = dict(self.floor)
        for w in waits:
            if w is None:
                continue
            sname, val = w
            assert val <= self.cnt[sname], (sname, val, self.cnt[sname])
            if ws.get(sname, 0) < val:
                ws[sname] = val
        evt = None
        if inc is not None:
            self.cnt[inc] += amt
            evt = (inc, self.cnt[inc])
        self.q[eng].append((fn, tuple(ws.items()), (inc, amt) if inc else None))
        return evt

    def dma(self, eng, out, in_, waits=(), sem="dS"):
        return self.emit(eng, lambda h, o=out, i=in_: h.dma_start(out=o, in_=i),
                         waits=waits, inc=sem, amt=16)

    def drain(self, eng):
        self.emit(eng, lambda h: h.drain())

    def replay(self, block):
        decos = {"s": block.sync, "g": block.gpsimd, "t": block.tensor,
                 "a": block.scalar, "v": block.vector}
        for ek in self.ENGS:
            ops = self.q[ek]
            if not ops:
                continue
            def body(h, ops=ops):
                hwm = {}
                for fn, waits, inc in ops:
                    for sname, val in waits:
                        if hwm.get(sname, 0) < val:
                            h.wait_ge(self.sems[sname], val)
                            hwm[sname] = val
                    ins = fn(h)
                    if inc is not None and ins is not None:
                        ins.then_inc(self.sems[inc[0]], inc[1])
            decos[ek](body)


def build_program():
    nc = bass.Bass("TRN2", target_bir_lowering=False)
    dt = nc.dram_tensor
    xseq = dt("xseq", [T, E], F32, kind="ExternalInput").ap()
    xown = dt("xown", [NT * 128, E], F32, kind="ExternalInput").ap()
    maskp_d = dt("maskp", [128, 256], F32, kind="ExternalInput").ap()
    bias_d = dt("biastab", [128, 512], F32, kind="ExternalInput").ap()
    ident_d = dt("identc", [128, 128], F32, kind="ExternalInput").ap()
    ishift_d = dt("ishift", [64, 256], F32, kind="ExternalInput").ap()
    ones_d = dt("onesc", [128, 128], F32, kind="ExternalInput").ap()
    scal_d = dt("scal", [128, 4], F32, kind="ExternalInput").ap()
    wq_h = dt("wq", [128, 8 * 1024], F32, kind="ExternalInput").ap()
    wk_h = dt("wk", [128, 8 * 1024], F32, kind="ExternalInput").ap()
    wv_h = dt("wv", [128, 8 * 1024], F32, kind="ExternalInput").ap()
    wp_h = dt("wp", [128, 8 * 1024], F32, kind="ExternalInput").ap()
    bqkv_h = dt("bqkv", [128, 8 * 128], F32, kind="ExternalInput").ap()
    bq_h = dt("bq", [128, 8 * 8], F32, kind="ExternalInput").ap()
    at_h = dt("at", [128, 1024], F32, kind="ExternalInput").ap()
    rw_h = dt("rw", [128, 8 * 32], F32, kind="ExternalInput").ap()
    rb_h = dt("rb", [1, 32], F32, kind="ExternalInput").ap()
    gw_h = dt("gw", [NEXP, 128, 1024], F32, kind="ExternalInput").ap()
    uw_h = dt("uw", [NEXP, 128, 1024], F32, kind="ExternalInput").ap()
    dw_h = dt("dw", [NEXP, 128, 1024], F32, kind="ExternalInput").ap()
    sg_h = dt("sg", [128, 8 * 512], F32, kind="ExternalInput").ap()
    su_h = dt("su", [128, 8 * 512], F32, kind="ExternalInput").ap()
    sd_h = dt("sd", [128, 4 * 1024], F32, kind="ExternalInput").ap()
    out_d = dt("out", [NT * 128, E], F32, kind="ExternalOutput").ap()
    l1s_d = dt("l1scr", [SGRP, 128, 4096], BF16, kind="Internal").ap()
    l1o_d = dt("l1oscr", [NG, 128, 4096], BF16, kind="Internal").ap()
    x1_d = dt("x1scr", [NT, 128, 1024], F32, kind="Internal").ap()
    wg_d = dt("wgscr", [2, 32, 1024], BF16, kind="Internal").ap()

    octx = ExitStack()
    S = Sched(nc)
    for s in ("dS", "dC", "dC2", "dW0", "dW1", "dP", "dF", "g0", "g1",
              "u0", "u1", "wr0", "wr1", "l10", "l11", "xq0", "xq1",
              "P", "A", "V", "O", "oc",
              "d0", "d1", "d2", "d3", "d4", "d5", "d6", "d7"):
        S.sem(octx, s)

    def SB(ctx, n, shp, d=F32):
        return ctx.enter_context(nc.sbuf_tensor(n, shp, d))

    ident_b = SB(octx, "ident_b", [128, 128], BF16)
    ishift_b = SB(octx, "ishift_b", [64, 256], BF16)
    ones_f = SB(octx, "ones_f", [128, 128], F32)
    ones_b = SB(octx, "ones_b", [128, 128], BF16)
    maskp_s = SB(octx, "maskp_s", [128, 256], F32)
    bias_s = SB(octx, "bias_s", [128, 512], F32)
    rb_s = SB(octx, "rb_s", [1, 32], F32)
    scal_s = SB(octx, "scal_s", [128, 4], F32)
    at_b = SB(octx, "at_b", [128, 1024], BF16)
    bqkv_b = SB(octx, "bqkv_b", [128, 8 * 128], BF16)
    bq_b = SB(octx, "bq_b", [128, 8 * 8], BF16)
    xt = [SB(octx, "xt0", [128, 1024]), SB(octx, "xt1", [128, 1024])]
    lnb = [SB(octx, "lnb0", [128, 1024], BF16), SB(octx, "lnb1", [128, 1024], BF16)]
    tmp_f = SB(octx, "tmpf", [128, 1024])
    lnscr = [dict(bn6=SB(octx, f"bn6_{i}", [128, 12]), mv=SB(octx, f"mv_{i}", [128, 2]),
                  sd=SB(octx, f"sd_{i}", [128, 1]), rs=SB(octx, f"rs_{i}", [128, 1]),
                  busy_v=None, busy_a=None) for i in range(2)]
    pA = octx.enter_context(nc.psum_tensor("pA", [128, 1024], F32))
    pB = octx.enter_context(nc.psum_tensor("pB", [128, 1024], F32))
    pC = octx.enter_context(nc.psum_tensor("pC", [128, 1024], F32))
    pT = octx.enter_context(nc.psum_tensor("pT", [128, 2048], BF16))

    ev_c = {}
    for key, dst, src in [("ident", ident_b, ident_d), ("ishift", ishift_b, ishift_d),
                          ("onesb", ones_b, ones_d), ("at", at_b, at_h),
                          ("bqkv", bqkv_b, bqkv_h), ("bq", bq_b, bq_h)]:
        ev_c[key] = S.dma("g", dst[:, :], src, sem="dC")
    for key, dst, src in [("ones", ones_f, ones_d), ("maskp", maskp_s, maskp_d),
                          ("bias", bias_s, bias_d), ("rb", rb_s, rb_h),
                          ("scal", scal_s, scal_d)]:
        ev_c[key] = S.dma("s", dst[:, :], src, sem="dC2")

    xt_free = [None, None]
    lnb_free = [None, None]
    pT_free = [None, None]
    ln_i = [0]

    def emit_ln(x_tensor, x_off, out_ap, x_evt, out_wait=()):
        x_ap = x_tensor[:, x_off:x_off + 1024]
        sc = lnscr[ln_i[0] % 2]
        ln_i[0] += 1
        S.emit("v", lambda h, a=sc["bn6"], b=x_tensor, o=x_off: h.bn_stats(
            a[:, 0:6], b[:, o:o + 512]),
            waits=[x_evt, sc["busy_v"], sc["busy_a"]])
        S.emit("v", lambda h, a=sc["bn6"], b=x_tensor, o=x_off: h.bn_stats(
            a[:, 6:12], b[:, o + 512:o + 1024]))
        S.drain("v")
        ev_mv = S.emit("v", lambda h, a=sc["mv"], b=sc["bn6"]: h.bn_aggr(a[:, :], b[:, :]), inc="V")
        ev_sd = S.emit("a", lambda h, a=sc["sd"], b=sc["mv"]: h.activation(
            out=a[:, :], in_=b[:, 1:2], func=AF.Sqrt, bias=scal_s[:, 0:1]),
            waits=[ev_mv, ev_c["scal"]], inc="A")
        S.emit("v", lambda h, a=sc["rs"], b=sc["sd"]: h.reciprocal(a[:, :], b[:, :]),
               waits=[ev_sd])
        S.drain("v")
        ev = S.emit("v", lambda h, o=out_ap, x=x_ap, m=sc["mv"], r=sc["rs"]: h.tensor_scalar(
            out=o, in0=x, scalar1=m[:, 0:1], scalar2=r[:, 0:1],
            op0=ALU.subtract, op1=ALU.mult), waits=list(out_wait), inc="V")
        sc["busy_v"] = ev
        sc["busy_a"] = ev_sd
        return ev

    def emit_ln_transpose(src_dram_row, dst_tensor, dst_off):
        i = ln_i[0] % 2
        ev_x = S.dma("a", xt[i][:, :], src_dram_row, waits=[xt_free[i]],
                     sem="xq%d" % i)
        ev_ln = emit_ln(xt[i], 0, lnb[i][:, :], ev_x, out_wait=[lnb_free[i]])
        xt_free[i] = ev_ln
        th = ln_i[0] % 2
        ev_tr = None
        for c in range(8):
            w = [ev_ln, ev_c["ident"]] + ([pT_free[th]] if c == 0 else [])
            ev_tr = S.emit("t", lambda h, c=c, i=i, th=th: h.transpose(
                pT[:, th * 1024 + c * 128: th * 1024 + (c + 1) * 128],
                in_=lnb[i][:, c * 128:(c + 1) * 128], identity=ident_b[:, :]),
                waits=w, inc="P")
        lnb_free[i] = ev_tr
        ev_cp = S.emit("a", lambda h, th=th, dst=mkap(dst_tensor, dst_off,
                       [[dst_tensor.shape[1], 128],
                        [dst_tensor.shape[1] // 8, 8], [1, 128]]): h.activation(
            out=dst, in_=pT[:, th * 1024:(th + 1) * 1024].rearrange("p (c n) -> p c n", n=128),
            func=AF.Copy), waits=[ev_tr], inc="A")
        pT_free[th] = ev_cp
        return ev_cp

    ab_ctx = ExitStack()
    O_sb = SB(ab_ctx, "O_sb", [128, 8 * 2048], BF16)
    a_ctx = ExitStack()
    t1kv = SB(a_ctx, "t1kv", [128, SGRP * 512], BF16)
    t1q = SB(a_ctx, "t1q", [8, NG * 512], BF16)
    l1t = [SB(a_ctx, "l1t0", [128, 4096], BF16), SB(a_ctx, "l1t1", [128, 4096], BF16)]
    wq_s = SB(a_ctx, "wq_s", [128, 8 * 512], BF16)
    wk_s = SB(a_ctx, "wk_s", [128, 8 * 512], BF16)
    wv_s = SB(a_ctx, "wv_s", [128, 8 * 512], BF16)
    K_sb = SB(a_ctx, "K_sb", [128, 4 * 4096], BF16)
    V_sb = SB(a_ctx, "V_sb", [128, 8 * 32 * 66], BF16)
    Q_sb = SB(a_ctx, "Q_sb", [128, 4 * 2048], BF16)
    Pt = [SB(a_ctx, "Pt0", [128, 512], BF16), SB(a_ctx, "Pt1", [128, 512], BF16)]
    rec_s = SB(a_ctx, "rec_s", [128, 512])
    bc_s = SB(a_ctx, "bc_s", [64, 512])
    On_s = SB(a_ctx, "On_s", [64, 512], BF16)

    l1t_free = [None, None]
    ev_t1_seq = [None] * SGRP
    ev_t1_own = [None] * NG
    pfree = {k: None for k in ("A0", "A1", "B0", "B1", "C0", "C1")}
    PREG = {"A0": (pA, 0), "A1": (pA, 512), "B0": (pB, 0), "B1": (pB, 512),
            "C0": (pC, 0), "C1": (pC, 512)}

    ev_vones = S.emit("v", lambda h: h.memset(
        V_sb[:, :].rearrange("p (s n) -> p s n", n=66)[:, :, 64:66], 1.0), inc="V")

    def emit_weight_half(hh, extra_wait):
        ev = None
        for dst, src in ((wq_s, wq_h), (wk_s, wk_h), (wv_s, wv_h)):
            for c in range(8):
                ev = S.dma("g", dst[:, c * 512:(c + 1) * 512],
                           src[:, c * 1024 + hh * 512: c * 1024 + hh * 512 + 512],
                           waits=extra_wait, sem="dW%d" % hh)
        return ev

    def emit_qkv_half(hh, ev_w, first):
        kv_last, q_last = None, None
        acc_regs = ["A0", "A1", "B0", "B1"]
        for g in range(SGRP):
            gb = g % 2
            if first:
                subev = []
                for s in range(4):
                    subev.append(emit_ln_transpose(
                        xseq[(g * 4 + s) * 128:(g * 4 + s + 1) * 128, :],
                        l1t[gb], s * 128))
                ev_l1 = subev[-1]
                S.dma("s", l1s_d[g], l1t[gb][:, :], waits=subev)
                for ic in range(8):
                    w = [ev_l1, ev_c["bqkv"]] + ([pfree["C0"]] if ic == 0 else [])
                    S.emit("t", lambda h, ic=ic, gb=gb: h.matmul(
                        pC[:, 0:512], lhsT=bqkv_b[:, ic * 128:(ic + 1) * 128],
                        rhs=l1t[gb][:, ic * 512:(ic + 1) * 512],
                        start=(ic == 0), stop=(ic == 7)), waits=w,
                        inc="P" if ic == 7 else None)
                ev_t1p = ("P", S.cnt["P"])
                S.emit("a", lambda h, g=g: h.activation(
                    out=t1kv[32:40, g * 512:(g + 1) * 512], in_=pC[32:40, 0:512],
                    func=AF.Copy, scale=scal_s[32:40, 1:2]),
                    waits=[ev_t1p, ev_c["scal"]], inc="A")
                ev_t1_seq[g] = S.emit("a", lambda h, g=g: h.activation(
                    out=t1kv[64:72, g * 512:(g + 1) * 512], in_=pC[64:72, 0:512],
                    func=AF.Copy, scale=scal_s[64:72, 1:2]), inc="A")
                pfree["C0"] = ev_t1_seq[g]
            else:
                ev_l1 = S.dma("s", l1t[gb][:, :], l1s_d[g], waits=[l1t_free[gb]],
                              sem="l1%d" % gb)
            for c in range(4):
                pp, off = PREG[acc_regs[c]]
                for ic in range(8):
                    w = [ev_l1, ev_w] + ([pfree[acc_regs[c]]] if ic == 0 else [])
                    S.emit("t", lambda h, c=c, ic=ic, gb=gb, pp=pp, off=off: h.matmul(
                        pp[:, off:off + 512],
                        lhsT=wk_s[:, ic * 512 + c * 128: ic * 512 + (c + 1) * 128],
                        rhs=l1t[gb][:, ic * 512:(ic + 1) * 512],
                        start=(ic == 0), stop=False), waits=w)
                S.emit("t", lambda h, c=c, g=g, hh=hh, pp=pp, off=off: h.matmul(
                    pp[:, off:off + 512],
                    lhsT=at_b[32:40, hh * 512 + c * 128: hh * 512 + (c + 1) * 128],
                    rhs=t1kv[32:40, g * 512:(g + 1) * 512],
                    start=False, stop=True),
                    waits=[ev_t1_seq[g], ev_c["at"]], inc="P")
                ev_kp = ("P", S.cnt["P"])
                kv_last = S.emit("a", lambda h, c=c, g=g, pp=pp, off=off: h.activation(
                    out=K_sb[:, c * 4096 + g * 512: c * 4096 + (g + 1) * 512],
                    in_=pp[:, off:off + 512], func=AF.Copy), waits=[ev_kp], inc="A")
                pfree[acc_regs[c]] = kv_last
            for s in range(4):
                kt = g * 4 + s
                vr = "C0" if s % 2 == 0 else "C1"
                pp, off = PREG[vr]
                for ic in range(8):
                    w = [ev_l1, ev_w] + ([pfree[vr]] if ic == 0 else [])
                    S.emit("t", lambda h, s=s, ic=ic, gb=gb, pp=pp, off=off: h.matmul(
                        pp[:, off:off + 512],
                        lhsT=l1t[gb][:, ic * 512 + s * 128: ic * 512 + (s + 1) * 128],
                        rhs=wv_s[:, ic * 512:(ic + 1) * 512],
                        start=(ic == 0), stop=False), waits=w)
                S.emit("t", lambda h, g=g, s=s, hh=hh, pp=pp, off=off: h.matmul(
                    pp[:, off:off + 512],
                    lhsT=t1kv[64:72, (g * 4 + s) * 128:(g * 4 + s + 1) * 128],
                    rhs=at_b[64:72, hh * 512:(hh + 1) * 512],
                    start=False, stop=True),
                    waits=[ev_t1_seq[g], ev_c["at"]], inc="P")
                ev_vp = ("P", S.cnt["P"])
                dst = mkap(V_sb, kt * 66, [[8 * 32 * 66, 128], [32 * 66, 8], [1, 64]])
                kv_last = S.emit("a", lambda h, pp=pp, off=off, dst=dst: h.activation(
                    out=dst, in_=pp[:, off:off + 512].rearrange("p (c n) -> p c n", n=64),
                    func=AF.Copy), waits=[ev_vp, ev_vones], inc="A")
                pfree[vr] = kv_last
            l1t_free[gb] = ("P", S.cnt["P"])
        for og in range(NG):
            gb = og % 2
            if first:
                subev = []
                for s in range(4):
                    subev.append(emit_ln_transpose(
                        xown[(og * 4 + s) * 128:(og * 4 + s + 1) * 128, :],
                        l1t[gb], s * 128))
                ev_l1 = subev[-1]
                S.dma("s", l1o_d[og], l1t[gb][:, :], waits=subev)
                for ic in range(8):
                    w = [ev_l1, ev_c["bq"]] + ([pfree["C0"]] if ic == 0 else [])
                    S.emit("t", lambda h, ic=ic, gb=gb: h.matmul(
                        pC[0:8, 0:512], lhsT=bq_b[:, ic * 8:(ic + 1) * 8],
                        rhs=l1t[gb][:, ic * 512:(ic + 1) * 512],
                        start=(ic == 0), stop=(ic == 7)), waits=w,
                        inc="P" if ic == 7 else None)
                ev_qt1p = ("P", S.cnt["P"])
                ev_t1_own[og] = S.emit("a", lambda h, og=og: h.activation(
                    out=t1q[0:8, og * 512:(og + 1) * 512], in_=pC[0:8, 0:512],
                    func=AF.Copy, scale=scal_s[0:8, 1:2]),
                    waits=[ev_qt1p, ev_c["scal"]], inc="A")
                pfree["C0"] = ev_t1_own[og]
            else:
                ev_l1 = S.dma("s", l1t[gb][:, :], l1o_d[og], waits=[l1t_free[gb]],
                              sem="l1%d" % gb)
            for c in range(4):
                pp, off = PREG[acc_regs[c]]
                for ic in range(8):
                    w = [ev_l1, ev_w] + ([pfree[acc_regs[c]]] if ic == 0 else [])
                    S.emit("t", lambda h, c=c, ic=ic, gb=gb, pp=pp, off=off: h.matmul(
                        pp[:, off:off + 512],
                        lhsT=wq_s[:, ic * 512 + c * 128: ic * 512 + (c + 1) * 128],
                        rhs=l1t[gb][:, ic * 512:(ic + 1) * 512],
                        start=(ic == 0), stop=False), waits=w)
                S.emit("t", lambda h, c=c, og=og, hh=hh, pp=pp, off=off: h.matmul(
                    pp[:, off:off + 512],
                    lhsT=at_b[0:8, hh * 512 + c * 128: hh * 512 + (c + 1) * 128],
                    rhs=t1q[0:8, og * 512:(og + 1) * 512],
                    start=False, stop=True),
                    waits=[ev_t1_own[og], ev_c["at"]], inc="P")
                ev_qp = ("P", S.cnt["P"])
                q_last = S.emit("a", lambda h, c=c, og=og, pp=pp, off=off: h.activation(
                    out=Q_sb[:, c * 2048 + og * 512: c * 2048 + (og + 1) * 512],
                    in_=pp[:, off:off + 512], func=AF.Copy), waits=[ev_qp], inc="A")
                pfree[acc_regs[c]] = q_last
            l1t_free[gb] = ("P", S.cnt["P"])
        return kv_last, q_last

    def emit_attn_half(hh, ev_ready):
        Pt_free = [None, None]
        state = dict(rec_free=None, bc_free=None, On_free=None, chunk_free=None)

        def emit_norm(gq, hl, ev_pv):
            cl, hp = hl // 2, hl % 2
            o_reg = "A0" if hp == 0 else "A1"
            ocol = (hh * 4 + cl) * 2048 + gq * 512
            ev_rec = S.emit("v", lambda h, hp=hp: h.reciprocal(
                rec_s[64:65, :], pA[64:65, hp * 512:(hp + 1) * 512]),
                waits=[ev_pv, state["rec_free"]], inc="V")
            ev_bc = S.emit("t", lambda h: h.matmul(
                pB[:, 512:1024], lhsT=ones_f[64:65, 0:128], rhs=rec_s[64:65, 0:512],
                start=True, stop=True),
                waits=[ev_rec, ev_c["ones"], pfree["B1"]], inc="P")
            state["rec_free"] = ev_bc
            ev_bce = S.emit("a", lambda h: h.activation(
                out=bc_s[:, :], in_=pB[0:64, 512:1024], func=AF.Copy),
                waits=[ev_bc, state["bc_free"]], inc="A")
            pfree["B1"] = ev_bce
            if hp == 0:
                ev_on = S.emit("v", lambda h, ocol=ocol: h.tensor_tensor(
                    out=O_sb[0:64, ocol:ocol + 512], in0=pA[0:64, 0:512],
                    in1=bc_s[:, :], op=ALU.mult), waits=[ev_bce], inc="V")
                state["bc_free"] = ev_on
                pfree["A0"] = ev_on
            else:
                ev_on = S.emit("v", lambda h: h.tensor_tensor(
                    out=On_s[:, :], in0=pA[0:64, 512:1024],
                    in1=bc_s[:, :], op=ALU.mult),
                    waits=[ev_bce, state["On_free"]], inc="V")
                state["bc_free"] = ev_on
                pfree["A1"] = ev_on
                state["On_free"] = S.dma(
                    "s", O_sb[64:128, ocol:ocol + 512], On_s[0:64, :],
                    waits=[ev_on], sem="oc")
            return state["On_free"]

        pending = None
        ev_oc = None
        for gq in range(NG):
            for hl in range(8):
                h_g = hh * 8 + hl
                cl, hp = hl // 2, hl % 2
                o_reg = "A0" if hp == 0 else "A1"
                nkt = 8 * gq + 8
                ev_pv = None
                for kt in range(nkt):
                    cs = 128 * max(0, kt // 2 - 4 * gq)
                    sr = "C0" if kt % 2 == 0 else "C1"
                    sbk = kt % 2
                    w = [ev_ready, pfree[sr]]
                    ev_s = S.emit("t", lambda h, hp=hp, cl=cl, kt=kt, gq=gq, cs=cs, sbk=sbk: h.matmul(
                        pC[:, sbk * 512 + cs:(sbk + 1) * 512],
                        lhsT=K_sb[hp * 64:(hp + 1) * 64,
                                  cl * 4096 + kt * 128: cl * 4096 + (kt + 1) * 128],
                        rhs=Q_sb[hp * 64:(hp + 1) * 64,
                                 cl * 2048 + gq * 512 + cs: cl * 2048 + (gq + 1) * 512],
                        start=True, stop=True), waits=w, inc="P")
                    ev_m = None
                    if kt >= 8 * gq:
                        jm = kt // 2 - 4 * gq
                        ev_m = S.emit("v", lambda h, sbk=sbk, jm=jm, kt=kt: h.tensor_tensor(
                            out=pC[:, sbk * 512 + jm * 128: sbk * 512 + (jm + 1) * 128],
                            in0=pC[:, sbk * 512 + jm * 128: sbk * 512 + (jm + 1) * 128],
                            in1=maskp_s[:, (kt % 2) * 128:(kt % 2 + 1) * 128],
                            op=ALU.add), waits=[ev_s, ev_c["maskp"]], inc="V")
                    ev_e = S.emit("a", lambda h, sbk=sbk, cs=cs, h_g=h_g, kt=kt: h.activation(
                        out=Pt[sbk][:, cs:512], in_=pC[:, sbk * 512 + cs:(sbk + 1) * 512],
                        func=AF.Exp, bias=bias_s[:, h_g * 32 + kt: h_g * 32 + kt + 1],
                        scale=scal_s[:, 2:3]),
                        waits=[ev_s, ev_m, ev_c["bias"], ev_c["scal"], Pt_free[sbk]], inc="A")
                    pfree[sr] = ev_e
                    w2 = [ev_e] + ([pfree[o_reg]] if kt == 0 else [])
                    ev_pv = S.emit("t", lambda h, hl=hl, kt=kt, cs=cs, sbk=sbk, hp=hp, nkt=nkt: h.matmul(
                        pA[0:65, hp * 512 + cs:(hp + 1) * 512],
                        lhsT=V_sb[:, (hl * 32 + kt) * 66: (hl * 32 + kt) * 66 + 65],
                        rhs=Pt[sbk][:, cs:512],
                        start=(kt == 0), stop=(kt == nkt - 1)), waits=w2, inc="P")
                    Pt_free[sbk] = ev_pv
                if pending is not None:
                    ev_oc = emit_norm(*pending) or ev_oc
                pending = (gq, hl, ev_pv)
        ev_oc = emit_norm(*pending) or ev_oc
        return [ev_oc, ("V", S.cnt["V"])]

    ev_w0 = emit_weight_half(0, [])
    kv0, q0 = emit_qkv_half(0, ev_w0, first=True)
    ev_oc0 = emit_attn_half(0, q0)
    ev_w1 = emit_weight_half(1, [("P", S.cnt["P"])])
    kv1, q1 = emit_qkv_half(1, ev_w1, first=False)
    ev_oc1 = emit_attn_half(1, q1)

    a_ctx.close()
    S.set_floor()
    b_ctx = ExitStack()
    wp_s = SB(b_ctx, "wp_s", [128, 8 * 1024], BF16)
    at_f = [SB(b_ctx, "atf0", [128, 1024]), SB(b_ctx, "atf1", [128, 1024])]
    xo_f = [SB(b_ctx, "xof0", [128, 1024]), SB(b_ctx, "xof1", [128, 1024])]
    x1t = [SB(b_ctx, "x1t0", [128, 1024]), SB(b_ctx, "x1t1", [128, 1024])]
    ev_wp = None
    for ic in range(8):
        ev_wp = S.dma("g", wp_s[:, ic * 1024:(ic + 1) * 1024],
                      wp_h[:, ic * 1024:(ic + 1) * 1024], sem="dP")
    pacc_free = [None, None]
    at_free = [None, None]
    xo_free = [None, None]
    x1t_free = [None, None]
    ev_x1 = [None] * NT
    for j in range(NT):
        pb = j % 2
        pp = pA if pb == 0 else pB
        ev_xo = S.dma("a", xo_f[pb][:, :], xown[j * 128:(j + 1) * 128, :],
                      waits=[xo_free[pb]], sem="xq%d" % pb)
        for ic in range(8):
            for hf in range(2):
                w = list(ev_oc1) + [ev_wp] + ([pacc_free[pb]] if (ic == 0 and hf == 0) else [])
                S.emit("t", lambda h, ic=ic, hf=hf, j=j, pp=pp: h.matmul(
                    pp[:, hf * 512:(hf + 1) * 512],
                    lhsT=O_sb[:, ic * 2048 + j * 128:ic * 2048 + (j + 1) * 128],
                    rhs=wp_s[:, ic * 1024 + hf * 512: ic * 1024 + (hf + 1) * 512],
                    start=(ic == 0), stop=(ic == 7)),
                    waits=w, inc="P" if (ic == 7 and hf == 1) else None)
        ev_wpp = ("P", S.cnt["P"])
        ev_ae = S.emit("a", lambda h, pb=pb, pp=pp: h.activation(
            out=at_f[pb][:, :], in_=pp[:, :], func=AF.Copy),
            waits=[ev_wpp, at_free[pb]], inc="A")
        pacc_free[pb] = ev_ae
        ev_ln = emit_ln(at_f[pb], 0, tmp_f[:, :], ev_ae)
        at_free[pb] = ev_ln
        S.drain("v")
        ev_x1t = S.emit("v", lambda h, pb=pb: h.tensor_tensor(
            out=x1t[pb][:, :], in0=tmp_f[:, :], in1=xo_f[pb][:, :], op=ALU.add),
            waits=[ev_xo, x1t_free[pb]], inc="V")
        xo_free[pb] = ev_x1t
        ev_x1[j] = S.dma("s", x1_d[j], x1t[pb][:, :], waits=[ev_x1t])
        x1t_free[pb] = ev_x1[j]

    if DEBUG_STOP_AT_X1:
        for j in range(NT):
            S.dma("s", out_d[j * 128:(j + 1) * 128, :], x1_d[j],
                  waits=[ev_x1[j]], sem="O")
        S.emit("s", lambda h: None, waits=[("O", S.cnt["O"])])
        b_ctx.close(); ab_ctx.close()
        with octx, nc.Block() as block:
            S.replay(block)
        return nc

    b_ctx.close()
    ab_ctx.close()
    S.set_floor()
    c_ctx = ExitStack()
    ln2T = SB(c_ctx, "ln2T", [128, 8 * 1024], BF16)
    rw_s = SB(c_ctx, "rw_s", [128, 8 * 32], BF16)
    sgw_s = SB(c_ctx, "sgw_s", [128, 8 * 512], BF16)
    suw_s = SB(c_ctx, "suw_s", [128, 8 * 512], BF16)
    sdw_s = SB(c_ctx, "sdw_s", [128, 4 * 1024], BF16)
    gw_s = [SB(c_ctx, "gw_s0", [128, 1024], BF16), SB(c_ctx, "gw_s1", [128, 1024], BF16)]
    uw_s = [SB(c_ctx, "uw_s0", [128, 1024], BF16), SB(c_ctx, "uw_s1", [128, 1024], BF16)]
    dw_s = [SB(c_ctx, f"dw_s{i}", [128, 1024], BF16) for i in range(8)]
    sc_f = SB(c_ctx, "sc_f", [128, 32])
    cur_f = SB(c_ctx, "cur_f", [128, 32])
    msk_f = SB(c_ctx, "msk_f", [128, 32])
    m_f = SB(c_ctx, "m_f", [128, 1])
    wu_f = SB(c_ctx, "wu_f", [128, 32])
    w_b = SB(c_ctx, "w_b", [128, 32], BF16)
    wfm = SB(c_ctx, "wfm", [32, 1024], BF16)
    wrow = [SB(c_ctx, "wrow0", [1, 1024], BF16), SB(c_ctx, "wrow1", [1, 1024], BF16)]
    sg_b = SB(c_ctx, "sg_b", [128, 1024], BF16)
    wb_b = SB(c_ctx, "wb_b", [128, 1024], BF16)
    h1_b = SB(c_ctx, "h1_b", [128, 1024], BF16)
    h_blk = SB(c_ctx, "h_blk", [128, 8 * 1024], BF16)
    hsh = SB(c_ctx, "hsh", [128, 4 * 1024], BF16)
    moe = SB(c_ctx, "moe", [128, 8 * 1024])
    out_f = [SB(c_ctx, "outf0", [128, 1024]), SB(c_ctx, "outf1", [128, 1024])]
    wdbg = SB(c_ctx, "wdbg", [128, 32])
    x1e = [SB(c_ctx, "x1e0", [128, 1024]), SB(c_ctx, "x1e1", [128, 1024])]

    ev_rw = S.dma("g", rw_s[:, :], rw_h, sem="dF")
    ev_sg = S.dma("g", sgw_s[:, :], sg_h, sem="dF")
    ev_su = S.dma("g", suw_s[:, :], su_h, sem="dF")
    ev_sd = S.dma("g", sdw_s[:, :], sd_h, sem="dF")

    def l2rhs(ic, hf):
        return mkap(ln2T, ic * 1024 + hf * 512, [[8192, 128], [1, 512]])

    ev_moe = [None] * 8
    w_b_free = [None]
    gw_free = [None, None]
    uw_free = [None, None]
    dw_free = [None] * 8
    wrow_free = [None, None]
    h_blk_free = [None] * 8
    sg_free, wb_free, h1_free = None, None, None
    x1e_free = [None, None]
    out_free = [None, None]
    topk_free = None
    for G in range(2):
        ev_l2_last = None
        ev_wfm = []
        pfree["C0"] = pfree.get("C0")
        for st in range(8):
            j = G * 8 + st
            pb = st % 2
            ev_x1l = S.dma("a", x1e[pb][:, :], x1_d[j], waits=[ev_x1[j], x1e_free[pb]],
                           sem="xq%d" % pb)
            ev_ln = emit_ln(x1e[pb], 0, lnb[pb][:, :], ev_x1l,
                            out_wait=[lnb_free[pb]])
            x1e_free[pb] = ev_ln
            th = st % 2
            ev_tr = None
            for c in range(8):
                w = [ev_ln] + ([pT_free[th]] if c == 0 else [])
                ev_tr = S.emit("t", lambda h, c=c, pb=pb, th=th: h.transpose(
                    pT[:, th * 1024 + c * 128: th * 1024 + (c + 1) * 128],
                    in_=lnb[pb][:, c * 128:(c + 1) * 128], identity=ident_b[:, :]),
                    waits=w, inc="P")
            lnb_free[pb] = ev_tr
            ev_l2_last = S.emit("a", lambda h, th=th, dst=mkap(
                ln2T, st * 128, [[8192, 128], [1024, 8], [1, 128]]): h.activation(
                out=dst, in_=pT[:, th * 1024:(th + 1) * 1024].rearrange("p (c n) -> p c n", n=128),
                func=AF.Copy), waits=[ev_tr], inc="A")
            pT_free[th] = ev_l2_last
            for ic in range(8):
                w = [ev_l2_last, ev_rw] + ([pfree["C0"]] if ic == 0 else [])
                S.emit("t", lambda h, ic=ic, st=st: h.matmul(
                    pC[:, 0:32],
                    lhsT=mkap(ln2T, ic * 1024 + st * 128, [[8192, 128], [1, 128]]),
                    rhs=rw_s[:, ic * 32:(ic + 1) * 32],
                    start=(ic == 0), stop=False), waits=w)
            S.emit("t", lambda h: h.matmul(
                pC[:, 0:32], lhsT=ones_f[0:1, 0:128], rhs=rb_s[0:1, :],
                start=False, stop=True), waits=[ev_c["rb"], ev_c["ones"]], inc="P")
            ev_rt = ("P", S.cnt["P"])
            ev_nm = S.emit("v", lambda h: h.reduce_max(
                m_f[:, :], pC[:, 0:32], axis=AX.X, negate=True),
                waits=[ev_rt, topk_free], inc="V")
            ev_sc = S.emit("a", lambda h: h.activation(
                out=sc_f[:, :], in_=pC[:, 0:32], func=AF.Exp, bias=m_f[:, 0:1]),
                waits=[ev_nm], inc="A")
            pfree["C0"] = ev_sc
            S.emit("v", lambda h: h.tensor_copy(cur_f[:, :], sc_f[:, :]), waits=[ev_sc])
            for it in range(3):
                S.drain("v")
                S.emit("v", lambda h: h.reduce_max(m_f[:, :], cur_f[:, :], axis=AX.X),
                       waits=[ev_sc])
                S.drain("v")
                S.emit("v", lambda h: h.tensor_scalar(
                    out=msk_f[:, :], in0=cur_f[:, :], scalar1=m_f[:, 0:1],
                    scalar2=1e30, op0=ALU.is_ge, op1=ALU.mult))
                S.drain("v")
                S.emit("v", lambda h: h.tensor_sub(cur_f[:, :], cur_f[:, :], msk_f[:, :]))
            S.drain("v")
            S.emit("v", lambda h: h.reduce_max(m_f[:, :], cur_f[:, :], axis=AX.X))
            S.drain("v")
            S.emit("v", lambda h: h.tensor_scalar(
                out=msk_f[:, :], in0=sc_f[:, :], scalar1=m_f[:, 0:1],
                scalar2=None, op0=ALU.is_ge))
            S.drain("v")
            S.emit("v", lambda h: h.tensor_mul(wu_f[:, :], sc_f[:, :], msk_f[:, :]))
            S.drain("v")
            S.emit("v", lambda h: h.reduce_sum(m_f[:, :], wu_f[:, :], axis=AX.X))
            S.drain("v")
            S.emit("v", lambda h: h.reciprocal(m_f[:, :], m_f[:, :]))
            S.drain("v")
            ev_wb = S.emit("v", lambda h: h.tensor_scalar(
                out=w_b[:, :], in0=wu_f[:, :], scalar1=m_f[:, 0:1], scalar2=None,
                op0=ALU.mult), waits=[w_b_free[0]], inc="V")
            topk_free = ev_wb
            ev_wt = S.emit("t", lambda h, th=th: h.transpose(
                pT[0:32, th * 1024: th * 1024 + 128],
                in_=w_b[:, :], identity=ident_b[:, :]),
                waits=[ev_wb, pT_free[th]], inc="P")
            ev_wf = S.emit("a", lambda h, st=st, th=th: h.activation(
                out=wfm[0:32, st * 128:(st + 1) * 128],
                in_=pT[0:32, th * 1024: th * 1024 + 128],
                func=AF.Copy), waits=[ev_wt], inc="A")
            ev_wfm.append(ev_wf)
            pT_free[th] = ev_wf
            w_b_free[0] = ev_wt
            if DEBUG_FFN_W:
                ev_wd = S.emit("v", lambda h: h.tensor_copy(wdbg[:, :], w_b[:, :]),
                               waits=[ev_wb], inc="V")
                S.dma("s", out_d[j * 128:(j + 1) * 128, 0:32], wdbg[:, :],
                      waits=[ev_wd], sem="O")
        ev_wsc = S.dma("s", wg_d[G], wfm[:, :], waits=ev_wfm)
        if DEBUG_FFN_W:
            continue
        for blk in range(DEBUG_NBLK):
            ev_h_all = None
            for e8 in range(8):
                e = blk * 8 + e8
                eb = e % 2
                ev_gw = S.dma("g", gw_s[eb][:, :], gw_h[e],
                              waits=[gw_free[eb]], sem="g%d" % eb)
                ev_uw = S.dma("g", uw_s[eb][:, :], uw_h[e],
                              waits=[uw_free[eb]], sem="u%d" % eb)
                ev_dwl = S.dma("g", dw_s[e8][:, :], dw_h[e],
                               waits=[dw_free[e8]], sem="d%d" % e8)
                dw_free[e8] = ev_dwl
                ev_wr = S.dma("s", wrow[eb][0:1, :], wg_d[G][e:e + 1, :],
                              waits=[ev_wsc, wrow_free[eb]], sem="wr%d" % eb)
                for ic in range(8):
                    for hf in range(2):
                        w = [ev_gw, ev_l2_last] + (
                            [pfree["A0"], pfree["A1"]] if (ic == 0 and hf == 0) else [])
                        S.emit("t", lambda h, ic=ic, hf=hf, eb=eb: h.matmul(
                            pA[:, hf * 512:(hf + 1) * 512],
                            lhsT=gw_s[eb][:, ic * 128:(ic + 1) * 128],
                            rhs=l2rhs(ic, hf), start=(ic == 0), stop=(ic == 7)),
                            waits=w, inc="P" if (ic == 7 and hf == 1) else None)
                ev_g = ("P", S.cnt["P"])
                gw_free[eb] = ev_g
                for ic in range(8):
                    for hf in range(2):
                        w = [ev_uw, ev_l2_last] + (
                            [pfree["B0"], pfree["B1"]] if (ic == 0 and hf == 0) else [])
                        S.emit("t", lambda h, ic=ic, hf=hf, eb=eb: h.matmul(
                            pB[:, hf * 512:(hf + 1) * 512],
                            lhsT=uw_s[eb][:, ic * 128:(ic + 1) * 128],
                            rhs=l2rhs(ic, hf), start=(ic == 0), stop=(ic == 7)),
                            waits=w, inc="P" if (ic == 7 and hf == 1) else None)
                ev_u = ("P", S.cnt["P"])
                uw_free[eb] = ev_u
                for hf in range(2):
                    w = [ev_wr, ev_c["onesb"]] + (
                        [pfree["C0"], pfree["C1"]] if hf == 0 else [])
                    S.emit("t", lambda h, hf=hf, eb=eb: h.matmul(
                        pC[:, hf * 512:(hf + 1) * 512], lhsT=ones_b[0:1, 0:128],
                        rhs=wrow[eb][0:1, hf * 512:(hf + 1) * 512],
                        start=True, stop=True), waits=w,
                        inc="P" if hf == 1 else None)
                ev_wbp = ("P", S.cnt["P"])
                wrow_free[eb] = ev_wbp
                ev_sgb = S.emit("a", lambda h: h.activation(
                    out=sg_b[:, :], in_=pA[:, :], func=AF.Silu),
                    waits=[ev_g, sg_free], inc="A")
                pfree["A0"] = pfree["A1"] = ev_sgb
                ev_wbe = S.emit("a", lambda h: h.activation(
                    out=wb_b[:, :], in_=pC[:, :], func=AF.Copy),
                    waits=[ev_wbp, wb_free], inc="A")
                pfree["C0"] = pfree["C1"] = ev_wbe
                ev_h1 = S.emit("v", lambda h: h.tensor_tensor(
                    out=h1_b[:, :], in0=sg_b[:, :], in1=pB[:, :], op=ALU.mult),
                    waits=[ev_sgb, ev_u, h1_free], inc="V")
                sg_free = ev_h1
                pfree["B0"] = pfree["B1"] = ev_h1
                ev_h = S.emit("v", lambda h, e8=e8: h.tensor_tensor(
                    out=h_blk[:, e8 * 1024:(e8 + 1) * 1024], in0=h1_b[:, :],
                    in1=wb_b[:, :], op=ALU.mult),
                    waits=[ev_h1, ev_wbe, h_blk_free[e8]], inc="V")
                h1_free = ev_h
                wb_free = ev_h
                ev_h_all = ev_h
            dn_regs = [("C0", "C1", pC), ("A0", "A1", pA)]
            for st in range(8):
                r0, r1, pp = dn_regs[st % 2]
                for e8 in range(8):
                    w = [ev_h_all, dw_free[e8]] + (
                        [pfree[r0], pfree[r1]] if e8 == 0 else [])
                    S.emit("t", lambda h, e8=e8, st=st, pp=pp: h.matmul(
                        pp[:, 0:512],
                        lhsT=h_blk[:, e8 * 1024 + st * 128: e8 * 1024 + (st + 1) * 128],
                        rhs=dw_s[e8][:, 0:512],
                        start=(e8 == 0), stop=False), waits=w)
                    S.emit("t", lambda h, e8=e8, st=st, pp=pp: h.matmul(
                        pp[:, 512:1024],
                        lhsT=h_blk[:, e8 * 1024 + st * 128: e8 * 1024 + (st + 1) * 128],
                        rhs=dw_s[e8][:, 512:1024],
                        start=(e8 == 0), stop=(e8 == 7)),
                        inc="P" if e8 == 7 else None)
                ev_dn = ("P", S.cnt["P"])
                if blk == 0:
                    ev_mo = S.emit("v", lambda h, st=st, pp=pp: h.tensor_copy(
                        moe[:, st * 1024:(st + 1) * 1024], pp[:, :]),
                        waits=[ev_dn, ev_moe[st]], inc="V")
                else:
                    ev_mo = S.emit("v", lambda h, st=st, pp=pp: h.tensor_tensor(
                        out=moe[:, st * 1024:(st + 1) * 1024],
                        in0=moe[:, st * 1024:(st + 1) * 1024], in1=pp[:, :],
                        op=ALU.add), waits=[ev_dn, ev_moe[st]], inc="V")
                ev_moe[st] = ev_mo
                pfree[r0] = pfree[r1] = ev_mo
            ev_dn_done = ("P", S.cnt["P"])
            for e8 in range(8):
                h_blk_free[e8] = ev_dn_done
                dw_free[e8] = ev_dn_done
        for hc in range(0 if DEBUG_NO_SHARED else 4):
            for ic in range(8):
                for hf in range(2):
                    w = [ev_sg, ev_l2_last] + (
                        [pfree["A0"], pfree["A1"]] if (ic == 0 and hf == 0) else [])
                    S.emit("t", lambda h, ic=ic, hf=hf, hc=hc: h.matmul(
                        pA[:, hf * 512:(hf + 1) * 512],
                        lhsT=sgw_s[:, ic * 512 + hc * 128: ic * 512 + (hc + 1) * 128],
                        rhs=l2rhs(ic, hf), start=(ic == 0), stop=(ic == 7)),
                        waits=w, inc="P" if (ic == 7 and hf == 1) else None)
            ev_g = ("P", S.cnt["P"])
            for ic in range(8):
                for hf in range(2):
                    w = [ev_su, ev_l2_last] + (
                        [pfree["B0"], pfree["B1"]] if (ic == 0 and hf == 0) else [])
                    S.emit("t", lambda h, ic=ic, hf=hf, hc=hc: h.matmul(
                        pB[:, hf * 512:(hf + 1) * 512],
                        lhsT=suw_s[:, ic * 512 + hc * 128: ic * 512 + (hc + 1) * 128],
                        rhs=l2rhs(ic, hf), start=(ic == 0), stop=(ic == 7)),
                        waits=w, inc="P" if (ic == 7 and hf == 1) else None)
            ev_u = ("P", S.cnt["P"])
            ev_sgb = S.emit("a", lambda h: h.activation(
                out=sg_b[:, :], in_=pA[:, :], func=AF.Silu),
                waits=[ev_g, sg_free], inc="A")
            pfree["A0"] = pfree["A1"] = ev_sgb
            ev_h = S.emit("v", lambda h, hc=hc: h.tensor_tensor(
                out=hsh[:, hc * 1024:(hc + 1) * 1024], in0=sg_b[:, :], in1=pB[:, :],
                op=ALU.mult), waits=[ev_sgb, ev_u], inc="V")
            sg_free = ev_h
            pfree["B0"] = pfree["B1"] = ev_h
        if DEBUG_NO_SHARED:
            for hc in []: pass
        ev_hsh = ("V", S.cnt["V"])
        dn_regs = [("C0", "C1", pC), ("A0", "A1", pA)]
        for st in range(0 if DEBUG_NO_SHARED else 8):
            r0, r1, pp = dn_regs[st % 2]
            for hc in range(4):
                w = [ev_hsh, ev_sd] + ([pfree[r0], pfree[r1]] if hc == 0 else [])
                S.emit("t", lambda h, hc=hc, st=st, pp=pp: h.matmul(
                    pp[:, 0:512],
                    lhsT=hsh[:, hc * 1024 + st * 128: hc * 1024 + (st + 1) * 128],
                    rhs=sdw_s[:, hc * 1024: hc * 1024 + 512],
                    start=(hc == 0), stop=False), waits=w)
                S.emit("t", lambda h, hc=hc, st=st, pp=pp: h.matmul(
                    pp[:, 512:1024],
                    lhsT=hsh[:, hc * 1024 + st * 128: hc * 1024 + (st + 1) * 128],
                    rhs=sdw_s[:, hc * 1024 + 512: (hc + 1) * 1024],
                    start=(hc == 0), stop=(hc == 3)),
                    inc="P" if hc == 3 else None)
            ev_dn = ("P", S.cnt["P"])
            ev_mo = S.emit("v", lambda h, st=st, pp=pp: h.tensor_tensor(
                out=moe[:, st * 1024:(st + 1) * 1024],
                in0=moe[:, st * 1024:(st + 1) * 1024], in1=pp[:, :], op=ALU.add),
                waits=[ev_dn, ev_moe[st]], inc="V")
            ev_moe[st] = ev_mo
            pfree[r0] = pfree[r1] = ev_mo
        for st in range(8):
            j = G * 8 + st
            pb = st % 2
            if DEBUG_FFN_RAW:
                out_free[pb] = S.dma("s", out_d[j * 128:(j + 1) * 128, :],
                                     moe[:, st * 1024:(st + 1) * 1024],
                                     waits=[ev_moe[st], out_free[pb]], sem="O")
                continue
            ev_x1l = S.dma("a", x1e[pb][:, :], x1_d[j], waits=[x1e_free[pb]],
                           sem="xq%d" % pb)
            ev_ln = emit_ln(moe, st * 1024, tmp_f[:, :], ev_moe[st])
            ev_moe[st] = ev_ln
            S.drain("v")
            ev_o = S.emit("v", lambda h, pb=pb: h.tensor_tensor(
                out=out_f[pb][:, :], in0=tmp_f[:, :], in1=x1e[pb][:, :], op=ALU.add),
                waits=[ev_x1l, out_free[pb]], inc="V")
            x1e_free[pb] = ev_o
            out_free[pb] = S.dma("s", out_d[j * 128:(j + 1) * 128, :], out_f[pb][:, :],
                                 waits=[ev_o], sem="O")
    S.emit("s", lambda h: None, waits=[("O", S.cnt["O"])])

    c_ctx.close()
    with octx, nc.Block() as block:
        S.replay(block)
    return nc


_NC_CACHE = {}


def _chunked(w, cols):
    return np.ascontiguousarray(
        w.reshape(8, 128, cols).transpose(1, 0, 2).reshape(128, 8 * cols))


def kernel(L0_attn_x, L0_attn_Wq, L0_attn_Aq, L0_attn_Bq, L0_attn_Wk, L0_attn_Ak,
           L0_attn_Bk, L0_attn_Wv, L0_attn_Av, L0_attn_Bv, L0_attn_Wp,
           L0_ffn_router_w, L0_ffn_router_bias, L0_ffn_gate_w, L0_ffn_up_w,
           L0_ffn_down_w, L0_ffn_sgate_w, L0_ffn_sup_w, L0_ffn_sdown_w):
    f = lambda a: np.ascontiguousarray(np.asarray(a, dtype=np.float32))
    X = f(L0_attn_x)
    wq = _chunked(f(L0_attn_Wq).T, 1024)
    wk = _chunked(f(L0_attn_Wk).T, 1024)
    wv = _chunked(f(L0_attn_Wv).T, 1024)
    wp = _chunked(f(L0_attn_Wp).T, 1024)
    Mkv = np.zeros((1024, 128), np.float32)
    Mkv[:, 32:40] = f(L0_attn_Bk).T
    Mkv[:, 64:72] = f(L0_attn_Bv).T
    bqkv = _chunked(Mkv, 128)
    bq = _chunked(f(L0_attn_Bq).T, 8)
    aT = np.zeros((128, 1024), np.float32)
    aT[0:8] = f(L0_attn_Aq).T
    aT[32:40] = f(L0_attn_Ak).T
    aT[64:72] = f(L0_attn_Av).T
    rw = _chunked(f(L0_ffn_router_w).T, 32)
    rb = f(L0_ffn_router_bias).reshape(1, 32)
    gw = np.ascontiguousarray(
        f(L0_ffn_gate_w).reshape(32, 128, 8, 128).transpose(0, 3, 2, 1).reshape(32, 128, 1024))
    uw = np.ascontiguousarray(
        f(L0_ffn_up_w).reshape(32, 128, 8, 128).transpose(0, 3, 2, 1).reshape(32, 128, 1024))
    dw = np.ascontiguousarray(f(L0_ffn_down_w).transpose(0, 2, 1))
    sg = _chunked(f(L0_ffn_sgate_w).T, 512)
    su = _chunked(f(L0_ffn_sup_w).T, 512)
    sd = np.ascontiguousarray(
        f(L0_ffn_sdown_w).T.reshape(4, 128, 1024).transpose(1, 0, 2).reshape(128, 4096))
    ident = np.eye(128, dtype=np.float32)
    ones = np.ones((128, 128), np.float32)
    scal = np.zeros((128, 4), np.float32)
    scal[:, 0] = EPS
    scal[:, 1] = 2.0
    scal[:, 2] = 0.125
    ishift = np.zeros((64, 256), np.float32)
    for d in range(64):
        ishift[d, d] = 1.0
        ishift[d, 128 + 64 + d] = 1.0
    slopes = np.array([2.0 ** (-(i + 1) / 2.0) for i in range(H)], np.float64)
    biastab = np.zeros((128, 512), np.float32)
    p = np.arange(128, dtype=np.float64)
    for hh in range(H):
        for kt in range(32):
            biastab[:, hh * 32 + kt] = (-slopes[hh] * (kt * 128 + p)).astype(np.float32)
    tri = np.where(np.arange(128)[:, None] > np.arange(128)[None, :],
                   NEGBIG, 0.0).astype(np.float32)
    full = np.full((128, 128), NEGBIG, np.float32)
    zero = np.zeros((128, 128), np.float32)
    maskr = [np.concatenate([tri, full], axis=1), np.concatenate([zero, tri], axis=1)]

    if "nc" not in _NC_CACHE:
        _NC_CACHE["nc"] = build_program()
    nc = _NC_CACHE["nc"]

    in_maps = []
    for c in range(8):
        b, r = c // 2, c % 2
        xs = np.ascontiguousarray(X[b])
        own_rows = np.concatenate(
            [np.arange((2 * j + r) * 128, (2 * j + r + 1) * 128) for j in range(NT)])
        xo = np.ascontiguousarray(xs[own_rows])
        in_maps.append(dict(
            xseq=xs, xown=xo, maskp=maskr[r], biastab=biastab, identc=ident,
            ishift=ishift, onesc=ones, scal=scal, wq=wq, wk=wk, wv=wv, wp=wp, bqkv=bqkv,
            bq=bq, at=aT, rw=rw, rb=rb, gw=gw, uw=uw, dw=dw, sg=sg, su=su, sd=sd))
    res = run_bass_kernel_spmd(nc, in_maps, core_ids=list(range(8)))
    out = np.empty((B, T, E), np.float32)
    for c in range(8):
        b, r = c // 2, c % 2
        oc = res.results[c]["out"]
        for j in range(NT):
            out[b, (2 * j + r) * 128:(2 * j + r + 1) * 128, :] = oc[j * 128:(j + 1) * 128, :]
    return out


# version 03
# speedup vs baseline: 1.0737x; 1.0011x over previous
import dataclasses
import math
from contextlib import ExitStack

import numpy as np
import concourse.bass as bass
import concourse.mybir as mybir
from concourse.bass_utils import run_bass_kernel_spmd

F32 = mybir.dt.float32
BF16 = mybir.dt.bfloat16
AF = mybir.ActivationFunctionType
ALU = mybir.AluOpType
AX = mybir.AxisListType

B, T, E, H, DH = 4, 4096, 1024, 16, 64
NEXP, HS = 32, 512
NT, NG, SGRP = 16, 4, 8
EPS = 1e-5
NEGBIG = -1e30
DEBUG_STOP_AT_X1 = False
DEBUG_FFN_RAW = False
DEBUG_FFN_W = False
DEBUG_NBLK = 4
DEBUG_NO_SHARED = False


def mkap(handle, off, pattern):
    a = handle[:, :] if len(handle.shape) == 2 else handle[:]
    return dataclasses.replace(a, offset=a.offset + off, ap=pattern)


class Sched:
    ENGS = ("s", "g", "t", "a", "v")

    def __init__(self, nc):
        self.nc = nc
        self.q = {e: [] for e in self.ENGS}
        self.sems = {}
        self.cnt = {}
        self.floor = {}

    def sem(self, ctx, name):
        self.sems[name] = ctx.enter_context(self.nc.semaphore(name))
        self.cnt[name] = 0

    def set_floor(self):
        self.floor = dict(self.cnt)

    def emit(self, eng, fn, waits=(), inc=None, amt=1):
        ws = dict(self.floor)
        for w in waits:
            if w is None:
                continue
            sname, val = w
            assert val <= self.cnt[sname], (sname, val, self.cnt[sname])
            if ws.get(sname, 0) < val:
                ws[sname] = val
        evt = None
        if inc is not None:
            self.cnt[inc] += amt
            evt = (inc, self.cnt[inc])
        self.q[eng].append((fn, tuple(ws.items()), (inc, amt) if inc else None))
        return evt

    def dma(self, eng, out, in_, waits=(), sem="dS"):
        return self.emit(eng, lambda h, o=out, i=in_: h.dma_start(out=o, in_=i),
                         waits=waits, inc=sem, amt=16)

    def drain(self, eng):
        self.emit(eng, lambda h: h.drain())

    def replay(self, block):
        decos = {"s": block.sync, "g": block.gpsimd, "t": block.tensor,
                 "a": block.scalar, "v": block.vector}
        for ek in self.ENGS:
            ops = self.q[ek]
            if not ops:
                continue
            def body(h, ops=ops):
                hwm = {}
                for fn, waits, inc in ops:
                    for sname, val in waits:
                        if hwm.get(sname, 0) < val:
                            h.wait_ge(self.sems[sname], val)
                            hwm[sname] = val
                    ins = fn(h)
                    if inc is not None and ins is not None:
                        ins.then_inc(self.sems[inc[0]], inc[1])
            decos[ek](body)


def build_program():
    nc = bass.Bass("TRN2", target_bir_lowering=False)
    dt = nc.dram_tensor
    xseq = dt("xseq", [T, E], F32, kind="ExternalInput").ap()
    xown = dt("xown", [NT * 128, E], F32, kind="ExternalInput").ap()
    maskp_d = dt("maskp", [128, 256], F32, kind="ExternalInput").ap()
    bias_d = dt("biastab", [128, 512], F32, kind="ExternalInput").ap()
    ident_d = dt("identc", [128, 128], F32, kind="ExternalInput").ap()
    ishift_d = dt("ishift", [64, 256], F32, kind="ExternalInput").ap()
    ones_d = dt("onesc", [128, 128], F32, kind="ExternalInput").ap()
    scal_d = dt("scal", [128, 4], F32, kind="ExternalInput").ap()
    wq_h = dt("wq", [128, 8 * 1024], F32, kind="ExternalInput").ap()
    wk_h = dt("wk", [128, 8 * 1024], F32, kind="ExternalInput").ap()
    wv_h = dt("wv", [128, 8 * 1024], F32, kind="ExternalInput").ap()
    wp_h = dt("wp", [128, 8 * 1024], F32, kind="ExternalInput").ap()
    bqkv_h = dt("bqkv", [128, 8 * 128], F32, kind="ExternalInput").ap()
    bq_h = dt("bq", [128, 8 * 8], F32, kind="ExternalInput").ap()
    at_h = dt("at", [128, 1024], F32, kind="ExternalInput").ap()
    rw_h = dt("rw", [128, 8 * 32], F32, kind="ExternalInput").ap()
    rb_h = dt("rb", [1, 32], F32, kind="ExternalInput").ap()
    gw_h = dt("gw", [NEXP, 128, 1024], F32, kind="ExternalInput").ap()
    uw_h = dt("uw", [NEXP, 128, 1024], F32, kind="ExternalInput").ap()
    dw_h = dt("dw", [NEXP, 128, 1024], F32, kind="ExternalInput").ap()
    sg_h = dt("sg", [128, 8 * 512], F32, kind="ExternalInput").ap()
    su_h = dt("su", [128, 8 * 512], F32, kind="ExternalInput").ap()
    sd_h = dt("sd", [128, 4 * 1024], F32, kind="ExternalInput").ap()
    out_d = dt("out", [NT * 128, E], F32, kind="ExternalOutput").ap()
    l1s_d = dt("l1scr", [SGRP, 128, 4096], BF16, kind="Internal").ap()
    l1o_d = dt("l1oscr", [NG, 128, 4096], BF16, kind="Internal").ap()
    x1_d = dt("x1scr", [NT, 128, 1024], F32, kind="Internal").ap()
    wg_d = dt("wgscr", [2, 32, 1024], BF16, kind="Internal").ap()

    octx = ExitStack()
    S = Sched(nc)
    for s in ("dS", "dC", "dC2", "dW0", "dW1", "dP", "dF", "g0", "g1",
              "u0", "u1", "wr0", "wr1", "l10", "l11", "xq0", "xq1",
              "P", "A", "V", "O", "oc",
              "d0", "d1", "d2", "d3", "d4", "d5", "d6", "d7"):
        S.sem(octx, s)

    def SB(ctx, n, shp, d=F32):
        return ctx.enter_context(nc.sbuf_tensor(n, shp, d))

    ident_b = SB(octx, "ident_b", [128, 128], BF16)
    ishift_b = SB(octx, "ishift_b", [64, 256], BF16)
    ones_f = SB(octx, "ones_f", [128, 128], F32)
    ones_b = SB(octx, "ones_b", [128, 128], BF16)
    maskp_s = SB(octx, "maskp_s", [128, 256], F32)
    bias_s = SB(octx, "bias_s", [128, 512], F32)
    rb_s = SB(octx, "rb_s", [1, 32], F32)
    scal_s = SB(octx, "scal_s", [128, 4], F32)
    at_b = SB(octx, "at_b", [128, 1024], BF16)
    bqkv_b = SB(octx, "bqkv_b", [128, 8 * 128], BF16)
    bq_b = SB(octx, "bq_b", [128, 8 * 8], BF16)
    xt = [SB(octx, "xt0", [128, 1024]), SB(octx, "xt1", [128, 1024])]
    lnb = [SB(octx, "lnb0", [128, 1024], BF16), SB(octx, "lnb1", [128, 1024], BF16)]
    tmp_f = SB(octx, "tmpf", [128, 1024])
    lnscr = [dict(bn6=SB(octx, f"bn6_{i}", [128, 12]), mv=SB(octx, f"mv_{i}", [128, 2]),
                  sd=SB(octx, f"sd_{i}", [128, 1]), rs=SB(octx, f"rs_{i}", [128, 1]),
                  busy_v=None, busy_a=None) for i in range(2)]
    pA = octx.enter_context(nc.psum_tensor("pA", [128, 1024], F32))
    pB = octx.enter_context(nc.psum_tensor("pB", [128, 1024], F32))
    pC = octx.enter_context(nc.psum_tensor("pC", [128, 1024], F32))
    pT = octx.enter_context(nc.psum_tensor("pT", [128, 2048], BF16))

    ev_c = {}
    for key, dst, src in [("ident", ident_b, ident_d), ("ishift", ishift_b, ishift_d),
                          ("onesb", ones_b, ones_d), ("at", at_b, at_h),
                          ("bqkv", bqkv_b, bqkv_h), ("bq", bq_b, bq_h)]:
        ev_c[key] = S.dma("g", dst[:, :], src, sem="dC")
    for key, dst, src in [("ones", ones_f, ones_d), ("maskp", maskp_s, maskp_d),
                          ("bias", bias_s, bias_d), ("rb", rb_s, rb_h),
                          ("scal", scal_s, scal_d)]:
        ev_c[key] = S.dma("s", dst[:, :], src, sem="dC2")

    xt_free = [None, None]
    lnb_free = [None, None]
    pT_free = [None, None]
    ln_i = [0]

    def emit_ln(x_tensor, x_off, out_ap, x_evt, out_wait=()):
        x_ap = x_tensor[:, x_off:x_off + 1024]
        sc = lnscr[ln_i[0] % 2]
        ln_i[0] += 1
        S.emit("v", lambda h, a=sc["bn6"], b=x_tensor, o=x_off: h.bn_stats(
            a[:, 0:6], b[:, o:o + 512]),
            waits=[x_evt, sc["busy_v"], sc["busy_a"]])
        S.emit("v", lambda h, a=sc["bn6"], b=x_tensor, o=x_off: h.bn_stats(
            a[:, 6:12], b[:, o + 512:o + 1024]))
        S.drain("v")
        ev_mv = S.emit("v", lambda h, a=sc["mv"], b=sc["bn6"]: h.bn_aggr(a[:, :], b[:, :]), inc="V")
        ev_sd = S.emit("a", lambda h, a=sc["sd"], b=sc["mv"]: h.activation(
            out=a[:, :], in_=b[:, 1:2], func=AF.Sqrt, bias=scal_s[:, 0:1]),
            waits=[ev_mv, ev_c["scal"]], inc="A")
        S.emit("v", lambda h, a=sc["rs"], b=sc["sd"]: h.reciprocal(a[:, :], b[:, :]),
               waits=[ev_sd])
        S.drain("v")
        ev = S.emit("v", lambda h, o=out_ap, x=x_ap, m=sc["mv"], r=sc["rs"]: h.tensor_scalar(
            out=o, in0=x, scalar1=m[:, 0:1], scalar2=r[:, 0:1],
            op0=ALU.subtract, op1=ALU.mult), waits=list(out_wait), inc="V")
        sc["busy_v"] = ev
        sc["busy_a"] = ev_sd
        return ev

    def emit_ln_transpose(src_dram_row, dst_tensor, dst_off):
        i = ln_i[0] % 2
        ev_x = S.dma("a", xt[i][:, :], src_dram_row, waits=[xt_free[i]],
                     sem="xq%d" % i)
        ev_ln = emit_ln(xt[i], 0, lnb[i][:, :], ev_x, out_wait=[lnb_free[i]])
        xt_free[i] = ev_ln
        th = ln_i[0] % 2
        ev_tr = None
        for c in range(8):
            w = [ev_ln, ev_c["ident"]] + ([pT_free[th]] if c == 0 else [])
            ev_tr = S.emit("t", lambda h, c=c, i=i, th=th: h.transpose(
                pT[:, th * 1024 + c * 128: th * 1024 + (c + 1) * 128],
                in_=lnb[i][:, c * 128:(c + 1) * 128], identity=ident_b[:, :]),
                waits=w, inc="P")
        lnb_free[i] = ev_tr
        ev_cp = S.emit("a", lambda h, th=th, dst=mkap(dst_tensor, dst_off,
                       [[dst_tensor.shape[1], 128],
                        [dst_tensor.shape[1] // 8, 8], [1, 128]]): h.activation(
            out=dst, in_=pT[:, th * 1024:(th + 1) * 1024].rearrange("p (c n) -> p c n", n=128),
            func=AF.Copy), waits=[ev_tr], inc="A")
        pT_free[th] = ev_cp
        return ev_cp

    ab_ctx = ExitStack()
    O_sb = SB(ab_ctx, "O_sb", [128, 8 * 2048], BF16)
    a_ctx = ExitStack()
    t1kv = SB(a_ctx, "t1kv", [128, SGRP * 512], BF16)
    t1q = SB(a_ctx, "t1q", [8, NG * 512], BF16)
    l1t = [SB(a_ctx, "l1t0", [128, 4096], BF16), SB(a_ctx, "l1t1", [128, 4096], BF16)]
    wq_s = SB(a_ctx, "wq_s", [128, 8 * 512], BF16)
    wk_s = SB(a_ctx, "wk_s", [128, 8 * 512], BF16)
    wv_s = SB(a_ctx, "wv_s", [128, 8 * 512], BF16)
    K_sb = SB(a_ctx, "K_sb", [128, 4 * 4096], BF16)
    V_sb = SB(a_ctx, "V_sb", [128, 8 * 32 * 66], BF16)
    Q_sb = SB(a_ctx, "Q_sb", [128, 4 * 2048], BF16)
    Pt = [SB(a_ctx, "Pt0", [128, 512], BF16), SB(a_ctx, "Pt1", [128, 512], BF16),
          SB(a_ctx, "Pt2", [128, 512], BF16)]
    rec_s = SB(a_ctx, "rec_s", [128, 512])
    bc_s = SB(a_ctx, "bc_s", [64, 512])
    On_s = SB(a_ctx, "On_s", [64, 512], BF16)

    l1t_free = [None, None]
    ev_t1_seq = [None] * SGRP
    ev_t1_own = [None] * NG
    pfree = {k: None for k in ("A0", "A1", "B0", "B1", "C0", "C1")}
    PREG = {"A0": (pA, 0), "A1": (pA, 512), "B0": (pB, 0), "B1": (pB, 512),
            "C0": (pC, 0), "C1": (pC, 512)}

    ev_vones = S.emit("v", lambda h: h.memset(
        V_sb[:, :].rearrange("p (s n) -> p s n", n=66)[:, :, 64:66], 1.0), inc="V")

    def emit_weight_half(hh, extra_wait):
        ev = None
        for dst, src in ((wq_s, wq_h), (wk_s, wk_h), (wv_s, wv_h)):
            for c in range(8):
                ev = S.dma("g", dst[:, c * 512:(c + 1) * 512],
                           src[:, c * 1024 + hh * 512: c * 1024 + hh * 512 + 512],
                           waits=extra_wait, sem="dW%d" % hh)
        return ev

    def emit_qkv_half(hh, ev_w, first):
        kv_last, q_last = None, None
        acc_regs = ["A0", "A1", "B0", "B1"]
        for g in range(SGRP):
            gb = g % 2
            if first:
                subev = []
                for s in range(4):
                    subev.append(emit_ln_transpose(
                        xseq[(g * 4 + s) * 128:(g * 4 + s + 1) * 128, :],
                        l1t[gb], s * 128))
                ev_l1 = subev[-1]
                S.dma("s", l1s_d[g], l1t[gb][:, :], waits=subev)
                for ic in range(8):
                    w = [ev_l1, ev_c["bqkv"]] + ([pfree["C0"]] if ic == 0 else [])
                    S.emit("t", lambda h, ic=ic, gb=gb: h.matmul(
                        pC[:, 0:512], lhsT=bqkv_b[:, ic * 128:(ic + 1) * 128],
                        rhs=l1t[gb][:, ic * 512:(ic + 1) * 512],
                        start=(ic == 0), stop=(ic == 7)), waits=w,
                        inc="P" if ic == 7 else None)
                ev_t1p = ("P", S.cnt["P"])
                S.emit("a", lambda h, g=g: h.activation(
                    out=t1kv[32:40, g * 512:(g + 1) * 512], in_=pC[32:40, 0:512],
                    func=AF.Copy, scale=scal_s[32:40, 1:2]),
                    waits=[ev_t1p, ev_c["scal"]], inc="A")
                ev_t1_seq[g] = S.emit("a", lambda h, g=g: h.activation(
                    out=t1kv[64:72, g * 512:(g + 1) * 512], in_=pC[64:72, 0:512],
                    func=AF.Copy, scale=scal_s[64:72, 1:2]), inc="A")
                pfree["C0"] = ev_t1_seq[g]
            else:
                ev_l1 = S.dma("s", l1t[gb][:, :], l1s_d[g], waits=[l1t_free[gb]],
                              sem="l1%d" % gb)
            for c in range(4):
                pp, off = PREG[acc_regs[c]]
                for ic in range(8):
                    w = [ev_l1, ev_w] + ([pfree[acc_regs[c]]] if ic == 0 else [])
                    S.emit("t", lambda h, c=c, ic=ic, gb=gb, pp=pp, off=off: h.matmul(
                        pp[:, off:off + 512],
                        lhsT=wk_s[:, ic * 512 + c * 128: ic * 512 + (c + 1) * 128],
                        rhs=l1t[gb][:, ic * 512:(ic + 1) * 512],
                        start=(ic == 0), stop=False), waits=w)
                S.emit("t", lambda h, c=c, g=g, hh=hh, pp=pp, off=off: h.matmul(
                    pp[:, off:off + 512],
                    lhsT=at_b[32:40, hh * 512 + c * 128: hh * 512 + (c + 1) * 128],
                    rhs=t1kv[32:40, g * 512:(g + 1) * 512],
                    start=False, stop=True),
                    waits=[ev_t1_seq[g], ev_c["at"]], inc="P")
                ev_kp = ("P", S.cnt["P"])
                kv_last = S.emit("a", lambda h, c=c, g=g, pp=pp, off=off: h.activation(
                    out=K_sb[:, c * 4096 + g * 512: c * 4096 + (g + 1) * 512],
                    in_=pp[:, off:off + 512], func=AF.Copy), waits=[ev_kp], inc="A")
                pfree[acc_regs[c]] = kv_last
            for s in range(4):
                kt = g * 4 + s
                vr = "C0" if s % 2 == 0 else "C1"
                pp, off = PREG[vr]
                for ic in range(8):
                    w = [ev_l1, ev_w] + ([pfree[vr]] if ic == 0 else [])
                    S.emit("t", lambda h, s=s, ic=ic, gb=gb, pp=pp, off=off: h.matmul(
                        pp[:, off:off + 512],
                        lhsT=l1t[gb][:, ic * 512 + s * 128: ic * 512 + (s + 1) * 128],
                        rhs=wv_s[:, ic * 512:(ic + 1) * 512],
                        start=(ic == 0), stop=False), waits=w)
                S.emit("t", lambda h, g=g, s=s, hh=hh, pp=pp, off=off: h.matmul(
                    pp[:, off:off + 512],
                    lhsT=t1kv[64:72, (g * 4 + s) * 128:(g * 4 + s + 1) * 128],
                    rhs=at_b[64:72, hh * 512:(hh + 1) * 512],
                    start=False, stop=True),
                    waits=[ev_t1_seq[g], ev_c["at"]], inc="P")
                ev_vp = ("P", S.cnt["P"])
                dst = mkap(V_sb, kt * 66, [[8 * 32 * 66, 128], [32 * 66, 8], [1, 64]])
                kv_last = S.emit("a", lambda h, pp=pp, off=off, dst=dst: h.activation(
                    out=dst, in_=pp[:, off:off + 512].rearrange("p (c n) -> p c n", n=64),
                    func=AF.Copy), waits=[ev_vp, ev_vones], inc="A")
                pfree[vr] = kv_last
            l1t_free[gb] = ("P", S.cnt["P"])
        for og in range(NG):
            gb = og % 2
            if first:
                subev = []
                for s in range(4):
                    subev.append(emit_ln_transpose(
                        xown[(og * 4 + s) * 128:(og * 4 + s + 1) * 128, :],
                        l1t[gb], s * 128))
                ev_l1 = subev[-1]
                S.dma("s", l1o_d[og], l1t[gb][:, :], waits=subev)
                for ic in range(8):
                    w = [ev_l1, ev_c["bq"]] + ([pfree["C0"]] if ic == 0 else [])
                    S.emit("t", lambda h, ic=ic, gb=gb: h.matmul(
                        pC[0:8, 0:512], lhsT=bq_b[:, ic * 8:(ic + 1) * 8],
                        rhs=l1t[gb][:, ic * 512:(ic + 1) * 512],
                        start=(ic == 0), stop=(ic == 7)), waits=w,
                        inc="P" if ic == 7 else None)
                ev_qt1p = ("P", S.cnt["P"])
                ev_t1_own[og] = S.emit("a", lambda h, og=og: h.activation(
                    out=t1q[0:8, og * 512:(og + 1) * 512], in_=pC[0:8, 0:512],
                    func=AF.Copy, scale=scal_s[0:8, 1:2]),
                    waits=[ev_qt1p, ev_c["scal"]], inc="A")
                pfree["C0"] = ev_t1_own[og]
            else:
                ev_l1 = S.dma("s", l1t[gb][:, :], l1o_d[og], waits=[l1t_free[gb]],
                              sem="l1%d" % gb)
            for c in range(4):
                pp, off = PREG[acc_regs[c]]
                for ic in range(8):
                    w = [ev_l1, ev_w] + ([pfree[acc_regs[c]]] if ic == 0 else [])
                    S.emit("t", lambda h, c=c, ic=ic, gb=gb, pp=pp, off=off: h.matmul(
                        pp[:, off:off + 512],
                        lhsT=wq_s[:, ic * 512 + c * 128: ic * 512 + (c + 1) * 128],
                        rhs=l1t[gb][:, ic * 512:(ic + 1) * 512],
                        start=(ic == 0), stop=False), waits=w)
                S.emit("t", lambda h, c=c, og=og, hh=hh, pp=pp, off=off: h.matmul(
                    pp[:, off:off + 512],
                    lhsT=at_b[0:8, hh * 512 + c * 128: hh * 512 + (c + 1) * 128],
                    rhs=t1q[0:8, og * 512:(og + 1) * 512],
                    start=False, stop=True),
                    waits=[ev_t1_own[og], ev_c["at"]], inc="P")
                ev_qp = ("P", S.cnt["P"])
                q_last = S.emit("a", lambda h, c=c, og=og, pp=pp, off=off: h.activation(
                    out=Q_sb[:, c * 2048 + og * 512: c * 2048 + (og + 1) * 512],
                    in_=pp[:, off:off + 512], func=AF.Copy), waits=[ev_qp], inc="A")
                pfree[acc_regs[c]] = q_last
            l1t_free[gb] = ("P", S.cnt["P"])
        return kv_last, q_last

    def emit_attn_half(hh, ev_ready):
        Pt_free = [None, None, None]
        SREG = [("C0", pC, 0), ("C1", pC, 512), ("B0", pB, 0)]
        state = dict(rec_free=None, bc_free=None, On_free=None, chunk_free=None)

        def emit_norm(gq, hl, ev_pv):
            cl, hp = hl // 2, hl % 2
            o_reg = "A0" if hp == 0 else "A1"
            ocol = (hh * 4 + cl) * 2048 + gq * 512
            ev_rec = S.emit("v", lambda h, hp=hp: h.reciprocal(
                rec_s[64:65, :], pA[64:65, hp * 512:(hp + 1) * 512]),
                waits=[ev_pv, state["rec_free"]], inc="V")
            ev_bc = S.emit("t", lambda h: h.matmul(
                pB[:, 512:1024], lhsT=ones_f[64:65, 0:128], rhs=rec_s[64:65, 0:512],
                start=True, stop=True),
                waits=[ev_rec, ev_c["ones"], pfree["B1"]], inc="P")
            state["rec_free"] = ev_bc
            ev_bce = S.emit("a", lambda h: h.activation(
                out=bc_s[:, :], in_=pB[0:64, 512:1024], func=AF.Copy),
                waits=[ev_bc, state["bc_free"]], inc="A")
            pfree["B1"] = ev_bce
            if hp == 0:
                ev_on = S.emit("v", lambda h, ocol=ocol: h.tensor_tensor(
                    out=O_sb[0:64, ocol:ocol + 512], in0=pA[0:64, 0:512],
                    in1=bc_s[:, :], op=ALU.mult), waits=[ev_bce], inc="V")
                state["bc_free"] = ev_on
                pfree["A0"] = ev_on
            else:
                ev_on = S.emit("v", lambda h: h.tensor_tensor(
                    out=On_s[:, :], in0=pA[0:64, 512:1024],
                    in1=bc_s[:, :], op=ALU.mult),
                    waits=[ev_bce, state["On_free"]], inc="V")
                state["bc_free"] = ev_on
                pfree["A1"] = ev_on
                state["On_free"] = S.dma(
                    "s", O_sb[64:128, ocol:ocol + 512], On_s[0:64, :],
                    waits=[ev_on], sem="oc")
            return state["On_free"]

        pending = None
        ev_oc = None
        for gq in range(NG):
            for hl in range(8):
                h_g = hh * 8 + hl
                cl, hp = hl // 2, hl % 2
                o_reg = "A0" if hp == 0 else "A1"
                nkt = 8 * gq + 8
                ev_pv = None
                for kt in range(nkt):
                    cs = 128 * max(0, kt // 2 - 4 * gq)
                    sr, sp, soff = SREG[kt % 3]
                    sbk = kt % 3
                    w = [ev_ready, pfree[sr]]
                    ev_s = S.emit("t", lambda h, hp=hp, cl=cl, kt=kt, gq=gq, cs=cs, sp=sp, soff=soff: h.matmul(
                        sp[:, soff + cs: soff + 512],
                        lhsT=K_sb[hp * 64:(hp + 1) * 64,
                                  cl * 4096 + kt * 128: cl * 4096 + (kt + 1) * 128],
                        rhs=Q_sb[hp * 64:(hp + 1) * 64,
                                 cl * 2048 + gq * 512 + cs: cl * 2048 + (gq + 1) * 512],
                        start=True, stop=True), waits=w, inc="P")
                    ev_m = None
                    if kt >= 8 * gq:
                        jm = kt // 2 - 4 * gq
                        ev_m = S.emit("v", lambda h, sp=sp, soff=soff, jm=jm, kt=kt: h.tensor_tensor(
                            out=sp[:, soff + jm * 128: soff + (jm + 1) * 128],
                            in0=sp[:, soff + jm * 128: soff + (jm + 1) * 128],
                            in1=maskp_s[:, (kt % 2) * 128:(kt % 2 + 1) * 128],
                            op=ALU.add), waits=[ev_s, ev_c["maskp"]], inc="V")
                    ev_e = S.emit("a", lambda h, sbk=sbk, cs=cs, h_g=h_g, kt=kt, sp=sp, soff=soff: h.activation(
                        out=Pt[sbk][:, cs:512], in_=sp[:, soff + cs: soff + 512],
                        func=AF.Exp, bias=bias_s[:, h_g * 32 + kt: h_g * 32 + kt + 1],
                        scale=scal_s[:, 2:3]),
                        waits=[ev_s, ev_m, ev_c["bias"], ev_c["scal"], Pt_free[sbk]], inc="A")
                    pfree[sr] = ev_e
                    w2 = [ev_e] + ([pfree[o_reg]] if kt == 0 else [])
                    ev_pv = S.emit("t", lambda h, hl=hl, kt=kt, cs=cs, sbk=sbk, hp=hp, nkt=nkt: h.matmul(
                        pA[0:65, hp * 512 + cs:(hp + 1) * 512],
                        lhsT=V_sb[:, (hl * 32 + kt) * 66: (hl * 32 + kt) * 66 + 65],
                        rhs=Pt[sbk][:, cs:512],
                        start=(kt == 0), stop=(kt == nkt - 1)), waits=w2, inc="P")
                    Pt_free[sbk] = ev_pv
                if pending is not None:
                    ev_oc = emit_norm(*pending) or ev_oc
                pending = (gq, hl, ev_pv)
        ev_oc = emit_norm(*pending) or ev_oc
        return [ev_oc, ("V", S.cnt["V"])]

    ev_w0 = emit_weight_half(0, [])
    kv0, q0 = emit_qkv_half(0, ev_w0, first=True)
    ev_oc0 = emit_attn_half(0, q0)
    ev_w1 = emit_weight_half(1, [("P", S.cnt["P"])])
    kv1, q1 = emit_qkv_half(1, ev_w1, first=False)
    ev_oc1 = emit_attn_half(1, q1)

    a_ctx.close()
    S.set_floor()
    b_ctx = ExitStack()
    wp_s = SB(b_ctx, "wp_s", [128, 8 * 1024], BF16)
    at_f = [SB(b_ctx, "atf0", [128, 1024]), SB(b_ctx, "atf1", [128, 1024])]
    xo_f = [SB(b_ctx, "xof0", [128, 1024]), SB(b_ctx, "xof1", [128, 1024])]
    x1t = [SB(b_ctx, "x1t0", [128, 1024]), SB(b_ctx, "x1t1", [128, 1024])]
    ev_wp = None
    for ic in range(8):
        ev_wp = S.dma("g", wp_s[:, ic * 1024:(ic + 1) * 1024],
                      wp_h[:, ic * 1024:(ic + 1) * 1024], sem="dP")
    pacc_free = [None, None]
    at_free = [None, None]
    xo_free = [None, None]
    x1t_free = [None, None]
    ev_x1 = [None] * NT
    for j in range(NT):
        pb = j % 2
        pp = pA if pb == 0 else pB
        ev_xo = S.dma("a", xo_f[pb][:, :], xown[j * 128:(j + 1) * 128, :],
                      waits=[xo_free[pb]], sem="xq%d" % pb)
        for ic in range(8):
            for hf in range(2):
                w = list(ev_oc1) + [ev_wp] + ([pacc_free[pb]] if (ic == 0 and hf == 0) else [])
                S.emit("t", lambda h, ic=ic, hf=hf, j=j, pp=pp: h.matmul(
                    pp[:, hf * 512:(hf + 1) * 512],
                    lhsT=O_sb[:, ic * 2048 + j * 128:ic * 2048 + (j + 1) * 128],
                    rhs=wp_s[:, ic * 1024 + hf * 512: ic * 1024 + (hf + 1) * 512],
                    start=(ic == 0), stop=(ic == 7)),
                    waits=w, inc="P" if (ic == 7 and hf == 1) else None)
        ev_wpp = ("P", S.cnt["P"])
        ev_ae = S.emit("a", lambda h, pb=pb, pp=pp: h.activation(
            out=at_f[pb][:, :], in_=pp[:, :], func=AF.Copy),
            waits=[ev_wpp, at_free[pb]], inc="A")
        pacc_free[pb] = ev_ae
        ev_ln = emit_ln(at_f[pb], 0, tmp_f[:, :], ev_ae)
        at_free[pb] = ev_ln
        S.drain("v")
        ev_x1t = S.emit("v", lambda h, pb=pb: h.tensor_tensor(
            out=x1t[pb][:, :], in0=tmp_f[:, :], in1=xo_f[pb][:, :], op=ALU.add),
            waits=[ev_xo, x1t_free[pb]], inc="V")
        xo_free[pb] = ev_x1t
        ev_x1[j] = S.dma("s", x1_d[j], x1t[pb][:, :], waits=[ev_x1t])
        x1t_free[pb] = ev_x1[j]

    if DEBUG_STOP_AT_X1:
        for j in range(NT):
            S.dma("s", out_d[j * 128:(j + 1) * 128, :], x1_d[j],
                  waits=[ev_x1[j]], sem="O")
        S.emit("s", lambda h: None, waits=[("O", S.cnt["O"])])
        b_ctx.close(); ab_ctx.close()
        with octx, nc.Block() as block:
            S.replay(block)
        return nc

    b_ctx.close()
    ab_ctx.close()
    S.set_floor()
    c_ctx = ExitStack()
    ln2T = SB(c_ctx, "ln2T", [128, 8 * 1024], BF16)
    rw_s = SB(c_ctx, "rw_s", [128, 8 * 32], BF16)
    sgw_s = SB(c_ctx, "sgw_s", [128, 8 * 512], BF16)
    suw_s = SB(c_ctx, "suw_s", [128, 8 * 512], BF16)
    sdw_s = SB(c_ctx, "sdw_s", [128, 4 * 1024], BF16)
    gw_s = [SB(c_ctx, "gw_s0", [128, 1024], BF16), SB(c_ctx, "gw_s1", [128, 1024], BF16)]
    uw_s = [SB(c_ctx, "uw_s0", [128, 1024], BF16), SB(c_ctx, "uw_s1", [128, 1024], BF16)]
    dw_s = [SB(c_ctx, f"dw_s{i}", [128, 1024], BF16) for i in range(8)]
    sc_f = SB(c_ctx, "sc_f", [128, 32])
    cur_f = SB(c_ctx, "cur_f", [128, 32])
    msk_f = SB(c_ctx, "msk_f", [128, 32])
    m_f = SB(c_ctx, "m_f", [128, 1])
    wu_f = SB(c_ctx, "wu_f", [128, 32])
    w_b = SB(c_ctx, "w_b", [128, 32], BF16)
    wfm = SB(c_ctx, "wfm", [32, 1024], BF16)
    wrow = [SB(c_ctx, "wrow0", [1, 1024], BF16), SB(c_ctx, "wrow1", [1, 1024], BF16)]
    sg_b = SB(c_ctx, "sg_b", [128, 1024], BF16)
    wb_b = SB(c_ctx, "wb_b", [128, 1024], BF16)
    h1_b = SB(c_ctx, "h1_b", [128, 1024], BF16)
    h_blk = SB(c_ctx, "h_blk", [128, 8 * 1024], BF16)
    hsh = SB(c_ctx, "hsh", [128, 4 * 1024], BF16)
    moe = SB(c_ctx, "moe", [128, 8 * 1024])
    out_f = [SB(c_ctx, "outf0", [128, 1024]), SB(c_ctx, "outf1", [128, 1024])]
    wdbg = SB(c_ctx, "wdbg", [128, 32])
    x1e = [SB(c_ctx, "x1e0", [128, 1024]), SB(c_ctx, "x1e1", [128, 1024])]

    ev_rw = S.dma("g", rw_s[:, :], rw_h, sem="dF")
    ev_sg = S.dma("g", sgw_s[:, :], sg_h, sem="dF")
    ev_su = S.dma("g", suw_s[:, :], su_h, sem="dF")
    ev_sd = S.dma("g", sdw_s[:, :], sd_h, sem="dF")

    def l2rhs(ic, hf):
        return mkap(ln2T, ic * 1024 + hf * 512, [[8192, 128], [1, 512]])

    ev_moe = [None] * 8
    w_b_free = [None]
    gw_free = [None, None]
    uw_free = [None, None]
    dw_free = [None] * 8
    wrow_free = [None, None]
    h_blk_free = [None] * 8
    sg_free, wb_free, h1_free = None, None, None
    x1e_free = [None, None]
    out_free = [None, None]
    topk_free = None
    for G in range(2):
        ev_l2_last = None
        ev_wfm = []
        pfree["C0"] = pfree.get("C0")
        for st in range(8):
            j = G * 8 + st
            pb = st % 2
            ev_x1l = S.dma("a", x1e[pb][:, :], x1_d[j], waits=[ev_x1[j], x1e_free[pb]],
                           sem="xq%d" % pb)
            ev_ln = emit_ln(x1e[pb], 0, lnb[pb][:, :], ev_x1l,
                            out_wait=[lnb_free[pb]])
            x1e_free[pb] = ev_ln
            th = st % 2
            ev_tr = None
            for c in range(8):
                w = [ev_ln] + ([pT_free[th]] if c == 0 else [])
                ev_tr = S.emit("t", lambda h, c=c, pb=pb, th=th: h.transpose(
                    pT[:, th * 1024 + c * 128: th * 1024 + (c + 1) * 128],
                    in_=lnb[pb][:, c * 128:(c + 1) * 128], identity=ident_b[:, :]),
                    waits=w, inc="P")
            lnb_free[pb] = ev_tr
            ev_l2_last = S.emit("a", lambda h, th=th, dst=mkap(
                ln2T, st * 128, [[8192, 128], [1024, 8], [1, 128]]): h.activation(
                out=dst, in_=pT[:, th * 1024:(th + 1) * 1024].rearrange("p (c n) -> p c n", n=128),
                func=AF.Copy), waits=[ev_tr], inc="A")
            pT_free[th] = ev_l2_last
            for ic in range(8):
                w = [ev_l2_last, ev_rw] + ([pfree["C0"]] if ic == 0 else [])
                S.emit("t", lambda h, ic=ic, st=st: h.matmul(
                    pC[:, 0:32],
                    lhsT=mkap(ln2T, ic * 1024 + st * 128, [[8192, 128], [1, 128]]),
                    rhs=rw_s[:, ic * 32:(ic + 1) * 32],
                    start=(ic == 0), stop=False), waits=w)
            S.emit("t", lambda h: h.matmul(
                pC[:, 0:32], lhsT=ones_f[0:1, 0:128], rhs=rb_s[0:1, :],
                start=False, stop=True), waits=[ev_c["rb"], ev_c["ones"]], inc="P")
            ev_rt = ("P", S.cnt["P"])
            ev_nm = S.emit("v", lambda h: h.reduce_max(
                m_f[:, :], pC[:, 0:32], axis=AX.X, negate=True),
                waits=[ev_rt, topk_free], inc="V")
            ev_sc = S.emit("a", lambda h: h.activation(
                out=sc_f[:, :], in_=pC[:, 0:32], func=AF.Exp, bias=m_f[:, 0:1]),
                waits=[ev_nm], inc="A")
            pfree["C0"] = ev_sc
            S.emit("v", lambda h: h.tensor_copy(cur_f[:, :], sc_f[:, :]), waits=[ev_sc])
            for it in range(3):
                S.drain("v")
                S.emit("v", lambda h: h.reduce_max(m_f[:, :], cur_f[:, :], axis=AX.X),
                       waits=[ev_sc])
                S.drain("v")
                S.emit("v", lambda h: h.tensor_scalar(
                    out=msk_f[:, :], in0=cur_f[:, :], scalar1=m_f[:, 0:1],
                    scalar2=1e30, op0=ALU.is_ge, op1=ALU.mult))
                S.drain("v")
                S.emit("v", lambda h: h.tensor_sub(cur_f[:, :], cur_f[:, :], msk_f[:, :]))
            S.drain("v")
            S.emit("v", lambda h: h.reduce_max(m_f[:, :], cur_f[:, :], axis=AX.X))
            S.drain("v")
            S.emit("v", lambda h: h.tensor_scalar(
                out=msk_f[:, :], in0=sc_f[:, :], scalar1=m_f[:, 0:1],
                scalar2=None, op0=ALU.is_ge))
            S.drain("v")
            S.emit("v", lambda h: h.tensor_mul(wu_f[:, :], sc_f[:, :], msk_f[:, :]))
            S.drain("v")
            S.emit("v", lambda h: h.reduce_sum(m_f[:, :], wu_f[:, :], axis=AX.X))
            S.drain("v")
            S.emit("v", lambda h: h.reciprocal(m_f[:, :], m_f[:, :]))
            S.drain("v")
            ev_wb = S.emit("v", lambda h: h.tensor_scalar(
                out=w_b[:, :], in0=wu_f[:, :], scalar1=m_f[:, 0:1], scalar2=None,
                op0=ALU.mult), waits=[w_b_free[0]], inc="V")
            topk_free = ev_wb
            ev_wt = S.emit("t", lambda h, th=th: h.transpose(
                pT[0:32, th * 1024: th * 1024 + 128],
                in_=w_b[:, :], identity=ident_b[:, :]),
                waits=[ev_wb, pT_free[th]], inc="P")
            ev_wf = S.emit("a", lambda h, st=st, th=th: h.activation(
                out=wfm[0:32, st * 128:(st + 1) * 128],
                in_=pT[0:32, th * 1024: th * 1024 + 128],
                func=AF.Copy), waits=[ev_wt], inc="A")
            ev_wfm.append(ev_wf)
            pT_free[th] = ev_wf
            w_b_free[0] = ev_wt
            if DEBUG_FFN_W:
                ev_wd = S.emit("v", lambda h: h.tensor_copy(wdbg[:, :], w_b[:, :]),
                               waits=[ev_wb], inc="V")
                S.dma("s", out_d[j * 128:(j + 1) * 128, 0:32], wdbg[:, :],
                      waits=[ev_wd], sem="O")
        ev_wsc = S.dma("s", wg_d[G], wfm[:, :], waits=ev_wfm)
        if DEBUG_FFN_W:
            continue
        for blk in range(DEBUG_NBLK):
            ev_h_all = None
            for e8 in range(8):
                e = blk * 8 + e8
                eb = e % 2
                ev_gw = S.dma("g", gw_s[eb][:, :], gw_h[e],
                              waits=[gw_free[eb]], sem="g%d" % eb)
                ev_uw = S.dma("g", uw_s[eb][:, :], uw_h[e],
                              waits=[uw_free[eb]], sem="u%d" % eb)
                ev_dwl = S.dma("g", dw_s[e8][:, :], dw_h[e],
                               waits=[dw_free[e8]], sem="d%d" % e8)
                dw_free[e8] = ev_dwl
                ev_wr = S.dma("s", wrow[eb][0:1, :], wg_d[G][e:e + 1, :],
                              waits=[ev_wsc, wrow_free[eb]], sem="wr%d" % eb)
                for ic in range(8):
                    for hf in range(2):
                        w = [ev_gw, ev_l2_last] + (
                            [pfree["A0"], pfree["A1"]] if (ic == 0 and hf == 0) else [])
                        S.emit("t", lambda h, ic=ic, hf=hf, eb=eb: h.matmul(
                            pA[:, hf * 512:(hf + 1) * 512],
                            lhsT=gw_s[eb][:, ic * 128:(ic + 1) * 128],
                            rhs=l2rhs(ic, hf), start=(ic == 0), stop=(ic == 7)),
                            waits=w, inc="P" if (ic == 7 and hf == 1) else None)
                ev_g = ("P", S.cnt["P"])
                gw_free[eb] = ev_g
                for ic in range(8):
                    for hf in range(2):
                        w = [ev_uw, ev_l2_last] + (
                            [pfree["B0"], pfree["B1"]] if (ic == 0 and hf == 0) else [])
                        S.emit("t", lambda h, ic=ic, hf=hf, eb=eb: h.matmul(
                            pB[:, hf * 512:(hf + 1) * 512],
                            lhsT=uw_s[eb][:, ic * 128:(ic + 1) * 128],
                            rhs=l2rhs(ic, hf), start=(ic == 0), stop=(ic == 7)),
                            waits=w, inc="P" if (ic == 7 and hf == 1) else None)
                ev_u = ("P", S.cnt["P"])
                uw_free[eb] = ev_u
                for hf in range(2):
                    w = [ev_wr, ev_c["onesb"]] + (
                        [pfree["C0"], pfree["C1"]] if hf == 0 else [])
                    S.emit("t", lambda h, hf=hf, eb=eb: h.matmul(
                        pC[:, hf * 512:(hf + 1) * 512], lhsT=ones_b[0:1, 0:128],
                        rhs=wrow[eb][0:1, hf * 512:(hf + 1) * 512],
                        start=True, stop=True), waits=w,
                        inc="P" if hf == 1 else None)
                ev_wbp = ("P", S.cnt["P"])
                wrow_free[eb] = ev_wbp
                ev_sgb = S.emit("a", lambda h: h.activation(
                    out=sg_b[:, :], in_=pA[:, :], func=AF.Silu),
                    waits=[ev_g, sg_free], inc="A")
                pfree["A0"] = pfree["A1"] = ev_sgb
                ev_wbe = S.emit("a", lambda h: h.activation(
                    out=wb_b[:, :], in_=pC[:, :], func=AF.Copy),
                    waits=[ev_wbp, wb_free], inc="A")
                pfree["C0"] = pfree["C1"] = ev_wbe
                ev_h1 = S.emit("v", lambda h: h.tensor_tensor(
                    out=h1_b[:, :], in0=sg_b[:, :], in1=pB[:, :], op=ALU.mult),
                    waits=[ev_sgb, ev_u, h1_free], inc="V")
                sg_free = ev_h1
                pfree["B0"] = pfree["B1"] = ev_h1
                ev_h = S.emit("v", lambda h, e8=e8: h.tensor_tensor(
                    out=h_blk[:, e8 * 1024:(e8 + 1) * 1024], in0=h1_b[:, :],
                    in1=wb_b[:, :], op=ALU.mult),
                    waits=[ev_h1, ev_wbe, h_blk_free[e8]], inc="V")
                h1_free = ev_h
                wb_free = ev_h
                ev_h_all = ev_h
            dn_regs = [("C0", "C1", pC), ("A0", "A1", pA)]
            for st in range(8):
                r0, r1, pp = dn_regs[st % 2]
                for e8 in range(8):
                    w = [ev_h_all, dw_free[e8]] + (
                        [pfree[r0], pfree[r1]] if e8 == 0 else [])
                    S.emit("t", lambda h, e8=e8, st=st, pp=pp: h.matmul(
                        pp[:, 0:512],
                        lhsT=h_blk[:, e8 * 1024 + st * 128: e8 * 1024 + (st + 1) * 128],
                        rhs=dw_s[e8][:, 0:512],
                        start=(e8 == 0), stop=False), waits=w)
                    S.emit("t", lambda h, e8=e8, st=st, pp=pp: h.matmul(
                        pp[:, 512:1024],
                        lhsT=h_blk[:, e8 * 1024 + st * 128: e8 * 1024 + (st + 1) * 128],
                        rhs=dw_s[e8][:, 512:1024],
                        start=(e8 == 0), stop=(e8 == 7)),
                        inc="P" if e8 == 7 else None)
                ev_dn = ("P", S.cnt["P"])
                if blk == 0:
                    ev_mo = S.emit("v", lambda h, st=st, pp=pp: h.tensor_copy(
                        moe[:, st * 1024:(st + 1) * 1024], pp[:, :]),
                        waits=[ev_dn, ev_moe[st]], inc="V")
                else:
                    ev_mo = S.emit("v", lambda h, st=st, pp=pp: h.tensor_tensor(
                        out=moe[:, st * 1024:(st + 1) * 1024],
                        in0=moe[:, st * 1024:(st + 1) * 1024], in1=pp[:, :],
                        op=ALU.add), waits=[ev_dn, ev_moe[st]], inc="V")
                ev_moe[st] = ev_mo
                pfree[r0] = pfree[r1] = ev_mo
            ev_dn_done = ("P", S.cnt["P"])
            for e8 in range(8):
                h_blk_free[e8] = ev_dn_done
                dw_free[e8] = ev_dn_done
        for hc in range(0 if DEBUG_NO_SHARED else 4):
            for ic in range(8):
                for hf in range(2):
                    w = [ev_sg, ev_l2_last] + (
                        [pfree["A0"], pfree["A1"]] if (ic == 0 and hf == 0) else [])
                    S.emit("t", lambda h, ic=ic, hf=hf, hc=hc: h.matmul(
                        pA[:, hf * 512:(hf + 1) * 512],
                        lhsT=sgw_s[:, ic * 512 + hc * 128: ic * 512 + (hc + 1) * 128],
                        rhs=l2rhs(ic, hf), start=(ic == 0), stop=(ic == 7)),
                        waits=w, inc="P" if (ic == 7 and hf == 1) else None)
            ev_g = ("P", S.cnt["P"])
            for ic in range(8):
                for hf in range(2):
                    w = [ev_su, ev_l2_last] + (
                        [pfree["B0"], pfree["B1"]] if (ic == 0 and hf == 0) else [])
                    S.emit("t", lambda h, ic=ic, hf=hf, hc=hc: h.matmul(
                        pB[:, hf * 512:(hf + 1) * 512],
                        lhsT=suw_s[:, ic * 512 + hc * 128: ic * 512 + (hc + 1) * 128],
                        rhs=l2rhs(ic, hf), start=(ic == 0), stop=(ic == 7)),
                        waits=w, inc="P" if (ic == 7 and hf == 1) else None)
            ev_u = ("P", S.cnt["P"])
            ev_sgb = S.emit("a", lambda h: h.activation(
                out=sg_b[:, :], in_=pA[:, :], func=AF.Silu),
                waits=[ev_g, sg_free], inc="A")
            pfree["A0"] = pfree["A1"] = ev_sgb
            ev_h = S.emit("v", lambda h, hc=hc: h.tensor_tensor(
                out=hsh[:, hc * 1024:(hc + 1) * 1024], in0=sg_b[:, :], in1=pB[:, :],
                op=ALU.mult), waits=[ev_sgb, ev_u], inc="V")
            sg_free = ev_h
            pfree["B0"] = pfree["B1"] = ev_h
        if DEBUG_NO_SHARED:
            for hc in []: pass
        ev_hsh = ("V", S.cnt["V"])
        dn_regs = [("C0", "C1", pC), ("A0", "A1", pA)]
        for st in range(0 if DEBUG_NO_SHARED else 8):
            r0, r1, pp = dn_regs[st % 2]
            for hc in range(4):
                w = [ev_hsh, ev_sd] + ([pfree[r0], pfree[r1]] if hc == 0 else [])
                S.emit("t", lambda h, hc=hc, st=st, pp=pp: h.matmul(
                    pp[:, 0:512],
                    lhsT=hsh[:, hc * 1024 + st * 128: hc * 1024 + (st + 1) * 128],
                    rhs=sdw_s[:, hc * 1024: hc * 1024 + 512],
                    start=(hc == 0), stop=False), waits=w)
                S.emit("t", lambda h, hc=hc, st=st, pp=pp: h.matmul(
                    pp[:, 512:1024],
                    lhsT=hsh[:, hc * 1024 + st * 128: hc * 1024 + (st + 1) * 128],
                    rhs=sdw_s[:, hc * 1024 + 512: (hc + 1) * 1024],
                    start=(hc == 0), stop=(hc == 3)),
                    inc="P" if hc == 3 else None)
            ev_dn = ("P", S.cnt["P"])
            ev_mo = S.emit("v", lambda h, st=st, pp=pp: h.tensor_tensor(
                out=moe[:, st * 1024:(st + 1) * 1024],
                in0=moe[:, st * 1024:(st + 1) * 1024], in1=pp[:, :], op=ALU.add),
                waits=[ev_dn, ev_moe[st]], inc="V")
            ev_moe[st] = ev_mo
            pfree[r0] = pfree[r1] = ev_mo
        for st in range(8):
            j = G * 8 + st
            pb = st % 2
            if DEBUG_FFN_RAW:
                out_free[pb] = S.dma("s", out_d[j * 128:(j + 1) * 128, :],
                                     moe[:, st * 1024:(st + 1) * 1024],
                                     waits=[ev_moe[st], out_free[pb]], sem="O")
                continue
            ev_x1l = S.dma("a", x1e[pb][:, :], x1_d[j], waits=[x1e_free[pb]],
                           sem="xq%d" % pb)
            ev_ln = emit_ln(moe, st * 1024, tmp_f[:, :], ev_moe[st])
            ev_moe[st] = ev_ln
            S.drain("v")
            ev_o = S.emit("v", lambda h, pb=pb: h.tensor_tensor(
                out=out_f[pb][:, :], in0=tmp_f[:, :], in1=x1e[pb][:, :], op=ALU.add),
                waits=[ev_x1l, out_free[pb]], inc="V")
            x1e_free[pb] = ev_o
            out_free[pb] = S.dma("s", out_d[j * 128:(j + 1) * 128, :], out_f[pb][:, :],
                                 waits=[ev_o], sem="O")
    S.emit("s", lambda h: None, waits=[("O", S.cnt["O"])])

    c_ctx.close()
    with octx, nc.Block() as block:
        S.replay(block)
    return nc


_NC_CACHE = {}


def _chunked(w, cols):
    return np.ascontiguousarray(
        w.reshape(8, 128, cols).transpose(1, 0, 2).reshape(128, 8 * cols))


def kernel(L0_attn_x, L0_attn_Wq, L0_attn_Aq, L0_attn_Bq, L0_attn_Wk, L0_attn_Ak,
           L0_attn_Bk, L0_attn_Wv, L0_attn_Av, L0_attn_Bv, L0_attn_Wp,
           L0_ffn_router_w, L0_ffn_router_bias, L0_ffn_gate_w, L0_ffn_up_w,
           L0_ffn_down_w, L0_ffn_sgate_w, L0_ffn_sup_w, L0_ffn_sdown_w):
    f = lambda a: np.ascontiguousarray(np.asarray(a, dtype=np.float32))
    X = f(L0_attn_x)
    wq = _chunked(f(L0_attn_Wq).T, 1024)
    wk = _chunked(f(L0_attn_Wk).T, 1024)
    wv = _chunked(f(L0_attn_Wv).T, 1024)
    wp = _chunked(f(L0_attn_Wp).T, 1024)
    Mkv = np.zeros((1024, 128), np.float32)
    Mkv[:, 32:40] = f(L0_attn_Bk).T
    Mkv[:, 64:72] = f(L0_attn_Bv).T
    bqkv = _chunked(Mkv, 128)
    bq = _chunked(f(L0_attn_Bq).T, 8)
    aT = np.zeros((128, 1024), np.float32)
    aT[0:8] = f(L0_attn_Aq).T
    aT[32:40] = f(L0_attn_Ak).T
    aT[64:72] = f(L0_attn_Av).T
    rw = _chunked(f(L0_ffn_router_w).T, 32)
    rb = f(L0_ffn_router_bias).reshape(1, 32)
    gw = np.ascontiguousarray(
        f(L0_ffn_gate_w).reshape(32, 128, 8, 128).transpose(0, 3, 2, 1).reshape(32, 128, 1024))
    uw = np.ascontiguousarray(
        f(L0_ffn_up_w).reshape(32, 128, 8, 128).transpose(0, 3, 2, 1).reshape(32, 128, 1024))
    dw = np.ascontiguousarray(f(L0_ffn_down_w).transpose(0, 2, 1))
    sg = _chunked(f(L0_ffn_sgate_w).T, 512)
    su = _chunked(f(L0_ffn_sup_w).T, 512)
    sd = np.ascontiguousarray(
        f(L0_ffn_sdown_w).T.reshape(4, 128, 1024).transpose(1, 0, 2).reshape(128, 4096))
    ident = np.eye(128, dtype=np.float32)
    ones = np.ones((128, 128), np.float32)
    scal = np.zeros((128, 4), np.float32)
    scal[:, 0] = EPS
    scal[:, 1] = 2.0
    scal[:, 2] = 0.125
    ishift = np.zeros((64, 256), np.float32)
    for d in range(64):
        ishift[d, d] = 1.0
        ishift[d, 128 + 64 + d] = 1.0
    slopes = np.array([2.0 ** (-(i + 1) / 2.0) for i in range(H)], np.float64)
    biastab = np.zeros((128, 512), np.float32)
    p = np.arange(128, dtype=np.float64)
    for hh in range(H):
        for kt in range(32):
            biastab[:, hh * 32 + kt] = (-slopes[hh] * (kt * 128 + p)).astype(np.float32)
    tri = np.where(np.arange(128)[:, None] > np.arange(128)[None, :],
                   NEGBIG, 0.0).astype(np.float32)
    full = np.full((128, 128), NEGBIG, np.float32)
    zero = np.zeros((128, 128), np.float32)
    maskr = [np.concatenate([tri, full], axis=1), np.concatenate([zero, tri], axis=1)]

    if "nc" not in _NC_CACHE:
        _NC_CACHE["nc"] = build_program()
    nc = _NC_CACHE["nc"]

    in_maps = []
    for c in range(8):
        b, r = c // 2, c % 2
        xs = np.ascontiguousarray(X[b])
        own_rows = np.concatenate(
            [np.arange((2 * j + r) * 128, (2 * j + r + 1) * 128) for j in range(NT)])
        xo = np.ascontiguousarray(xs[own_rows])
        in_maps.append(dict(
            xseq=xs, xown=xo, maskp=maskr[r], biastab=biastab, identc=ident,
            ishift=ishift, onesc=ones, scal=scal, wq=wq, wk=wk, wv=wv, wp=wp, bqkv=bqkv,
            bq=bq, at=aT, rw=rw, rb=rb, gw=gw, uw=uw, dw=dw, sg=sg, su=su, sd=sd))
    res = run_bass_kernel_spmd(nc, in_maps, core_ids=list(range(8)))
    out = np.empty((B, T, E), np.float32)
    for c in range(8):
        b, r = c // 2, c % 2
        oc = res.results[c]["out"]
        for j in range(NT):
            out[b, (2 * j + r) * 128:(2 * j + r + 1) * 128, :] = oc[j * 128:(j + 1) * 128, :]
    return out


# version 04
# speedup vs baseline: 1.5579x; 1.4510x over previous
import dataclasses
import math
from contextlib import ExitStack

import numpy as np
import concourse.bass as bass
import concourse.mybir as mybir
from concourse.bass_utils import run_bass_kernel_spmd

F32 = mybir.dt.float32
BF16 = mybir.dt.bfloat16
AF = mybir.ActivationFunctionType
ALU = mybir.AluOpType
AX = mybir.AxisListType

B, T, E, H, DH = 4, 4096, 1024, 16, 64
NEXP, HS = 32, 512
NT, NG, SGRP = 16, 4, 8
EPS = 1e-5
NEGBIG = -1e30
DEBUG_STOP_AT_X1 = False
DEBUG_FFN_RAW = False
DEBUG_FFN_W = False
DEBUG_NBLK = 4
DEBUG_NO_SHARED = False


def mkap(handle, off, pattern):
    a = handle[:, :] if len(handle.shape) == 2 else handle[:]
    return dataclasses.replace(a, offset=a.offset + off, ap=pattern)


class Sched:
    ENGS = ("s", "g", "t", "a", "v")

    def __init__(self, nc):
        self.nc = nc
        self.q = {e: [] for e in self.ENGS}
        self.sems = {}
        self.cnt = {}
        self.floor = {}

    def sem(self, ctx, name):
        self.sems[name] = ctx.enter_context(self.nc.semaphore(name))
        self.cnt[name] = 0

    def set_floor(self):
        self.floor = dict(self.cnt)

    def emit(self, eng, fn, waits=(), inc=None, amt=1):
        ws = dict(self.floor)
        for w in waits:
            if w is None:
                continue
            sname, val = w
            assert val <= self.cnt[sname], (sname, val, self.cnt[sname])
            if ws.get(sname, 0) < val:
                ws[sname] = val
        evt = None
        if inc is not None:
            self.cnt[inc] += amt
            evt = (inc, self.cnt[inc])
        self.q[eng].append((fn, tuple(ws.items()), (inc, amt) if inc else None))
        return evt

    def dma(self, eng, out, in_, waits=(), sem="dS"):
        return self.emit(eng, lambda h, o=out, i=in_: h.dma_start(out=o, in_=i),
                         waits=waits, inc=sem, amt=16)

    def drain(self, eng):
        self.emit(eng, lambda h: h.drain())

    def replay(self, block):
        decos = {"s": block.sync, "g": block.gpsimd, "t": block.tensor,
                 "a": block.scalar, "v": block.vector}
        for ek in self.ENGS:
            ops = self.q[ek]
            if not ops:
                continue
            def body(h, ops=ops):
                hwm = {}
                for fn, waits, inc in ops:
                    for sname, val in waits:
                        if hwm.get(sname, 0) < val:
                            h.wait_ge(self.sems[sname], val)
                            hwm[sname] = val
                    ins = fn(h)
                    if inc is not None and ins is not None:
                        ins.then_inc(self.sems[inc[0]], inc[1])
            decos[ek](body)


def build_program():
    nc = bass.Bass("TRN2", target_bir_lowering=False)
    dt = nc.dram_tensor
    xseq = dt("xseq", [T, E], F32, kind="ExternalInput").ap()
    xown = dt("xown", [NT * 128, E], F32, kind="ExternalInput").ap()
    maskp_d = dt("maskp", [128, 256], F32, kind="ExternalInput").ap()
    bias_d = dt("biastab", [128, 512], F32, kind="ExternalInput").ap()
    ident_d = dt("identc", [128, 128], F32, kind="ExternalInput").ap()
    ishift_d = dt("ishift", [64, 256], F32, kind="ExternalInput").ap()
    ones_d = dt("onesc", [128, 128], F32, kind="ExternalInput").ap()
    scal_d = dt("scal", [128, 4], F32, kind="ExternalInput").ap()
    wq_h = dt("wq", [128, 8 * 1024], F32, kind="ExternalInput").ap()
    wk_h = dt("wk", [128, 8 * 1024], F32, kind="ExternalInput").ap()
    wv_h = dt("wv", [128, 8 * 1024], F32, kind="ExternalInput").ap()
    wp_h = dt("wp", [128, 8 * 1024], F32, kind="ExternalInput").ap()
    bqkv_h = dt("bqkv", [128, 8 * 128], F32, kind="ExternalInput").ap()
    bq_h = dt("bq", [128, 8 * 8], F32, kind="ExternalInput").ap()
    at_h = dt("at", [128, 1024], F32, kind="ExternalInput").ap()
    rw_h = dt("rw", [128, 8 * 32], F32, kind="ExternalInput").ap()
    rb_h = dt("rb", [1, 32], F32, kind="ExternalInput").ap()
    gw_h = dt("gw", [NEXP, 128, 1024], F32, kind="ExternalInput").ap()
    uw_h = dt("uw", [NEXP, 128, 1024], F32, kind="ExternalInput").ap()
    dw_h = dt("dw", [NEXP, 128, 1024], F32, kind="ExternalInput").ap()
    sg_h = dt("sg", [128, 8 * 512], F32, kind="ExternalInput").ap()
    su_h = dt("su", [128, 8 * 512], F32, kind="ExternalInput").ap()
    sd_h = dt("sd", [128, 4 * 1024], F32, kind="ExternalInput").ap()
    out_d = dt("out", [NT * 128, E], F32, kind="ExternalOutput").ap()
    l1s_d = dt("l1scr", [SGRP, 128, 4096], BF16, kind="Internal").ap()
    l1o_d = dt("l1oscr", [NG, 128, 4096], BF16, kind="Internal").ap()
    x1_d = dt("x1scr", [NT, 128, 1024], F32, kind="Internal").ap()
    wg_d = dt("wgscr", [2, 32, 1024], BF16, kind="Internal").ap()

    octx = ExitStack()
    S = Sched(nc)
    for s in ("dS", "dC", "dC2", "dW0", "dW1", "dP", "dF", "g0", "g1",
              "u0", "u1", "wr0", "wr1", "l10", "l11", "xq0", "xq1",
              "P", "A", "V", "O", "oc",
              "d0", "d1", "d2", "d3", "d4", "d5", "d6", "d7"):
        S.sem(octx, s)

    def SB(ctx, n, shp, d=F32):
        return ctx.enter_context(nc.sbuf_tensor(n, shp, d))

    ident_b = SB(octx, "ident_b", [128, 128], BF16)
    ishift_b = SB(octx, "ishift_b", [64, 256], BF16)
    ones_f = SB(octx, "ones_f", [128, 128], F32)
    ones_b = SB(octx, "ones_b", [128, 128], BF16)
    maskp_s = SB(octx, "maskp_s", [128, 256], F32)
    bias_s = SB(octx, "bias_s", [128, 512], F32)
    rb_s = SB(octx, "rb_s", [1, 32], F32)
    scal_s = SB(octx, "scal_s", [128, 4], F32)
    at_b = SB(octx, "at_b", [128, 1024], BF16)
    bqkv_b = SB(octx, "bqkv_b", [128, 8 * 128], BF16)
    bq_b = SB(octx, "bq_b", [128, 8 * 8], BF16)
    xt = [SB(octx, "xt0", [128, 1024]), SB(octx, "xt1", [128, 1024])]
    lnb = [SB(octx, "lnb0", [128, 1024], BF16), SB(octx, "lnb1", [128, 1024], BF16)]
    tmp_f = SB(octx, "tmpf", [128, 1024])
    lnscr = [dict(bn6=SB(octx, f"bn6_{i}", [128, 12]), mv=SB(octx, f"mv_{i}", [128, 2]),
                  sd=SB(octx, f"sd_{i}", [128, 1]), rs=SB(octx, f"rs_{i}", [128, 1]),
                  busy_v=None, busy_a=None) for i in range(2)]
    pA = octx.enter_context(nc.psum_tensor("pA", [128, 1024], F32))
    pB = octx.enter_context(nc.psum_tensor("pB", [128, 1024], F32))
    pC = octx.enter_context(nc.psum_tensor("pC", [128, 1024], F32))
    pT = octx.enter_context(nc.psum_tensor("pT", [128, 2048], BF16))

    ev_c = {}
    for key, dst, src in [("ident", ident_b, ident_d), ("ishift", ishift_b, ishift_d),
                          ("onesb", ones_b, ones_d), ("at", at_b, at_h),
                          ("bqkv", bqkv_b, bqkv_h), ("bq", bq_b, bq_h)]:
        ev_c[key] = S.dma("g", dst[:, :], src, sem="dC")
    for key, dst, src in [("ones", ones_f, ones_d), ("maskp", maskp_s, maskp_d),
                          ("bias", bias_s, bias_d), ("rb", rb_s, rb_h),
                          ("scal", scal_s, scal_d)]:
        ev_c[key] = S.dma("s", dst[:, :], src, sem="dC2")

    xt_free = [None, None]
    lnb_free = [None, None]
    pT_free = [None, None]
    ln_i = [0]

    def emit_ln(x_tensor, x_off, out_ap, x_evt, out_wait=()):
        x_ap = x_tensor[:, x_off:x_off + 1024]
        sc = lnscr[ln_i[0] % 2]
        ln_i[0] += 1
        S.emit("v", lambda h, a=sc["bn6"], b=x_tensor, o=x_off: h.bn_stats(
            a[:, 0:6], b[:, o:o + 512]),
            waits=[x_evt, sc["busy_v"], sc["busy_a"]])
        S.emit("v", lambda h, a=sc["bn6"], b=x_tensor, o=x_off: h.bn_stats(
            a[:, 6:12], b[:, o + 512:o + 1024]))
        S.drain("v")
        ev_mv = S.emit("v", lambda h, a=sc["mv"], b=sc["bn6"]: h.bn_aggr(a[:, :], b[:, :]), inc="V")
        ev_sd = S.emit("a", lambda h, a=sc["sd"], b=sc["mv"]: h.activation(
            out=a[:, :], in_=b[:, 1:2], func=AF.Sqrt, bias=scal_s[:, 0:1]),
            waits=[ev_mv, ev_c["scal"]], inc="A")
        S.emit("v", lambda h, a=sc["rs"], b=sc["sd"]: h.reciprocal(a[:, :], b[:, :]),
               waits=[ev_sd])
        S.drain("v")
        ev = S.emit("v", lambda h, o=out_ap, x=x_ap, m=sc["mv"], r=sc["rs"]: h.tensor_scalar(
            out=o, in0=x, scalar1=m[:, 0:1], scalar2=r[:, 0:1],
            op0=ALU.subtract, op1=ALU.mult), waits=list(out_wait), inc="V")
        sc["busy_v"] = ev
        sc["busy_a"] = ev_sd
        return ev

    def emit_ln_transpose(src_dram_row, dst_tensor, dst_off):
        i = ln_i[0] % 2
        ev_x = S.dma("a", xt[i][:, :], src_dram_row, waits=[xt_free[i]],
                     sem="xq%d" % i)
        ev_ln = emit_ln(xt[i], 0, lnb[i][:, :], ev_x, out_wait=[lnb_free[i]])
        xt_free[i] = ev_ln
        th = ln_i[0] % 2
        ev_tr = None
        for c in range(8):
            w = [ev_ln, ev_c["ident"]] + ([pT_free[th]] if c == 0 else [])
            ev_tr = S.emit("t", lambda h, c=c, i=i, th=th: h.transpose(
                pT[:, th * 1024 + c * 128: th * 1024 + (c + 1) * 128],
                in_=lnb[i][:, c * 128:(c + 1) * 128], identity=ident_b[:, :]),
                waits=w, inc="P")
        lnb_free[i] = ev_tr
        ev_cp = S.emit("a", lambda h, th=th, dst=mkap(dst_tensor, dst_off,
                       [[dst_tensor.shape[1], 128],
                        [dst_tensor.shape[1] // 8, 8], [1, 128]]): h.activation(
            out=dst, in_=pT[:, th * 1024:(th + 1) * 1024].rearrange("p (c n) -> p c n", n=128),
            func=AF.Copy), waits=[ev_tr], inc="A")
        pT_free[th] = ev_cp
        return ev_cp

    ab_ctx = ExitStack()
    O_sb = SB(ab_ctx, "O_sb", [128, 8 * 2048], BF16)
    a_ctx = ExitStack()
    t1kv = SB(a_ctx, "t1kv", [128, SGRP * 512], BF16)
    t1q = SB(a_ctx, "t1q", [8, NG * 512], BF16)
    l1t = [SB(a_ctx, "l1t0", [128, 4096], BF16), SB(a_ctx, "l1t1", [128, 4096], BF16)]
    wq_s = SB(a_ctx, "wq_s", [128, 8 * 512], BF16)
    wk_s = SB(a_ctx, "wk_s", [128, 8 * 512], BF16)
    wv_s = SB(a_ctx, "wv_s", [128, 8 * 512], BF16)
    K_sb = SB(a_ctx, "K_sb", [128, 4 * 4096], BF16)
    V_sb = SB(a_ctx, "V_sb", [128, 8 * 32 * 66], BF16)
    Q_sb = SB(a_ctx, "Q_sb", [128, 4 * 2048], BF16)
    Pt = [SB(a_ctx, "Pt0", [128, 512], BF16), SB(a_ctx, "Pt1", [128, 512], BF16),
          SB(a_ctx, "Pt2", [128, 512], BF16)]
    rec_s = SB(a_ctx, "rec_s", [128, 512], BF16)
    lnd_s = SB(a_ctx, "lnd_s", [128, 512])
    bc_s = SB(a_ctx, "bc_s", [64, 512])
    On_s = SB(a_ctx, "On_s", [64, 512], BF16)

    l1t_free = [None, None]
    ev_t1_seq = [None] * SGRP
    ev_t1_own = [None] * NG
    pfree = {k: None for k in ("A0", "A1", "B0", "B1", "C0", "C1")}
    PREG = {"A0": (pA, 0), "A1": (pA, 512), "B0": (pB, 0), "B1": (pB, 512),
            "C0": (pC, 0), "C1": (pC, 512)}

    ev_vones = S.emit("v", lambda h: h.memset(
        V_sb[:, :].rearrange("p (s n) -> p s n", n=66)[:, :, 64:66], 1.0), inc="V")

    def emit_weight_half(hh, extra_wait):
        ev = None
        for dst, src in ((wq_s, wq_h), (wk_s, wk_h), (wv_s, wv_h)):
            for c in range(8):
                ev = S.dma("g", dst[:, c * 512:(c + 1) * 512],
                           src[:, c * 1024 + hh * 512: c * 1024 + hh * 512 + 512],
                           waits=extra_wait, sem="dW%d" % hh)
        return ev

    def emit_qkv_half(hh, ev_w, first):
        kv_last, q_last = None, None
        acc_regs = ["A0", "A1", "B0", "B1"]
        for g in range(SGRP):
            gb = g % 2
            if first:
                subev = []
                for s in range(4):
                    subev.append(emit_ln_transpose(
                        xseq[(g * 4 + s) * 128:(g * 4 + s + 1) * 128, :],
                        l1t[gb], s * 128))
                ev_l1 = subev[-1]
                S.dma("s", l1s_d[g], l1t[gb][:, :], waits=subev)
                for ic in range(8):
                    w = [ev_l1, ev_c["bqkv"]] + ([pfree["C0"]] if ic == 0 else [])
                    S.emit("t", lambda h, ic=ic, gb=gb: h.matmul(
                        pC[:, 0:512], lhsT=bqkv_b[:, ic * 128:(ic + 1) * 128],
                        rhs=l1t[gb][:, ic * 512:(ic + 1) * 512],
                        start=(ic == 0), stop=(ic == 7)), waits=w,
                        inc="P" if ic == 7 else None)
                ev_t1p = ("P", S.cnt["P"])
                S.emit("a", lambda h, g=g: h.activation(
                    out=t1kv[32:40, g * 512:(g + 1) * 512], in_=pC[32:40, 0:512],
                    func=AF.Copy, scale=scal_s[32:40, 1:2]),
                    waits=[ev_t1p, ev_c["scal"]], inc="A")
                ev_t1_seq[g] = S.emit("a", lambda h, g=g: h.activation(
                    out=t1kv[64:72, g * 512:(g + 1) * 512], in_=pC[64:72, 0:512],
                    func=AF.Copy, scale=scal_s[64:72, 1:2]), inc="A")
                pfree["C0"] = ev_t1_seq[g]
            else:
                ev_l1 = S.dma("s", l1t[gb][:, :], l1s_d[g], waits=[l1t_free[gb]],
                              sem="l1%d" % gb)
            for c in range(4):
                pp, off = PREG[acc_regs[c]]
                for ic in range(8):
                    w = [ev_l1, ev_w] + ([pfree[acc_regs[c]]] if ic == 0 else [])
                    S.emit("t", lambda h, c=c, ic=ic, gb=gb, pp=pp, off=off: h.matmul(
                        pp[:, off:off + 512],
                        lhsT=wk_s[:, ic * 512 + c * 128: ic * 512 + (c + 1) * 128],
                        rhs=l1t[gb][:, ic * 512:(ic + 1) * 512],
                        start=(ic == 0), stop=False), waits=w)
                S.emit("t", lambda h, c=c, g=g, hh=hh, pp=pp, off=off: h.matmul(
                    pp[:, off:off + 512],
                    lhsT=at_b[32:40, hh * 512 + c * 128: hh * 512 + (c + 1) * 128],
                    rhs=t1kv[32:40, g * 512:(g + 1) * 512],
                    start=False, stop=True),
                    waits=[ev_t1_seq[g], ev_c["at"]], inc="P")
                ev_kp = ("P", S.cnt["P"])
                kv_last = S.emit("a", lambda h, c=c, g=g, pp=pp, off=off: h.activation(
                    out=K_sb[:, c * 4096 + g * 512: c * 4096 + (g + 1) * 512],
                    in_=pp[:, off:off + 512], func=AF.Copy), waits=[ev_kp], inc="A")
                pfree[acc_regs[c]] = kv_last
            for s in range(4):
                kt = g * 4 + s
                vr = "C0" if s % 2 == 0 else "C1"
                pp, off = PREG[vr]
                for ic in range(8):
                    w = [ev_l1, ev_w] + ([pfree[vr]] if ic == 0 else [])
                    S.emit("t", lambda h, s=s, ic=ic, gb=gb, pp=pp, off=off: h.matmul(
                        pp[:, off:off + 512],
                        lhsT=l1t[gb][:, ic * 512 + s * 128: ic * 512 + (s + 1) * 128],
                        rhs=wv_s[:, ic * 512:(ic + 1) * 512],
                        start=(ic == 0), stop=False), waits=w)
                S.emit("t", lambda h, g=g, s=s, hh=hh, pp=pp, off=off: h.matmul(
                    pp[:, off:off + 512],
                    lhsT=t1kv[64:72, (g * 4 + s) * 128:(g * 4 + s + 1) * 128],
                    rhs=at_b[64:72, hh * 512:(hh + 1) * 512],
                    start=False, stop=True),
                    waits=[ev_t1_seq[g], ev_c["at"]], inc="P")
                ev_vp = ("P", S.cnt["P"])
                dst = mkap(V_sb, kt * 66, [[8 * 32 * 66, 128], [32 * 66, 8], [1, 64]])
                kv_last = S.emit("a", lambda h, pp=pp, off=off, dst=dst: h.activation(
                    out=dst, in_=pp[:, off:off + 512].rearrange("p (c n) -> p c n", n=64),
                    func=AF.Copy), waits=[ev_vp, ev_vones], inc="A")
                pfree[vr] = kv_last
            l1t_free[gb] = ("P", S.cnt["P"])
        for og in range(NG):
            gb = og % 2
            if first:
                subev = []
                for s in range(4):
                    subev.append(emit_ln_transpose(
                        xown[(og * 4 + s) * 128:(og * 4 + s + 1) * 128, :],
                        l1t[gb], s * 128))
                ev_l1 = subev[-1]
                S.dma("s", l1o_d[og], l1t[gb][:, :], waits=subev)
                for ic in range(8):
                    w = [ev_l1, ev_c["bq"]] + ([pfree["C0"]] if ic == 0 else [])
                    S.emit("t", lambda h, ic=ic, gb=gb: h.matmul(
                        pC[0:8, 0:512], lhsT=bq_b[:, ic * 8:(ic + 1) * 8],
                        rhs=l1t[gb][:, ic * 512:(ic + 1) * 512],
                        start=(ic == 0), stop=(ic == 7)), waits=w,
                        inc="P" if ic == 7 else None)
                ev_qt1p = ("P", S.cnt["P"])
                ev_t1_own[og] = S.emit("a", lambda h, og=og: h.activation(
                    out=t1q[0:8, og * 512:(og + 1) * 512], in_=pC[0:8, 0:512],
                    func=AF.Copy, scale=scal_s[0:8, 1:2]),
                    waits=[ev_qt1p, ev_c["scal"]], inc="A")
                pfree["C0"] = ev_t1_own[og]
            else:
                ev_l1 = S.dma("s", l1t[gb][:, :], l1o_d[og], waits=[l1t_free[gb]],
                              sem="l1%d" % gb)
            for c in range(4):
                pp, off = PREG[acc_regs[c]]
                for ic in range(8):
                    w = [ev_l1, ev_w] + ([pfree[acc_regs[c]]] if ic == 0 else [])
                    S.emit("t", lambda h, c=c, ic=ic, gb=gb, pp=pp, off=off: h.matmul(
                        pp[:, off:off + 512],
                        lhsT=wq_s[:, ic * 512 + c * 128: ic * 512 + (c + 1) * 128],
                        rhs=l1t[gb][:, ic * 512:(ic + 1) * 512],
                        start=(ic == 0), stop=False), waits=w)
                S.emit("t", lambda h, c=c, og=og, hh=hh, pp=pp, off=off: h.matmul(
                    pp[:, off:off + 512],
                    lhsT=at_b[0:8, hh * 512 + c * 128: hh * 512 + (c + 1) * 128],
                    rhs=t1q[0:8, og * 512:(og + 1) * 512],
                    start=False, stop=True),
                    waits=[ev_t1_own[og], ev_c["at"]], inc="P")
                ev_qp = ("P", S.cnt["P"])
                q_last = S.emit("a", lambda h, c=c, og=og, pp=pp, off=off: h.activation(
                    out=Q_sb[:, c * 2048 + og * 512: c * 2048 + (og + 1) * 512],
                    in_=pp[:, off:off + 512], func=AF.Copy), waits=[ev_qp], inc="A")
                pfree[acc_regs[c]] = q_last
            l1t_free[gb] = ("P", S.cnt["P"])
        return kv_last, q_last

    def emit_attn_half(hh, ev_ready):
        Pt_free = [None, None, None]
        SREG = [("C0", pC, 0), ("C1", pC, 512), ("B0", pB, 0)]
        state = dict(rec_free=None, bc_free=None, On_free=None, chunk_free=None)

        def emit_norm(gq, hl, ev_pv):
            cl, hp = hl // 2, hl % 2
            o_reg = "A0" if hp == 0 else "A1"
            ocol = (hh * 4 + cl) * 2048 + gq * 512
            ev_ld = S.emit("a", lambda h, hp=hp: h.activation(
                out=lnd_s[64:65, :], in_=pA[64:65, hp * 512:(hp + 1) * 512],
                func=AF.Ln), waits=[ev_pv, state["rec_free"]], inc="A")
            ev_rec = S.emit("a", lambda h: h.activation(
                out=rec_s[64:65, :], in_=lnd_s[64:65, :], func=AF.Exp,
                scale=scal_s[64:65, 3:4]), waits=[ev_c["scal"]], inc="A")
            ev_bc = S.emit("t", lambda h: h.matmul(
                pB[:, 512:1024], lhsT=ones_b[64:65, 0:128], rhs=rec_s[64:65, 0:512],
                start=True, stop=True),
                waits=[ev_rec, ev_c["onesb"], pfree["B1"]], inc="P")
            state["rec_free"] = ev_bc
            ev_bce = S.emit("v", lambda h: h.tensor_copy(
                bc_s[:, :], pB[0:64, 512:1024]),
                waits=[ev_bc, state["bc_free"]], inc="V")
            pfree["B1"] = ev_bce
            if hp == 0:
                ev_on = S.emit("v", lambda h, ocol=ocol: h.tensor_tensor(
                    out=O_sb[0:64, ocol:ocol + 512], in0=pA[0:64, 0:512],
                    in1=bc_s[:, :], op=ALU.mult), waits=[ev_bce], inc="V")
                state["bc_free"] = ev_on
                pfree["A0"] = ev_on
            else:
                ev_on = S.emit("v", lambda h: h.tensor_tensor(
                    out=On_s[:, :], in0=pA[0:64, 512:1024],
                    in1=bc_s[:, :], op=ALU.mult),
                    waits=[ev_bce, state["On_free"]], inc="V")
                state["bc_free"] = ev_on
                pfree["A1"] = ev_on
                state["On_free"] = S.dma(
                    "s", O_sb[64:128, ocol:ocol + 512], On_s[0:64, :],
                    waits=[ev_on], sem="oc")
            return state["On_free"]

        pending = None
        ev_oc = None
        for gq in range(NG):
            for hl in range(8):
                h_g = hh * 8 + hl
                cl, hp = hl // 2, hl % 2
                o_reg = "A0" if hp == 0 else "A1"
                nkt = 8 * gq + 8
                ev_pv = None
                def emit_s(kt):
                    cs = 128 * max(0, kt // 2 - 4 * gq)
                    sr, sp, soff = SREG[kt % 3]
                    w = [ev_ready, pfree[sr]]
                    ev_s = S.emit("t", lambda h, hp=hp, cl=cl, kt=kt, gq=gq, cs=cs, sp=sp, soff=soff: h.matmul(
                        sp[:, soff + cs: soff + 512],
                        lhsT=K_sb[hp * 64:(hp + 1) * 64,
                                  cl * 4096 + kt * 128: cl * 4096 + (kt + 1) * 128],
                        rhs=Q_sb[hp * 64:(hp + 1) * 64,
                                 cl * 2048 + gq * 512 + cs: cl * 2048 + (gq + 1) * 512],
                        start=True, stop=True), waits=w, inc="P")
                    ev_m = None
                    if kt >= 8 * gq:
                        jm = kt // 2 - 4 * gq
                        ev_m = S.emit("v", lambda h, sp=sp, soff=soff, jm=jm, kt=kt: h.tensor_tensor(
                            out=sp[:, soff + jm * 128: soff + (jm + 1) * 128],
                            in0=sp[:, soff + jm * 128: soff + (jm + 1) * 128],
                            in1=maskp_s[:, (kt % 2) * 128:(kt % 2 + 1) * 128],
                            op=ALU.add), waits=[ev_s, ev_c["maskp"]], inc="V")
                    ev_e = S.emit("a", lambda h, kt=kt, cs=cs, h_g=h_g, sp=sp, soff=soff: h.activation(
                        out=Pt[kt % 3][:, cs:512], in_=sp[:, soff + cs: soff + 512],
                        func=AF.Exp, bias=bias_s[:, h_g * 32 + kt: h_g * 32 + kt + 1],
                        scale=scal_s[:, 2:3]),
                        waits=[ev_s, ev_m, ev_c["bias"], ev_c["scal"], Pt_free[kt % 3]], inc="A")
                    pfree[sr] = ev_e
                    return ev_e

                def emit_pv(kt, ev_e):
                    cs = 128 * max(0, kt // 2 - 4 * gq)
                    w2 = [ev_e] + ([pfree[o_reg]] if kt == 0 else [])
                    ev_pv = S.emit("t", lambda h, hl=hl, kt=kt, cs=cs, hp=hp, nkt=nkt: h.matmul(
                        pA[0:65, hp * 512 + cs:(hp + 1) * 512],
                        lhsT=V_sb[:, (hl * 32 + kt) * 66: (hl * 32 + kt) * 66 + 65],
                        rhs=Pt[kt % 3][:, cs:512],
                        start=(kt == 0), stop=(kt == nkt - 1)), waits=w2, inc="P")
                    Pt_free[kt % 3] = ev_pv
                    return ev_pv

                LAG = 2
                ev_es = {}
                ev_pv = None
                for kt in range(nkt + LAG):
                    if kt < nkt:
                        ev_es[kt] = emit_s(kt)
                    if kt - LAG >= 0:
                        ev_pv = emit_pv(kt - LAG, ev_es.pop(kt - LAG))
                if pending is not None:
                    ev_oc = emit_norm(*pending) or ev_oc
                pending = (gq, hl, ev_pv)
        ev_oc = emit_norm(*pending) or ev_oc
        return [ev_oc, ("V", S.cnt["V"])]

    ev_w0 = emit_weight_half(0, [])
    kv0, q0 = emit_qkv_half(0, ev_w0, first=True)
    ev_oc0 = emit_attn_half(0, q0)
    ev_w1 = emit_weight_half(1, [("P", S.cnt["P"])])
    kv1, q1 = emit_qkv_half(1, ev_w1, first=False)
    ev_oc1 = emit_attn_half(1, q1)

    a_ctx.close()
    S.set_floor()
    b_ctx = ExitStack()
    wp_s = SB(b_ctx, "wp_s", [128, 8 * 1024], BF16)
    at_f = [SB(b_ctx, "atf0", [128, 1024]), SB(b_ctx, "atf1", [128, 1024])]
    xo_f = [SB(b_ctx, "xof0", [128, 1024]), SB(b_ctx, "xof1", [128, 1024])]
    x1t = [SB(b_ctx, "x1t0", [128, 1024]), SB(b_ctx, "x1t1", [128, 1024])]
    ev_wp = None
    for ic in range(8):
        ev_wp = S.dma("g", wp_s[:, ic * 1024:(ic + 1) * 1024],
                      wp_h[:, ic * 1024:(ic + 1) * 1024], sem="dP")
    pacc_free = [None, None]
    at_free = [None, None]
    xo_free = [None, None]
    x1t_free = [None, None]
    ev_x1 = [None] * NT
    for j in range(NT):
        pb = j % 2
        pp = pA if pb == 0 else pB
        ev_xo = S.dma("a", xo_f[pb][:, :], xown[j * 128:(j + 1) * 128, :],
                      waits=[xo_free[pb]], sem="xq%d" % pb)
        for ic in range(8):
            for hf in range(2):
                w = list(ev_oc1) + [ev_wp] + ([pacc_free[pb]] if (ic == 0 and hf == 0) else [])
                S.emit("t", lambda h, ic=ic, hf=hf, j=j, pp=pp: h.matmul(
                    pp[:, hf * 512:(hf + 1) * 512],
                    lhsT=O_sb[:, ic * 2048 + j * 128:ic * 2048 + (j + 1) * 128],
                    rhs=wp_s[:, ic * 1024 + hf * 512: ic * 1024 + (hf + 1) * 512],
                    start=(ic == 0), stop=(ic == 7)),
                    waits=w, inc="P" if (ic == 7 and hf == 1) else None)
        ev_wpp = ("P", S.cnt["P"])
        ev_ae = S.emit("a", lambda h, pb=pb, pp=pp: h.activation(
            out=at_f[pb][:, :], in_=pp[:, :], func=AF.Copy),
            waits=[ev_wpp, at_free[pb]], inc="A")
        pacc_free[pb] = ev_ae
        ev_ln = emit_ln(at_f[pb], 0, tmp_f[:, :], ev_ae)
        at_free[pb] = ev_ln
        S.drain("v")
        ev_x1t = S.emit("v", lambda h, pb=pb: h.tensor_tensor(
            out=x1t[pb][:, :], in0=tmp_f[:, :], in1=xo_f[pb][:, :], op=ALU.add),
            waits=[ev_xo, x1t_free[pb]], inc="V")
        xo_free[pb] = ev_x1t
        ev_x1[j] = S.dma("s", x1_d[j], x1t[pb][:, :], waits=[ev_x1t])
        x1t_free[pb] = ev_x1[j]

    if DEBUG_STOP_AT_X1:
        for j in range(NT):
            S.dma("s", out_d[j * 128:(j + 1) * 128, :], x1_d[j],
                  waits=[ev_x1[j]], sem="O")
        S.emit("s", lambda h: None, waits=[("O", S.cnt["O"])])
        b_ctx.close(); ab_ctx.close()
        with octx, nc.Block() as block:
            S.replay(block)
        return nc

    b_ctx.close()
    ab_ctx.close()
    S.set_floor()
    c_ctx = ExitStack()
    ln2T = SB(c_ctx, "ln2T", [128, 8 * 1024], BF16)
    rw_s = SB(c_ctx, "rw_s", [128, 8 * 32], BF16)
    sgw_s = SB(c_ctx, "sgw_s", [128, 8 * 512], BF16)
    suw_s = SB(c_ctx, "suw_s", [128, 8 * 512], BF16)
    sdw_s = SB(c_ctx, "sdw_s", [128, 4 * 1024], BF16)
    gw_s = [SB(c_ctx, "gw_s0", [128, 1024], BF16), SB(c_ctx, "gw_s1", [128, 1024], BF16)]
    uw_s = [SB(c_ctx, "uw_s0", [128, 1024], BF16), SB(c_ctx, "uw_s1", [128, 1024], BF16)]
    dw_s = [SB(c_ctx, f"dw_s{i}", [128, 1024], BF16) for i in range(8)]
    sc_f = SB(c_ctx, "sc_f", [128, 32])
    cur_f = SB(c_ctx, "cur_f", [128, 32])
    msk_f = SB(c_ctx, "msk_f", [128, 32])
    m_f = SB(c_ctx, "m_f", [128, 1])
    wu_f = SB(c_ctx, "wu_f", [128, 32])
    w_b = SB(c_ctx, "w_b", [128, 32], BF16)
    wfm = SB(c_ctx, "wfm", [32, 1024], BF16)
    wrow = [SB(c_ctx, "wrow0", [1, 1024], BF16), SB(c_ctx, "wrow1", [1, 1024], BF16)]
    sg_b = SB(c_ctx, "sg_b", [128, 1024], BF16)
    wb_b = SB(c_ctx, "wb_b", [128, 1024], BF16)
    h1_b = SB(c_ctx, "h1_b", [128, 1024], BF16)
    h_blk = SB(c_ctx, "h_blk", [128, 8 * 1024], BF16)
    hsh = SB(c_ctx, "hsh", [128, 4 * 1024], BF16)
    moe = SB(c_ctx, "moe", [128, 8 * 1024])
    out_f = [SB(c_ctx, "outf0", [128, 1024]), SB(c_ctx, "outf1", [128, 1024])]
    wdbg = SB(c_ctx, "wdbg", [128, 32])
    x1e = [SB(c_ctx, "x1e0", [128, 1024]), SB(c_ctx, "x1e1", [128, 1024])]

    ev_rw = S.dma("g", rw_s[:, :], rw_h, sem="dF")
    ev_sg = S.dma("g", sgw_s[:, :], sg_h, sem="dF")
    ev_su = S.dma("g", suw_s[:, :], su_h, sem="dF")
    ev_sd = S.dma("g", sdw_s[:, :], sd_h, sem="dF")

    def l2rhs(ic, hf):
        return mkap(ln2T, ic * 1024 + hf * 512, [[8192, 128], [1, 512]])

    ev_moe = [None] * 8
    w_b_free = [None]
    gw_free = [None, None]
    uw_free = [None, None]
    dw_free = [None] * 8
    wrow_free = [None, None]
    h_blk_free = [None] * 8
    sg_free, wb_free, h1_free = None, None, None
    x1e_free = [None, None]
    out_free = [None, None]
    topk_free = None
    for G in range(2):
        ev_l2_last = None
        ev_wfm = []
        pfree["C0"] = pfree.get("C0")
        for st in range(8):
            j = G * 8 + st
            pb = st % 2
            ev_x1l = S.dma("a", x1e[pb][:, :], x1_d[j], waits=[ev_x1[j], x1e_free[pb]],
                           sem="xq%d" % pb)
            ev_ln = emit_ln(x1e[pb], 0, lnb[pb][:, :], ev_x1l,
                            out_wait=[lnb_free[pb]])
            x1e_free[pb] = ev_ln
            th = st % 2
            ev_tr = None
            for c in range(8):
                w = [ev_ln] + ([pT_free[th]] if c == 0 else [])
                ev_tr = S.emit("t", lambda h, c=c, pb=pb, th=th: h.transpose(
                    pT[:, th * 1024 + c * 128: th * 1024 + (c + 1) * 128],
                    in_=lnb[pb][:, c * 128:(c + 1) * 128], identity=ident_b[:, :]),
                    waits=w, inc="P")
            lnb_free[pb] = ev_tr
            ev_l2_last = S.emit("a", lambda h, th=th, dst=mkap(
                ln2T, st * 128, [[8192, 128], [1024, 8], [1, 128]]): h.activation(
                out=dst, in_=pT[:, th * 1024:(th + 1) * 1024].rearrange("p (c n) -> p c n", n=128),
                func=AF.Copy), waits=[ev_tr], inc="A")
            pT_free[th] = ev_l2_last
            for ic in range(8):
                w = [ev_l2_last, ev_rw] + ([pfree["C0"]] if ic == 0 else [])
                S.emit("t", lambda h, ic=ic, st=st: h.matmul(
                    pC[:, 0:32],
                    lhsT=mkap(ln2T, ic * 1024 + st * 128, [[8192, 128], [1, 128]]),
                    rhs=rw_s[:, ic * 32:(ic + 1) * 32],
                    start=(ic == 0), stop=False), waits=w)
            S.emit("t", lambda h: h.matmul(
                pC[:, 0:32], lhsT=ones_f[0:1, 0:128], rhs=rb_s[0:1, :],
                start=False, stop=True), waits=[ev_c["rb"], ev_c["ones"]], inc="P")
            ev_rt = ("P", S.cnt["P"])
            ev_nm = S.emit("v", lambda h: h.reduce_max(
                m_f[:, :], pC[:, 0:32], axis=AX.X, negate=True),
                waits=[ev_rt, topk_free], inc="V")
            ev_sc = S.emit("a", lambda h: h.activation(
                out=sc_f[:, :], in_=pC[:, 0:32], func=AF.Exp, bias=m_f[:, 0:1]),
                waits=[ev_nm], inc="A")
            pfree["C0"] = ev_sc
            S.emit("v", lambda h: h.tensor_copy(cur_f[:, :], sc_f[:, :]), waits=[ev_sc])
            for it in range(3):
                S.drain("v")
                S.emit("v", lambda h: h.reduce_max(m_f[:, :], cur_f[:, :], axis=AX.X),
                       waits=[ev_sc])
                S.drain("v")
                S.emit("v", lambda h: h.tensor_scalar(
                    out=msk_f[:, :], in0=cur_f[:, :], scalar1=m_f[:, 0:1],
                    scalar2=1e30, op0=ALU.is_ge, op1=ALU.mult))
                S.drain("v")
                S.emit("v", lambda h: h.tensor_sub(cur_f[:, :], cur_f[:, :], msk_f[:, :]))
            S.drain("v")
            S.emit("v", lambda h: h.reduce_max(m_f[:, :], cur_f[:, :], axis=AX.X))
            S.drain("v")
            S.emit("v", lambda h: h.tensor_scalar(
                out=msk_f[:, :], in0=sc_f[:, :], scalar1=m_f[:, 0:1],
                scalar2=None, op0=ALU.is_ge))
            S.drain("v")
            S.emit("v", lambda h: h.tensor_mul(wu_f[:, :], sc_f[:, :], msk_f[:, :]))
            S.drain("v")
            S.emit("v", lambda h: h.reduce_sum(m_f[:, :], wu_f[:, :], axis=AX.X))
            S.drain("v")
            S.emit("v", lambda h: h.reciprocal(m_f[:, :], m_f[:, :]))
            S.drain("v")
            ev_wb = S.emit("v", lambda h: h.tensor_scalar(
                out=w_b[:, :], in0=wu_f[:, :], scalar1=m_f[:, 0:1], scalar2=None,
                op0=ALU.mult), waits=[w_b_free[0]], inc="V")
            topk_free = ev_wb
            ev_wt = S.emit("t", lambda h, th=th: h.transpose(
                pT[0:32, th * 1024: th * 1024 + 128],
                in_=w_b[:, :], identity=ident_b[:, :]),
                waits=[ev_wb, pT_free[th]], inc="P")
            ev_wf = S.emit("a", lambda h, st=st, th=th: h.activation(
                out=wfm[0:32, st * 128:(st + 1) * 128],
                in_=pT[0:32, th * 1024: th * 1024 + 128],
                func=AF.Copy), waits=[ev_wt], inc="A")
            ev_wfm.append(ev_wf)
            pT_free[th] = ev_wf
            w_b_free[0] = ev_wt
            if DEBUG_FFN_W:
                ev_wd = S.emit("v", lambda h: h.tensor_copy(wdbg[:, :], w_b[:, :]),
                               waits=[ev_wb], inc="V")
                S.dma("s", out_d[j * 128:(j + 1) * 128, 0:32], wdbg[:, :],
                      waits=[ev_wd], sem="O")
        ev_wsc = S.dma("s", wg_d[G], wfm[:, :], waits=ev_wfm)
        if DEBUG_FFN_W:
            continue
        for blk in range(DEBUG_NBLK):
            ev_h_all = None
            for e8 in range(8):
                e = blk * 8 + e8
                eb = e % 2
                ev_gw = S.dma("g", gw_s[eb][:, :], gw_h[e],
                              waits=[gw_free[eb]], sem="g%d" % eb)
                ev_uw = S.dma("g", uw_s[eb][:, :], uw_h[e],
                              waits=[uw_free[eb]], sem="u%d" % eb)
                ev_dwl = S.dma("g", dw_s[e8][:, :], dw_h[e],
                               waits=[dw_free[e8]], sem="d%d" % e8)
                dw_free[e8] = ev_dwl
                ev_wr = S.dma("s", wrow[eb][0:1, :], wg_d[G][e:e + 1, :],
                              waits=[ev_wsc, wrow_free[eb]], sem="wr%d" % eb)
                for ic in range(8):
                    for hf in range(2):
                        w = [ev_gw, ev_l2_last] + (
                            [pfree["A0"], pfree["A1"]] if (ic == 0 and hf == 0) else [])
                        S.emit("t", lambda h, ic=ic, hf=hf, eb=eb: h.matmul(
                            pA[:, hf * 512:(hf + 1) * 512],
                            lhsT=gw_s[eb][:, ic * 128:(ic + 1) * 128],
                            rhs=l2rhs(ic, hf), start=(ic == 0), stop=(ic == 7)),
                            waits=w, inc="P" if (ic == 7 and hf == 1) else None)
                ev_g = ("P", S.cnt["P"])
                gw_free[eb] = ev_g
                for ic in range(8):
                    for hf in range(2):
                        w = [ev_uw, ev_l2_last] + (
                            [pfree["B0"], pfree["B1"]] if (ic == 0 and hf == 0) else [])
                        S.emit("t", lambda h, ic=ic, hf=hf, eb=eb: h.matmul(
                            pB[:, hf * 512:(hf + 1) * 512],
                            lhsT=uw_s[eb][:, ic * 128:(ic + 1) * 128],
                            rhs=l2rhs(ic, hf), start=(ic == 0), stop=(ic == 7)),
                            waits=w, inc="P" if (ic == 7 and hf == 1) else None)
                ev_u = ("P", S.cnt["P"])
                uw_free[eb] = ev_u
                for hf in range(2):
                    w = [ev_wr, ev_c["onesb"]] + (
                        [pfree["C0"], pfree["C1"]] if hf == 0 else [])
                    S.emit("t", lambda h, hf=hf, eb=eb: h.matmul(
                        pC[:, hf * 512:(hf + 1) * 512], lhsT=ones_b[0:1, 0:128],
                        rhs=wrow[eb][0:1, hf * 512:(hf + 1) * 512],
                        start=True, stop=True), waits=w,
                        inc="P" if hf == 1 else None)
                ev_wbp = ("P", S.cnt["P"])
                wrow_free[eb] = ev_wbp
                ev_sgb = S.emit("a", lambda h: h.activation(
                    out=sg_b[:, :], in_=pA[:, :], func=AF.Silu),
                    waits=[ev_g, sg_free], inc="A")
                pfree["A0"] = pfree["A1"] = ev_sgb
                ev_wbe = S.emit("a", lambda h: h.activation(
                    out=wb_b[:, :], in_=pC[:, :], func=AF.Copy),
                    waits=[ev_wbp, wb_free], inc="A")
                pfree["C0"] = pfree["C1"] = ev_wbe
                ev_h1 = S.emit("v", lambda h: h.tensor_tensor(
                    out=h1_b[:, :], in0=sg_b[:, :], in1=pB[:, :], op=ALU.mult),
                    waits=[ev_sgb, ev_u, h1_free], inc="V")
                sg_free = ev_h1
                pfree["B0"] = pfree["B1"] = ev_h1
                ev_h = S.emit("v", lambda h, e8=e8: h.tensor_tensor(
                    out=h_blk[:, e8 * 1024:(e8 + 1) * 1024], in0=h1_b[:, :],
                    in1=wb_b[:, :], op=ALU.mult),
                    waits=[ev_h1, ev_wbe, h_blk_free[e8]], inc="V")
                h1_free = ev_h
                wb_free = ev_h
                ev_h_all = ev_h
            dn_regs = [("C0", "C1", pC), ("A0", "A1", pA)]
            for st in range(8):
                r0, r1, pp = dn_regs[st % 2]
                for e8 in range(8):
                    w = [ev_h_all, dw_free[e8]] + (
                        [pfree[r0], pfree[r1]] if e8 == 0 else [])
                    S.emit("t", lambda h, e8=e8, st=st, pp=pp: h.matmul(
                        pp[:, 0:512],
                        lhsT=h_blk[:, e8 * 1024 + st * 128: e8 * 1024 + (st + 1) * 128],
                        rhs=dw_s[e8][:, 0:512],
                        start=(e8 == 0), stop=False), waits=w)
                    S.emit("t", lambda h, e8=e8, st=st, pp=pp: h.matmul(
                        pp[:, 512:1024],
                        lhsT=h_blk[:, e8 * 1024 + st * 128: e8 * 1024 + (st + 1) * 128],
                        rhs=dw_s[e8][:, 512:1024],
                        start=(e8 == 0), stop=(e8 == 7)),
                        inc="P" if e8 == 7 else None)
                ev_dn = ("P", S.cnt["P"])
                if blk == 0:
                    ev_mo = S.emit("v", lambda h, st=st, pp=pp: h.tensor_copy(
                        moe[:, st * 1024:(st + 1) * 1024], pp[:, :]),
                        waits=[ev_dn, ev_moe[st]], inc="V")
                else:
                    ev_mo = S.emit("v", lambda h, st=st, pp=pp: h.tensor_tensor(
                        out=moe[:, st * 1024:(st + 1) * 1024],
                        in0=moe[:, st * 1024:(st + 1) * 1024], in1=pp[:, :],
                        op=ALU.add), waits=[ev_dn, ev_moe[st]], inc="V")
                ev_moe[st] = ev_mo
                pfree[r0] = pfree[r1] = ev_mo
            ev_dn_done = ("P", S.cnt["P"])
            for e8 in range(8):
                h_blk_free[e8] = ev_dn_done
                dw_free[e8] = ev_dn_done
        for hc in range(0 if DEBUG_NO_SHARED else 4):
            for ic in range(8):
                for hf in range(2):
                    w = [ev_sg, ev_l2_last] + (
                        [pfree["A0"], pfree["A1"]] if (ic == 0 and hf == 0) else [])
                    S.emit("t", lambda h, ic=ic, hf=hf, hc=hc: h.matmul(
                        pA[:, hf * 512:(hf + 1) * 512],
                        lhsT=sgw_s[:, ic * 512 + hc * 128: ic * 512 + (hc + 1) * 128],
                        rhs=l2rhs(ic, hf), start=(ic == 0), stop=(ic == 7)),
                        waits=w, inc="P" if (ic == 7 and hf == 1) else None)
            ev_g = ("P", S.cnt["P"])
            for ic in range(8):
                for hf in range(2):
                    w = [ev_su, ev_l2_last] + (
                        [pfree["B0"], pfree["B1"]] if (ic == 0 and hf == 0) else [])
                    S.emit("t", lambda h, ic=ic, hf=hf, hc=hc: h.matmul(
                        pB[:, hf * 512:(hf + 1) * 512],
                        lhsT=suw_s[:, ic * 512 + hc * 128: ic * 512 + (hc + 1) * 128],
                        rhs=l2rhs(ic, hf), start=(ic == 0), stop=(ic == 7)),
                        waits=w, inc="P" if (ic == 7 and hf == 1) else None)
            ev_u = ("P", S.cnt["P"])
            ev_sgb = S.emit("a", lambda h: h.activation(
                out=sg_b[:, :], in_=pA[:, :], func=AF.Silu),
                waits=[ev_g, sg_free], inc="A")
            pfree["A0"] = pfree["A1"] = ev_sgb
            ev_h = S.emit("v", lambda h, hc=hc: h.tensor_tensor(
                out=hsh[:, hc * 1024:(hc + 1) * 1024], in0=sg_b[:, :], in1=pB[:, :],
                op=ALU.mult), waits=[ev_sgb, ev_u], inc="V")
            sg_free = ev_h
            pfree["B0"] = pfree["B1"] = ev_h
        if DEBUG_NO_SHARED:
            for hc in []: pass
        ev_hsh = ("V", S.cnt["V"])
        dn_regs = [("C0", "C1", pC), ("A0", "A1", pA)]
        for st in range(0 if DEBUG_NO_SHARED else 8):
            r0, r1, pp = dn_regs[st % 2]
            for hc in range(4):
                w = [ev_hsh, ev_sd] + ([pfree[r0], pfree[r1]] if hc == 0 else [])
                S.emit("t", lambda h, hc=hc, st=st, pp=pp: h.matmul(
                    pp[:, 0:512],
                    lhsT=hsh[:, hc * 1024 + st * 128: hc * 1024 + (st + 1) * 128],
                    rhs=sdw_s[:, hc * 1024: hc * 1024 + 512],
                    start=(hc == 0), stop=False), waits=w)
                S.emit("t", lambda h, hc=hc, st=st, pp=pp: h.matmul(
                    pp[:, 512:1024],
                    lhsT=hsh[:, hc * 1024 + st * 128: hc * 1024 + (st + 1) * 128],
                    rhs=sdw_s[:, hc * 1024 + 512: (hc + 1) * 1024],
                    start=(hc == 0), stop=(hc == 3)),
                    inc="P" if hc == 3 else None)
            ev_dn = ("P", S.cnt["P"])
            ev_mo = S.emit("v", lambda h, st=st, pp=pp: h.tensor_tensor(
                out=moe[:, st * 1024:(st + 1) * 1024],
                in0=moe[:, st * 1024:(st + 1) * 1024], in1=pp[:, :], op=ALU.add),
                waits=[ev_dn, ev_moe[st]], inc="V")
            ev_moe[st] = ev_mo
            pfree[r0] = pfree[r1] = ev_mo
        for st in range(8):
            j = G * 8 + st
            pb = st % 2
            if DEBUG_FFN_RAW:
                out_free[pb] = S.dma("s", out_d[j * 128:(j + 1) * 128, :],
                                     moe[:, st * 1024:(st + 1) * 1024],
                                     waits=[ev_moe[st], out_free[pb]], sem="O")
                continue
            ev_x1l = S.dma("a", x1e[pb][:, :], x1_d[j], waits=[x1e_free[pb]],
                           sem="xq%d" % pb)
            ev_ln = emit_ln(moe, st * 1024, tmp_f[:, :], ev_moe[st])
            ev_moe[st] = ev_ln
            S.drain("v")
            ev_o = S.emit("v", lambda h, pb=pb: h.tensor_tensor(
                out=out_f[pb][:, :], in0=tmp_f[:, :], in1=x1e[pb][:, :], op=ALU.add),
                waits=[ev_x1l, out_free[pb]], inc="V")
            x1e_free[pb] = ev_o
            out_free[pb] = S.dma("s", out_d[j * 128:(j + 1) * 128, :], out_f[pb][:, :],
                                 waits=[ev_o], sem="O")
    S.emit("s", lambda h: None, waits=[("O", S.cnt["O"])])

    c_ctx.close()
    with octx, nc.Block() as block:
        S.replay(block)
    return nc


_NC_CACHE = {}


def _chunked(w, cols):
    return np.ascontiguousarray(
        w.reshape(8, 128, cols).transpose(1, 0, 2).reshape(128, 8 * cols))


def kernel(L0_attn_x, L0_attn_Wq, L0_attn_Aq, L0_attn_Bq, L0_attn_Wk, L0_attn_Ak,
           L0_attn_Bk, L0_attn_Wv, L0_attn_Av, L0_attn_Bv, L0_attn_Wp,
           L0_ffn_router_w, L0_ffn_router_bias, L0_ffn_gate_w, L0_ffn_up_w,
           L0_ffn_down_w, L0_ffn_sgate_w, L0_ffn_sup_w, L0_ffn_sdown_w):
    f = lambda a: np.ascontiguousarray(np.asarray(a, dtype=np.float32))
    X = f(L0_attn_x)
    wq = _chunked(f(L0_attn_Wq).T, 1024)
    wk = _chunked(f(L0_attn_Wk).T, 1024)
    wv = _chunked(f(L0_attn_Wv).T, 1024)
    wp = _chunked(f(L0_attn_Wp).T, 1024)
    Mkv = np.zeros((1024, 128), np.float32)
    Mkv[:, 32:40] = f(L0_attn_Bk).T
    Mkv[:, 64:72] = f(L0_attn_Bv).T
    bqkv = _chunked(Mkv, 128)
    bq = _chunked(f(L0_attn_Bq).T, 8)
    aT = np.zeros((128, 1024), np.float32)
    aT[0:8] = f(L0_attn_Aq).T
    aT[32:40] = f(L0_attn_Ak).T
    aT[64:72] = f(L0_attn_Av).T
    rw = _chunked(f(L0_ffn_router_w).T, 32)
    rb = f(L0_ffn_router_bias).reshape(1, 32)
    gw = np.ascontiguousarray(
        f(L0_ffn_gate_w).reshape(32, 128, 8, 128).transpose(0, 3, 2, 1).reshape(32, 128, 1024))
    uw = np.ascontiguousarray(
        f(L0_ffn_up_w).reshape(32, 128, 8, 128).transpose(0, 3, 2, 1).reshape(32, 128, 1024))
    dw = np.ascontiguousarray(f(L0_ffn_down_w).transpose(0, 2, 1))
    sg = _chunked(f(L0_ffn_sgate_w).T, 512)
    su = _chunked(f(L0_ffn_sup_w).T, 512)
    sd = np.ascontiguousarray(
        f(L0_ffn_sdown_w).T.reshape(4, 128, 1024).transpose(1, 0, 2).reshape(128, 4096))
    ident = np.eye(128, dtype=np.float32)
    ones = np.ones((128, 128), np.float32)
    scal = np.zeros((128, 4), np.float32)
    scal[:, 0] = EPS
    scal[:, 1] = 2.0
    scal[:, 2] = 0.125
    scal[:, 3] = -1.0
    ishift = np.zeros((64, 256), np.float32)
    for d in range(64):
        ishift[d, d] = 1.0
        ishift[d, 128 + 64 + d] = 1.0
    slopes = np.array([2.0 ** (-(i + 1) / 2.0) for i in range(H)], np.float64)
    biastab = np.zeros((128, 512), np.float32)
    p = np.arange(128, dtype=np.float64)
    for hh in range(H):
        for kt in range(32):
            biastab[:, hh * 32 + kt] = (-slopes[hh] * (kt * 128 + p)).astype(np.float32)
    tri = np.where(np.arange(128)[:, None] > np.arange(128)[None, :],
                   NEGBIG, 0.0).astype(np.float32)
    full = np.full((128, 128), NEGBIG, np.float32)
    zero = np.zeros((128, 128), np.float32)
    maskr = [np.concatenate([tri, full], axis=1), np.concatenate([zero, tri], axis=1)]

    if "nc" not in _NC_CACHE:
        _NC_CACHE["nc"] = build_program()
    nc = _NC_CACHE["nc"]

    in_maps = []
    for c in range(8):
        b, r = c // 2, c % 2
        xs = np.ascontiguousarray(X[b])
        own_rows = np.concatenate(
            [np.arange((2 * j + r) * 128, (2 * j + r + 1) * 128) for j in range(NT)])
        xo = np.ascontiguousarray(xs[own_rows])
        in_maps.append(dict(
            xseq=xs, xown=xo, maskp=maskr[r], biastab=biastab, identc=ident,
            ishift=ishift, onesc=ones, scal=scal, wq=wq, wk=wk, wv=wv, wp=wp, bqkv=bqkv,
            bq=bq, at=aT, rw=rw, rb=rb, gw=gw, uw=uw, dw=dw, sg=sg, su=su, sd=sd))
    res = run_bass_kernel_spmd(nc, in_maps, core_ids=list(range(8)))
    out = np.empty((B, T, E), np.float32)
    for c in range(8):
        b, r = c // 2, c % 2
        oc = res.results[c]["out"]
        for j in range(NT):
            out[b, (2 * j + r) * 128:(2 * j + r + 1) * 128, :] = oc[j * 128:(j + 1) * 128, :]
    return out


# version 06
# speedup vs baseline: 1.6860x; 1.0822x over previous
import dataclasses
import math
from contextlib import ExitStack

import numpy as np
import concourse.bass as bass
import concourse.mybir as mybir
from concourse.bass_utils import run_bass_kernel_spmd

F32 = mybir.dt.float32
BF16 = mybir.dt.bfloat16
AF = mybir.ActivationFunctionType
ALU = mybir.AluOpType
AX = mybir.AxisListType

B, T, E, H, DH = 4, 4096, 1024, 16, 64
NEXP, HS = 32, 512
NT, NG, SGRP = 16, 4, 8
EPS = 1e-5
NEGBIG = -1e30
DEBUG_STOP_AT_X1 = False
DEBUG_FFN_RAW = False
DEBUG_FFN_W = False
DEBUG_NBLK = 4
DEBUG_NO_SHARED = False


def mkap(handle, off, pattern):
    a = handle[:, :] if len(handle.shape) == 2 else handle[:]
    return dataclasses.replace(a, offset=a.offset + off, ap=pattern)


class Sched:
    ENGS = ("s", "g", "t", "a", "v")

    def __init__(self, nc):
        self.nc = nc
        self.q = {e: [] for e in self.ENGS}
        self.sems = {}
        self.cnt = {}
        self.floor = {}

    def sem(self, ctx, name):
        self.sems[name] = ctx.enter_context(self.nc.semaphore(name))
        self.cnt[name] = 0

    def set_floor(self):
        self.floor = dict(self.cnt)

    def emit(self, eng, fn, waits=(), inc=None, amt=1):
        ws = dict(self.floor)
        for w in waits:
            if w is None:
                continue
            sname, val = w
            assert val <= self.cnt[sname], (sname, val, self.cnt[sname])
            if ws.get(sname, 0) < val:
                ws[sname] = val
        evt = None
        if inc is not None:
            self.cnt[inc] += amt
            evt = (inc, self.cnt[inc])
        self.q[eng].append((fn, tuple(ws.items()), (inc, amt) if inc else None))
        return evt

    def dma(self, eng, out, in_, waits=(), sem="dS"):
        return self.emit(eng, lambda h, o=out, i=in_: h.dma_start(out=o, in_=i),
                         waits=waits, inc=sem, amt=16)

    def drain(self, eng):
        self.emit(eng, lambda h: h.drain())

    def replay(self, block):
        decos = {"s": block.sync, "g": block.gpsimd, "t": block.tensor,
                 "a": block.scalar, "v": block.vector}
        for ek in self.ENGS:
            ops = self.q[ek]
            if not ops:
                continue
            def body(h, ops=ops):
                hwm = {}
                for fn, waits, inc in ops:
                    for sname, val in waits:
                        if hwm.get(sname, 0) < val:
                            h.wait_ge(self.sems[sname], val)
                            hwm[sname] = val
                    ins = fn(h)
                    if inc is not None and ins is not None:
                        ins.then_inc(self.sems[inc[0]], inc[1])
            decos[ek](body)


def build_program():
    nc = bass.Bass("TRN2", target_bir_lowering=False)
    dt = nc.dram_tensor
    xseq = dt("xseq", [T, E], F32, kind="ExternalInput").ap()
    xown = dt("xown", [NT * 128, E], F32, kind="ExternalInput").ap()
    maskp_d = dt("maskp", [128, 256], F32, kind="ExternalInput").ap()
    bias_d = dt("biastab", [128, 512], F32, kind="ExternalInput").ap()
    ident_d = dt("identc", [128, 128], F32, kind="ExternalInput").ap()
    ishift_d = dt("ishift", [64, 256], F32, kind="ExternalInput").ap()
    ones_d = dt("onesc", [128, 128], F32, kind="ExternalInput").ap()
    scal_d = dt("scal", [128, 8], F32, kind="ExternalInput").ap()
    wq_h = dt("wq", [128, 8 * 1024], F32, kind="ExternalInput").ap()
    wk_h = dt("wk", [128, 8 * 1024], F32, kind="ExternalInput").ap()
    wv_h = dt("wv", [128, 8 * 1024], F32, kind="ExternalInput").ap()
    wp_h = dt("wp", [128, 8 * 1024], F32, kind="ExternalInput").ap()
    bqkv_h = dt("bqkv", [128, 8 * 128], F32, kind="ExternalInput").ap()
    bq_h = dt("bq", [128, 8 * 8], F32, kind="ExternalInput").ap()
    at_h = dt("at", [128, 1024], F32, kind="ExternalInput").ap()
    rw_h = dt("rw", [128, 8 * 32], F32, kind="ExternalInput").ap()
    rb_h = dt("rb", [1, 32], F32, kind="ExternalInput").ap()
    gw_h = dt("gw", [NEXP, 128, 1024], F32, kind="ExternalInput").ap()
    uw_h = dt("uw", [NEXP, 128, 1024], F32, kind="ExternalInput").ap()
    dw_h = dt("dw", [NEXP, 128, 1024], F32, kind="ExternalInput").ap()
    sg_h = dt("sg", [128, 8 * 512], F32, kind="ExternalInput").ap()
    su_h = dt("su", [128, 8 * 512], F32, kind="ExternalInput").ap()
    sd_h = dt("sd", [128, 4 * 1024], F32, kind="ExternalInput").ap()
    out_d = dt("out", [NT * 128, E], F32, kind="ExternalOutput").ap()
    l1s_d = dt("l1scr", [SGRP, 128, 4096], BF16, kind="Internal").ap()
    l1o_d = dt("l1oscr", [NG, 128, 4096], BF16, kind="Internal").ap()
    x1_d = dt("x1scr", [NT, 128, 1024], F32, kind="Internal").ap()
    wg_d = dt("wgscr", [2, 32, 1024], BF16, kind="Internal").ap()

    octx = ExitStack()
    S = Sched(nc)
    for s in ("dS", "dC", "dC2", "dW0", "dW1", "dP", "dF", "g0", "g1",
              "u0", "u1", "wr0", "wr1", "l10", "l11", "xq0", "xq1",
              "P", "A", "V", "O", "oc",
              "d0", "d1", "d2", "d3", "d4", "d5", "d6", "d7"):
        S.sem(octx, s)

    def SB(ctx, n, shp, d=F32):
        return ctx.enter_context(nc.sbuf_tensor(n, shp, d))

    ident_b = SB(octx, "ident_b", [128, 128], BF16)
    ishift_b = SB(octx, "ishift_b", [64, 256], BF16)
    ones_f = SB(octx, "ones_f", [128, 128], F32)
    ones_b = SB(octx, "ones_b", [128, 128], BF16)
    maskp_s = SB(octx, "maskp_s", [128, 256], F32)
    bias_s = SB(octx, "bias_s", [128, 512], F32)
    rb_s = SB(octx, "rb_s", [1, 32], F32)
    scal_s = SB(octx, "scal_s", [128, 8], F32)
    at_b = SB(octx, "at_b", [128, 1024], BF16)
    bqkv_b = SB(octx, "bqkv_b", [128, 8 * 128], BF16)
    bq_b = SB(octx, "bq_b", [128, 8 * 8], BF16)
    xt = [SB(octx, "xt0", [128, 1024]), SB(octx, "xt1", [128, 1024])]
    lnb = [SB(octx, "lnb0", [128, 1024], BF16), SB(octx, "lnb1", [128, 1024], BF16)]
    tmp_f = SB(octx, "tmpf", [128, 1024])
    lnscr = [dict(bn6=SB(octx, f"bn6_{i}", [128, 12]), mv=SB(octx, f"mv_{i}", [128, 2]),
                  sd=SB(octx, f"sd_{i}", [128, 1]), rs=SB(octx, f"rs_{i}", [128, 1]),
                  busy_v=None, busy_a=None) for i in range(2)]
    pA = octx.enter_context(nc.psum_tensor("pA", [128, 1024], F32))
    pB = octx.enter_context(nc.psum_tensor("pB", [128, 1024], F32))
    pC = octx.enter_context(nc.psum_tensor("pC", [128, 1024], F32))
    pT = octx.enter_context(nc.psum_tensor("pT", [128, 2048], BF16))

    ev_c = {}
    for key, dst, src in [("ident", ident_b, ident_d), ("ishift", ishift_b, ishift_d),
                          ("onesb", ones_b, ones_d), ("at", at_b, at_h),
                          ("bqkv", bqkv_b, bqkv_h), ("bq", bq_b, bq_h)]:
        ev_c[key] = S.dma("g", dst[:, :], src, sem="dC")
    for key, dst, src in [("ones", ones_f, ones_d), ("maskp", maskp_s, maskp_d),
                          ("bias", bias_s, bias_d), ("rb", rb_s, rb_h),
                          ("scal", scal_s, scal_d)]:
        ev_c[key] = S.dma("s", dst[:, :], src, sem="dC2")

    xt_free = [None, None]
    lnb_free = [None, None]
    pT_free = [None, None]
    ln_i = [0]

    def emit_ln(x_tensor, x_off, out_ap, x_evt, out_wait=()):
        x_ap = x_tensor[:, x_off:x_off + 1024]
        sc = lnscr[ln_i[0] % 2]
        ln_i[0] += 1
        S.emit("v", lambda h, a=sc["bn6"], b=x_tensor, o=x_off: h.bn_stats(
            a[:, 0:6], b[:, o:o + 512]),
            waits=[x_evt, sc["busy_v"], sc["busy_a"]])
        S.emit("v", lambda h, a=sc["bn6"], b=x_tensor, o=x_off: h.bn_stats(
            a[:, 6:12], b[:, o + 512:o + 1024]))
        S.drain("v")
        ev_mv = S.emit("v", lambda h, a=sc["mv"], b=sc["bn6"]: h.bn_aggr(a[:, :], b[:, :]), inc="V")
        S.emit("a", lambda h, a=sc["sd"], b=sc["mv"]: h.activation(
            out=a[:, :], in_=b[:, 1:2], func=AF.Ln, bias=scal_s[:, 0:1]),
            waits=[ev_mv, ev_c["scal"]])
        S.drain("a")
        ev_sd = S.emit("a", lambda h, a=sc["rs"], b=sc["sd"]: h.activation(
            out=a[:, :], in_=b[:, :], func=AF.Exp, scale=scal_s[:, 4:5]), inc="A")
        ev = S.emit("v", lambda h, o=out_ap, x=x_ap, m=sc["mv"], r=sc["rs"]: h.tensor_scalar(
            out=o, in0=x, scalar1=m[:, 0:1], scalar2=r[:, 0:1],
            op0=ALU.subtract, op1=ALU.mult), waits=list(out_wait) + [ev_sd], inc="V")
        sc["busy_v"] = ev
        sc["busy_a"] = ev_sd
        return ev

    def emit_ln_transpose(src_dram_row, dst_tensor, dst_off):
        i = ln_i[0] % 2
        ev_x = S.dma("s", xt[i][:, :], src_dram_row, waits=[xt_free[i]],
                     sem="xq%d" % i)
        ev_ln = emit_ln(xt[i], 0, lnb[i][:, :], ev_x, out_wait=[lnb_free[i]])
        xt_free[i] = ev_ln
        th = ln_i[0] % 2
        ev_tr = None
        for c in range(8):
            w = [ev_ln, ev_c["ident"]] + ([pT_free[th]] if c == 0 else [])
            ev_tr = S.emit("t", lambda h, c=c, i=i, th=th: h.transpose(
                pT[:, th * 1024 + c * 128: th * 1024 + (c + 1) * 128],
                in_=lnb[i][:, c * 128:(c + 1) * 128], identity=ident_b[:, :]),
                waits=w, inc="P")
        lnb_free[i] = ev_tr
        ev_cp = S.emit("a", lambda h, th=th, dst=mkap(dst_tensor, dst_off,
                       [[dst_tensor.shape[1], 128],
                        [dst_tensor.shape[1] // 8, 8], [1, 128]]): h.activation(
            out=dst, in_=pT[:, th * 1024:(th + 1) * 1024].rearrange("p (c n) -> p c n", n=128),
            func=AF.Copy), waits=[ev_tr], inc="A")
        pT_free[th] = ev_cp
        return ev_cp

    ab_ctx = ExitStack()
    O_sb = SB(ab_ctx, "O_sb", [128, 8 * 2048], BF16)
    a_ctx = ExitStack()
    t1kv = SB(a_ctx, "t1kv", [128, SGRP * 512], BF16)
    t1q = SB(a_ctx, "t1q", [8, NG * 512], BF16)
    l1t = [SB(a_ctx, "l1t0", [128, 4096], BF16), SB(a_ctx, "l1t1", [128, 4096], BF16)]
    wq_s = SB(a_ctx, "wq_s", [128, 8 * 512], BF16)
    wk_s = SB(a_ctx, "wk_s", [128, 8 * 512], BF16)
    wv_s = SB(a_ctx, "wv_s", [128, 8 * 512], BF16)
    K_sb = SB(a_ctx, "K_sb", [128, 4 * 4096], BF16)
    V_sb = SB(a_ctx, "V_sb", [128, 8 * 32 * 66], BF16)
    Q_sb = SB(a_ctx, "Q_sb", [128, 4 * 2048], BF16)
    Pt = [SB(a_ctx, "Pt0", [128, 512], BF16), SB(a_ctx, "Pt1", [128, 512], BF16),
          SB(a_ctx, "Pt2", [128, 512], BF16)]
    rec_s = SB(a_ctx, "rec_s", [128, 512], BF16)
    lnd_s = SB(a_ctx, "lnd_s", [128, 512])
    bc_s = SB(a_ctx, "bc_s", [64, 512])
    On_s = SB(a_ctx, "On_s", [64, 512], BF16)

    l1t_free = [None, None]
    ev_t1_seq = [None] * SGRP
    ev_t1_own = [None] * NG
    pfree = {k: None for k in ("A0", "A1", "B0", "B1", "C0", "C1")}
    PREG = {"A0": (pA, 0), "A1": (pA, 512), "B0": (pB, 0), "B1": (pB, 512),
            "C0": (pC, 0), "C1": (pC, 512)}

    ev_vones = S.emit("v", lambda h: h.memset(
        V_sb[:, :].rearrange("p (s n) -> p s n", n=66)[:, :, 64:66], 1.0), inc="V")

    def emit_weight_half(hh, extra_wait):
        ev = None
        for dst, src in ((wq_s, wq_h), (wk_s, wk_h), (wv_s, wv_h)):
            for c in range(8):
                ev = S.dma("g", dst[:, c * 512:(c + 1) * 512],
                           src[:, c * 1024 + hh * 512: c * 1024 + hh * 512 + 512],
                           waits=extra_wait, sem="dW%d" % hh)
        return ev

    def emit_qkv_half(hh, ev_w, first):
        kv_last, q_last = None, None
        acc_regs = ["A0", "A1", "B0", "B1"]
        for g in range(SGRP):
            gb = g % 2
            if first:
                subev = []
                for s in range(4):
                    subev.append(emit_ln_transpose(
                        xseq[(g * 4 + s) * 128:(g * 4 + s + 1) * 128, :],
                        l1t[gb], s * 128))
                ev_l1 = subev[-1]
                S.dma("g", l1s_d[g], l1t[gb][:, :], waits=subev)
                for ic in range(8):
                    w = [ev_l1, ev_c["bqkv"]] + ([pfree["C0"]] if ic == 0 else [])
                    S.emit("t", lambda h, ic=ic, gb=gb: h.matmul(
                        pC[:, 0:512], lhsT=bqkv_b[:, ic * 128:(ic + 1) * 128],
                        rhs=l1t[gb][:, ic * 512:(ic + 1) * 512],
                        start=(ic == 0), stop=(ic == 7)), waits=w,
                        inc="P" if ic == 7 else None)
                ev_t1p = ("P", S.cnt["P"])
                S.emit("a", lambda h, g=g: h.activation(
                    out=t1kv[32:40, g * 512:(g + 1) * 512], in_=pC[32:40, 0:512],
                    func=AF.Copy, scale=scal_s[32:40, 1:2]),
                    waits=[ev_t1p, ev_c["scal"]], inc="A")
                ev_t1_seq[g] = S.emit("a", lambda h, g=g: h.activation(
                    out=t1kv[64:72, g * 512:(g + 1) * 512], in_=pC[64:72, 0:512],
                    func=AF.Copy, scale=scal_s[64:72, 1:2]), inc="A")
                pfree["C0"] = ev_t1_seq[g]
            else:
                ev_l1 = S.dma("s", l1t[gb][:, :], l1s_d[g], waits=[l1t_free[gb]],
                              sem="l1%d" % gb)
            for c in range(4):
                pp, off = PREG[acc_regs[c]]
                for ic in range(8):
                    w = [ev_l1, ev_w] + ([pfree[acc_regs[c]]] if ic == 0 else [])
                    S.emit("t", lambda h, c=c, ic=ic, gb=gb, pp=pp, off=off: h.matmul(
                        pp[:, off:off + 512],
                        lhsT=wk_s[:, ic * 512 + c * 128: ic * 512 + (c + 1) * 128],
                        rhs=l1t[gb][:, ic * 512:(ic + 1) * 512],
                        start=(ic == 0), stop=False), waits=w)
                S.emit("t", lambda h, c=c, g=g, hh=hh, pp=pp, off=off: h.matmul(
                    pp[:, off:off + 512],
                    lhsT=at_b[32:40, hh * 512 + c * 128: hh * 512 + (c + 1) * 128],
                    rhs=t1kv[32:40, g * 512:(g + 1) * 512],
                    start=False, stop=True),
                    waits=[ev_t1_seq[g], ev_c["at"]], inc="P")
                ev_kp = ("P", S.cnt["P"])
                kv_last = S.emit("a", lambda h, c=c, g=g, pp=pp, off=off: h.activation(
                    out=K_sb[:, c * 4096 + g * 512: c * 4096 + (g + 1) * 512],
                    in_=pp[:, off:off + 512], func=AF.Copy), waits=[ev_kp], inc="A")
                pfree[acc_regs[c]] = kv_last
            for s in range(4):
                kt = g * 4 + s
                vr = "C0" if s % 2 == 0 else "C1"
                pp, off = PREG[vr]
                for ic in range(8):
                    w = [ev_l1, ev_w] + ([pfree[vr]] if ic == 0 else [])
                    S.emit("t", lambda h, s=s, ic=ic, gb=gb, pp=pp, off=off: h.matmul(
                        pp[:, off:off + 512],
                        lhsT=l1t[gb][:, ic * 512 + s * 128: ic * 512 + (s + 1) * 128],
                        rhs=wv_s[:, ic * 512:(ic + 1) * 512],
                        start=(ic == 0), stop=False), waits=w)
                S.emit("t", lambda h, g=g, s=s, hh=hh, pp=pp, off=off: h.matmul(
                    pp[:, off:off + 512],
                    lhsT=t1kv[64:72, (g * 4 + s) * 128:(g * 4 + s + 1) * 128],
                    rhs=at_b[64:72, hh * 512:(hh + 1) * 512],
                    start=False, stop=True),
                    waits=[ev_t1_seq[g], ev_c["at"]], inc="P")
                ev_vp = ("P", S.cnt["P"])
                dst = mkap(V_sb, kt * 66, [[8 * 32 * 66, 128], [32 * 66, 8], [1, 64]])
                kv_last = S.emit("a", lambda h, pp=pp, off=off, dst=dst: h.activation(
                    out=dst, in_=pp[:, off:off + 512].rearrange("p (c n) -> p c n", n=64),
                    func=AF.Copy), waits=[ev_vp, ev_vones], inc="A")
                pfree[vr] = kv_last
            l1t_free[gb] = ("P", S.cnt["P"])
        for og in range(NG):
            gb = og % 2
            if first:
                subev = []
                for s in range(4):
                    subev.append(emit_ln_transpose(
                        xown[(og * 4 + s) * 128:(og * 4 + s + 1) * 128, :],
                        l1t[gb], s * 128))
                ev_l1 = subev[-1]
                S.dma("g", l1o_d[og], l1t[gb][:, :], waits=subev)
                for ic in range(8):
                    w = [ev_l1, ev_c["bq"]] + ([pfree["C0"]] if ic == 0 else [])
                    S.emit("t", lambda h, ic=ic, gb=gb: h.matmul(
                        pC[0:8, 0:512], lhsT=bq_b[:, ic * 8:(ic + 1) * 8],
                        rhs=l1t[gb][:, ic * 512:(ic + 1) * 512],
                        start=(ic == 0), stop=(ic == 7)), waits=w,
                        inc="P" if ic == 7 else None)
                ev_qt1p = ("P", S.cnt["P"])
                ev_t1_own[og] = S.emit("a", lambda h, og=og: h.activation(
                    out=t1q[0:8, og * 512:(og + 1) * 512], in_=pC[0:8, 0:512],
                    func=AF.Copy, scale=scal_s[0:8, 1:2]),
                    waits=[ev_qt1p, ev_c["scal"]], inc="A")
                pfree["C0"] = ev_t1_own[og]
            else:
                ev_l1 = S.dma("s", l1t[gb][:, :], l1o_d[og], waits=[l1t_free[gb]],
                              sem="l1%d" % gb)
            for c in range(4):
                pp, off = PREG[acc_regs[c]]
                for ic in range(8):
                    w = [ev_l1, ev_w] + ([pfree[acc_regs[c]]] if ic == 0 else [])
                    S.emit("t", lambda h, c=c, ic=ic, gb=gb, pp=pp, off=off: h.matmul(
                        pp[:, off:off + 512],
                        lhsT=wq_s[:, ic * 512 + c * 128: ic * 512 + (c + 1) * 128],
                        rhs=l1t[gb][:, ic * 512:(ic + 1) * 512],
                        start=(ic == 0), stop=False), waits=w)
                S.emit("t", lambda h, c=c, og=og, hh=hh, pp=pp, off=off: h.matmul(
                    pp[:, off:off + 512],
                    lhsT=at_b[0:8, hh * 512 + c * 128: hh * 512 + (c + 1) * 128],
                    rhs=t1q[0:8, og * 512:(og + 1) * 512],
                    start=False, stop=True),
                    waits=[ev_t1_own[og], ev_c["at"]], inc="P")
                ev_qp = ("P", S.cnt["P"])
                q_last = S.emit("a", lambda h, c=c, og=og, pp=pp, off=off: h.activation(
                    out=Q_sb[:, c * 2048 + og * 512: c * 2048 + (og + 1) * 512],
                    in_=pp[:, off:off + 512], func=AF.Copy), waits=[ev_qp], inc="A")
                pfree[acc_regs[c]] = q_last
            l1t_free[gb] = ("P", S.cnt["P"])
        return kv_last, q_last

    def emit_attn_half(hh, ev_ready):
        Pt_free = [None, None, None]
        SREG = [("C0", pC, 0), ("C1", pC, 512), ("B0", pB, 0)]
        state = dict(rec_free=None, bc_free=None, On_free=None, chunk_free=None)

        def emit_norm(gq, hl, ev_pv):
            cl, hp = hl // 2, hl % 2
            o_reg = "A0" if hp == 0 else "A1"
            ocol = (hh * 4 + cl) * 2048 + gq * 512
            ev_ld = S.emit("a", lambda h, hp=hp: h.activation(
                out=lnd_s[64:65, :], in_=pA[64:65, hp * 512:(hp + 1) * 512],
                func=AF.Ln), waits=[ev_pv, state["rec_free"]], inc="A")
            ev_rec = S.emit("a", lambda h: h.activation(
                out=rec_s[64:65, :], in_=lnd_s[64:65, :], func=AF.Exp,
                scale=scal_s[64:65, 3:4]), waits=[ev_c["scal"]], inc="A")
            ev_bc = S.emit("t", lambda h: h.matmul(
                pB[:, 512:1024], lhsT=ones_b[64:65, 0:128], rhs=rec_s[64:65, 0:512],
                start=True, stop=True),
                waits=[ev_rec, ev_c["onesb"], pfree["B1"]], inc="P")
            state["rec_free"] = ev_bc
            ev_bce = S.emit("v", lambda h: h.tensor_copy(
                bc_s[:, :], pB[0:64, 512:1024]),
                waits=[ev_bc, state["bc_free"]], inc="V")
            pfree["B1"] = ev_bce
            if hp == 0:
                ev_on = S.emit("v", lambda h, ocol=ocol: h.tensor_tensor(
                    out=O_sb[0:64, ocol:ocol + 512], in0=pA[0:64, 0:512],
                    in1=bc_s[:, :], op=ALU.mult), waits=[ev_bce], inc="V")
                state["bc_free"] = ev_on
                pfree["A0"] = ev_on
            else:
                ev_on = S.emit("v", lambda h: h.tensor_tensor(
                    out=On_s[:, :], in0=pA[0:64, 512:1024],
                    in1=bc_s[:, :], op=ALU.mult),
                    waits=[ev_bce, state["On_free"]], inc="V")
                state["bc_free"] = ev_on
                pfree["A1"] = ev_on
                state["On_free"] = S.dma(
                    "s", O_sb[64:128, ocol:ocol + 512], On_s[0:64, :],
                    waits=[ev_on], sem="oc")
            return state["On_free"]

        pending = None
        ev_oc = None
        for gq in range(NG):
            for hl in range(8):
                h_g = hh * 8 + hl
                cl, hp = hl // 2, hl % 2
                o_reg = "A0" if hp == 0 else "A1"
                nkt = 8 * gq + 8
                ev_pv = None
                def emit_s(kt):
                    cs = 128 * max(0, kt // 2 - 4 * gq)
                    sr, sp, soff = SREG[kt % 3]
                    w = [ev_ready, pfree[sr]]
                    ev_s = S.emit("t", lambda h, hp=hp, cl=cl, kt=kt, gq=gq, cs=cs, sp=sp, soff=soff: h.matmul(
                        sp[:, soff + cs: soff + 512],
                        lhsT=K_sb[hp * 64:(hp + 1) * 64,
                                  cl * 4096 + kt * 128: cl * 4096 + (kt + 1) * 128],
                        rhs=Q_sb[hp * 64:(hp + 1) * 64,
                                 cl * 2048 + gq * 512 + cs: cl * 2048 + (gq + 1) * 512],
                        start=True, stop=True), waits=w, inc="P")
                    ev_m = None
                    if kt >= 8 * gq:
                        jm = kt // 2 - 4 * gq
                        ev_m = S.emit("v", lambda h, sp=sp, soff=soff, jm=jm, kt=kt: h.tensor_tensor(
                            out=sp[:, soff + jm * 128: soff + (jm + 1) * 128],
                            in0=sp[:, soff + jm * 128: soff + (jm + 1) * 128],
                            in1=maskp_s[:, (kt % 2) * 128:(kt % 2 + 1) * 128],
                            op=ALU.add), waits=[ev_s, ev_c["maskp"]], inc="V")
                    ev_e = S.emit("a", lambda h, kt=kt, cs=cs, h_g=h_g, sp=sp, soff=soff: h.activation(
                        out=Pt[kt % 3][:, cs:512], in_=sp[:, soff + cs: soff + 512],
                        func=AF.Exp, bias=bias_s[:, h_g * 32 + kt: h_g * 32 + kt + 1],
                        scale=scal_s[:, 2:3]),
                        waits=[ev_s, ev_m, ev_c["bias"], ev_c["scal"], Pt_free[kt % 3]], inc="A")
                    pfree[sr] = ev_e
                    return ev_e

                def emit_pv(kt, ev_e):
                    cs = 128 * max(0, kt // 2 - 4 * gq)
                    w2 = [ev_e] + ([pfree[o_reg]] if kt == 0 else [])
                    ev_pv = S.emit("t", lambda h, hl=hl, kt=kt, cs=cs, hp=hp, nkt=nkt: h.matmul(
                        pA[0:65, hp * 512 + cs:(hp + 1) * 512],
                        lhsT=V_sb[:, (hl * 32 + kt) * 66: (hl * 32 + kt) * 66 + 65],
                        rhs=Pt[kt % 3][:, cs:512],
                        start=(kt == 0), stop=(kt == nkt - 1)), waits=w2, inc="P")
                    Pt_free[kt % 3] = ev_pv
                    return ev_pv

                LAG = 2
                ev_es = {}
                ev_pv = None
                for kt in range(nkt + LAG):
                    if kt < nkt:
                        ev_es[kt] = emit_s(kt)
                    if kt - LAG >= 0:
                        ev_pv = emit_pv(kt - LAG, ev_es.pop(kt - LAG))
                if pending is not None:
                    ev_oc = emit_norm(*pending) or ev_oc
                pending = (gq, hl, ev_pv)
        ev_oc = emit_norm(*pending) or ev_oc
        return [ev_oc, ("V", S.cnt["V"])]

    ev_w0 = emit_weight_half(0, [])
    kv0, q0 = emit_qkv_half(0, ev_w0, first=True)
    ev_oc0 = emit_attn_half(0, q0)
    ev_w1 = emit_weight_half(1, [("P", S.cnt["P"])])
    kv1, q1 = emit_qkv_half(1, ev_w1, first=False)
    ev_oc1 = emit_attn_half(1, q1)

    a_ctx.close()
    S.set_floor()
    b_ctx = ExitStack()
    wp_s = SB(b_ctx, "wp_s", [128, 8 * 1024], BF16)
    at_f = [SB(b_ctx, "atf0", [128, 1024]), SB(b_ctx, "atf1", [128, 1024])]
    xo_f = [SB(b_ctx, "xof0", [128, 1024]), SB(b_ctx, "xof1", [128, 1024])]
    x1t = [SB(b_ctx, "x1t0", [128, 1024]), SB(b_ctx, "x1t1", [128, 1024])]
    ev_wp = None
    for ic in range(8):
        ev_wp = S.dma("g", wp_s[:, ic * 1024:(ic + 1) * 1024],
                      wp_h[:, ic * 1024:(ic + 1) * 1024], sem="dP")
    pacc_free = [None, None]
    at_free = [None, None]
    xo_free = [None, None]
    x1t_free = [None, None]
    ev_x1 = [None] * NT
    for j in range(NT):
        pb = j % 2
        pp = pA if pb == 0 else pB
        ev_xo = S.dma("s", xo_f[pb][:, :], xown[j * 128:(j + 1) * 128, :],
                      waits=[xo_free[pb]], sem="xq%d" % pb)
        for ic in range(8):
            for hf in range(2):
                w = list(ev_oc1) + [ev_wp] + ([pacc_free[pb]] if (ic == 0 and hf == 0) else [])
                S.emit("t", lambda h, ic=ic, hf=hf, j=j, pp=pp: h.matmul(
                    pp[:, hf * 512:(hf + 1) * 512],
                    lhsT=O_sb[:, ic * 2048 + j * 128:ic * 2048 + (j + 1) * 128],
                    rhs=wp_s[:, ic * 1024 + hf * 512: ic * 1024 + (hf + 1) * 512],
                    start=(ic == 0), stop=(ic == 7)),
                    waits=w, inc="P" if (ic == 7 and hf == 1) else None)
        ev_wpp = ("P", S.cnt["P"])
        ev_ae = S.emit("a", lambda h, pb=pb, pp=pp: h.activation(
            out=at_f[pb][:, :], in_=pp[:, :], func=AF.Copy),
            waits=[ev_wpp, at_free[pb]], inc="A")
        pacc_free[pb] = ev_ae
        ev_ln = emit_ln(at_f[pb], 0, tmp_f[:, :], ev_ae)
        at_free[pb] = ev_ln
        S.drain("v")
        ev_x1t = S.emit("v", lambda h, pb=pb: h.tensor_tensor(
            out=x1t[pb][:, :], in0=tmp_f[:, :], in1=xo_f[pb][:, :], op=ALU.add),
            waits=[ev_xo, x1t_free[pb]], inc="V")
        xo_free[pb] = ev_x1t
        ev_x1[j] = S.dma("g", x1_d[j], x1t[pb][:, :], waits=[ev_x1t])
        x1t_free[pb] = ev_x1[j]

    if DEBUG_STOP_AT_X1:
        for j in range(NT):
            S.dma("s", out_d[j * 128:(j + 1) * 128, :], x1_d[j],
                  waits=[ev_x1[j]], sem="O")
        S.emit("s", lambda h: None, waits=[("O", S.cnt["O"])])
        b_ctx.close(); ab_ctx.close()
        with octx, nc.Block() as block:
            S.replay(block)
        return nc

    b_ctx.close()
    ab_ctx.close()
    S.set_floor()
    c_ctx = ExitStack()
    ln2T = [SB(c_ctx, "ln2T0", [128, 8 * 1024], BF16),
            SB(c_ctx, "ln2T1", [128, 8 * 1024], BF16)]
    rw_s = SB(c_ctx, "rw_s", [128, 8 * 32], BF16)
    sgw_s = SB(c_ctx, "sgw_s", [128, 8 * 512], BF16)
    suw_s = SB(c_ctx, "suw_s", [128, 8 * 512], BF16)
    sdw_s = SB(c_ctx, "sdw_s", [128, 4 * 1024], BF16)
    gw_s = [SB(c_ctx, "gw_s0", [128, 1024], BF16), SB(c_ctx, "gw_s1", [128, 1024], BF16)]
    uw_s = [SB(c_ctx, "uw_s0", [128, 1024], BF16), SB(c_ctx, "uw_s1", [128, 1024], BF16)]
    dw_s = [SB(c_ctx, f"dw_s{i}", [128, 1024], BF16) for i in range(8)]
    sc_f = SB(c_ctx, "sc_f", [128, 32])
    cur_f = SB(c_ctx, "cur_f", [128, 32])
    msk_f = SB(c_ctx, "msk_f", [128, 32])
    m_f = SB(c_ctx, "m_f", [128, 1])
    wu_f = SB(c_ctx, "wu_f", [128, 32])
    w_b = SB(c_ctx, "w_b", [128, 32], BF16)
    wfm = SB(c_ctx, "wfm", [32, 1024], BF16)
    wrow = [SB(c_ctx, "wrow0", [1, 1024], BF16), SB(c_ctx, "wrow1", [1, 1024], BF16)]
    sg_b = SB(c_ctx, "sg_b", [128, 1024], BF16)
    wb_b = SB(c_ctx, "wb_b", [128, 1024], BF16)
    h1_b = SB(c_ctx, "h1_b", [128, 1024], BF16)
    h_blk = SB(c_ctx, "h_blk", [128, 8 * 1024], BF16)
    hsh = SB(c_ctx, "hsh", [128, 4 * 1024], BF16)
    moe = SB(c_ctx, "moe", [128, 8 * 1024])
    out_f = [SB(c_ctx, "outf0", [128, 1024]), SB(c_ctx, "outf1", [128, 1024])]
    wdbg = SB(c_ctx, "wdbg", [128, 32])
    x1e = [SB(c_ctx, "x1e0", [128, 1024]), SB(c_ctx, "x1e1", [128, 1024])]

    ev_rw = S.dma("g", rw_s[:, :], rw_h, sem="dF")
    ev_sg = S.dma("g", sgw_s[:, :], sg_h, sem="dF")
    ev_su = S.dma("g", suw_s[:, :], su_h, sem="dF")
    ev_sd = S.dma("g", sdw_s[:, :], sd_h, sem="dF")

    def l2rhs(G, ic, hf):
        return mkap(ln2T[G], ic * 1024 + hf * 512, [[8192, 128], [1, 512]])

    ev_moe = [None] * 8
    w_b_free = [None]
    gw_free = [None, None]
    uw_free = [None, None]
    dw_free = [None] * 8
    wrow_free = [None, None]
    h_blk_free = [None] * 8
    sg_free, wb_free, h1_free = None, None, None
    x1e_free = [None, None]
    out_free = [None, None]
    topk_free = None
    for G in range(2):
        ev_l2_last = None
        ev_wfm = []
        pfree["C0"] = pfree.get("C0")
        ev_lnst = [None] * 8
        for st in range(8):
            j = G * 8 + st
            pb = st % 2
            ev_x1l = S.dma("s", x1e[pb][:, :], x1_d[j], waits=[ev_x1[j], x1e_free[pb]],
                           sem="xq%d" % pb)
            ev_ln = emit_ln(x1e[pb], 0, lnb[pb][:, :], ev_x1l,
                            out_wait=[lnb_free[pb]])
            x1e_free[pb] = ev_ln
            th = st % 2
            ev_tr = None
            for c in range(8):
                w = [ev_ln] + ([pT_free[th]] if c == 0 else [])
                ev_tr = S.emit("t", lambda h, c=c, pb=pb, th=th: h.transpose(
                    pT[:, th * 1024 + c * 128: th * 1024 + (c + 1) * 128],
                    in_=lnb[pb][:, c * 128:(c + 1) * 128], identity=ident_b[:, :]),
                    waits=w, inc="P")
            lnb_free[pb] = ev_tr
            ev_l2_last = S.emit("a", lambda h, th=th, dst=mkap(
                ln2T[G], st * 128, [[8192, 128], [1024, 8], [1, 128]]): h.activation(
                out=dst, in_=pT[:, th * 1024:(th + 1) * 1024].rearrange("p (c n) -> p c n", n=128),
                func=AF.Copy), waits=[ev_tr], inc="A")
            ev_lnst[st] = ev_l2_last
            pT_free[th] = ev_l2_last
        for st in range(8):
            th = st % 2
            for ic in range(8):
                w = [ev_lnst[st], ev_rw] + ([pfree["C0"]] if ic == 0 else [])
                S.emit("t", lambda h, ic=ic, st=st, G=G: h.matmul(
                    pC[:, 0:32],
                    lhsT=mkap(ln2T[G], ic * 1024 + st * 128, [[8192, 128], [1, 128]]),
                    rhs=rw_s[:, ic * 32:(ic + 1) * 32],
                    start=(ic == 0), stop=False), waits=w)
            S.emit("t", lambda h: h.matmul(
                pC[:, 0:32], lhsT=ones_f[0:1, 0:128], rhs=rb_s[0:1, :],
                start=False, stop=True), waits=[ev_c["rb"], ev_c["ones"]], inc="P")
            ev_rt = ("P", S.cnt["P"])
            ev_nm = S.emit("v", lambda h: h.reduce_max(
                m_f[:, :], pC[:, 0:32], axis=AX.X, negate=True),
                waits=[ev_rt, topk_free], inc="V")
            ev_sc = S.emit("a", lambda h: h.activation(
                out=sc_f[:, :], in_=pC[:, 0:32], func=AF.Exp, bias=m_f[:, 0:1]),
                waits=[ev_nm], inc="A")
            pfree["C0"] = ev_sc
            S.emit("v", lambda h: h.tensor_copy(cur_f[:, :], sc_f[:, :]), waits=[ev_sc])
            for it in range(3):
                S.drain("v")
                S.emit("v", lambda h: h.reduce_max(m_f[:, :], cur_f[:, :], axis=AX.X),
                       waits=[ev_sc])
                S.drain("v")
                S.emit("v", lambda h: h.tensor_scalar(
                    out=msk_f[:, :], in0=cur_f[:, :], scalar1=m_f[:, 0:1],
                    scalar2=1e30, op0=ALU.is_ge, op1=ALU.mult))
                S.drain("v")
                S.emit("v", lambda h: h.tensor_sub(cur_f[:, :], cur_f[:, :], msk_f[:, :]))
            S.drain("v")
            S.emit("v", lambda h: h.reduce_max(m_f[:, :], cur_f[:, :], axis=AX.X))
            S.drain("v")
            S.emit("v", lambda h: h.tensor_scalar(
                out=msk_f[:, :], in0=sc_f[:, :], scalar1=m_f[:, 0:1],
                scalar2=None, op0=ALU.is_ge))
            S.drain("v")
            S.emit("v", lambda h: h.tensor_mul(wu_f[:, :], sc_f[:, :], msk_f[:, :]))
            S.drain("v")
            S.emit("v", lambda h: h.reduce_sum(m_f[:, :], wu_f[:, :], axis=AX.X))
            S.drain("v")
            S.emit("v", lambda h: h.reciprocal(m_f[:, :], m_f[:, :]))
            S.drain("v")
            ev_wb = S.emit("v", lambda h: h.tensor_scalar(
                out=w_b[:, :], in0=wu_f[:, :], scalar1=m_f[:, 0:1], scalar2=None,
                op0=ALU.mult), waits=[w_b_free[0]], inc="V")
            topk_free = ev_wb
            ev_wt = S.emit("t", lambda h, th=th: h.transpose(
                pT[0:32, th * 1024: th * 1024 + 128],
                in_=w_b[:, :], identity=ident_b[:, :]),
                waits=[ev_wb, pT_free[th]], inc="P")
            ev_wf = S.emit("a", lambda h, st=st, th=th: h.activation(
                out=wfm[0:32, st * 128:(st + 1) * 128],
                in_=pT[0:32, th * 1024: th * 1024 + 128],
                func=AF.Copy), waits=[ev_wt], inc="A")
            ev_wfm.append(ev_wf)
            pT_free[th] = ev_wf
        ev_wsc = S.dma("g", wg_d[G], wfm[:, :], waits=ev_wfm)
        if DEBUG_FFN_W:
            continue
        for blk in range(DEBUG_NBLK):
            ev_h_all = None
            for e8 in range(8):
                e = blk * 8 + e8
                eb = e % 2
                ev_gw = S.dma("g", gw_s[eb][:, :], gw_h[e],
                              waits=[gw_free[eb]], sem="g%d" % eb)
                ev_uw = S.dma("g", uw_s[eb][:, :], uw_h[e],
                              waits=[uw_free[eb]], sem="u%d" % eb)
                ev_dwl = S.dma("g", dw_s[e8][:, :], dw_h[e],
                               waits=[dw_free[e8]], sem="d%d" % e8)
                dw_free[e8] = ev_dwl
                ev_wr = S.dma("s", wrow[eb][0:1, :], wg_d[G][e:e + 1, :],
                              waits=[ev_wsc, wrow_free[eb]], sem="wr%d" % eb)
                for ic in range(8):
                    for hf in range(2):
                        w = [ev_gw, ev_l2_last] + (
                            [pfree["A0"], pfree["A1"]] if (ic == 0 and hf == 0) else [])
                        S.emit("t", lambda h, ic=ic, hf=hf, eb=eb, G=G: h.matmul(
                            pA[:, hf * 512:(hf + 1) * 512],
                            lhsT=gw_s[eb][:, ic * 128:(ic + 1) * 128],
                            rhs=l2rhs(G, ic, hf), start=(ic == 0), stop=(ic == 7)),
                            waits=w, inc="P" if (ic == 7 and hf == 1) else None)
                ev_g = ("P", S.cnt["P"])
                gw_free[eb] = ev_g
                for ic in range(8):
                    for hf in range(2):
                        w = [ev_uw, ev_l2_last] + (
                            [pfree["B0"], pfree["B1"]] if (ic == 0 and hf == 0) else [])
                        S.emit("t", lambda h, ic=ic, hf=hf, eb=eb, G=G: h.matmul(
                            pB[:, hf * 512:(hf + 1) * 512],
                            lhsT=uw_s[eb][:, ic * 128:(ic + 1) * 128],
                            rhs=l2rhs(G, ic, hf), start=(ic == 0), stop=(ic == 7)),
                            waits=w, inc="P" if (ic == 7 and hf == 1) else None)
                ev_u = ("P", S.cnt["P"])
                uw_free[eb] = ev_u
                for hf in range(2):
                    w = [ev_wr, ev_c["onesb"]] + (
                        [pfree["C0"], pfree["C1"]] if hf == 0 else [])
                    S.emit("t", lambda h, hf=hf, eb=eb: h.matmul(
                        pC[:, hf * 512:(hf + 1) * 512], lhsT=ones_b[0:1, 0:128],
                        rhs=wrow[eb][0:1, hf * 512:(hf + 1) * 512],
                        start=True, stop=True), waits=w,
                        inc="P" if hf == 1 else None)
                ev_wbp = ("P", S.cnt["P"])
                wrow_free[eb] = ev_wbp
                ev_sgb = S.emit("a", lambda h: h.activation(
                    out=sg_b[:, :], in_=pA[:, :], func=AF.Silu),
                    waits=[ev_g, sg_free], inc="A")
                pfree["A0"] = pfree["A1"] = ev_sgb
                ev_wbe = S.emit("a", lambda h: h.activation(
                    out=wb_b[:, :], in_=pC[:, :], func=AF.Copy),
                    waits=[ev_wbp, wb_free], inc="A")
                pfree["C0"] = pfree["C1"] = ev_wbe
                ev_h1 = S.emit("v", lambda h: h.tensor_tensor(
                    out=h1_b[:, :], in0=sg_b[:, :], in1=pB[:, :], op=ALU.mult),
                    waits=[ev_sgb, ev_u, h1_free], inc="V")
                sg_free = ev_h1
                pfree["B0"] = pfree["B1"] = ev_h1
                ev_h = S.emit("v", lambda h, e8=e8: h.tensor_tensor(
                    out=h_blk[:, e8 * 1024:(e8 + 1) * 1024], in0=h1_b[:, :],
                    in1=wb_b[:, :], op=ALU.mult),
                    waits=[ev_h1, ev_wbe, h_blk_free[e8]], inc="V")
                h1_free = ev_h
                wb_free = ev_h
                ev_h_all = ev_h
            dn_regs = [("C0", "C1", pC), ("A0", "A1", pA)]
            for st in range(8):
                r0, r1, pp = dn_regs[st % 2]
                for e8 in range(8):
                    w = [ev_h_all, dw_free[e8]] + (
                        [pfree[r0], pfree[r1]] if e8 == 0 else [])
                    S.emit("t", lambda h, e8=e8, st=st, pp=pp: h.matmul(
                        pp[:, 0:512],
                        lhsT=h_blk[:, e8 * 1024 + st * 128: e8 * 1024 + (st + 1) * 128],
                        rhs=dw_s[e8][:, 0:512],
                        start=(e8 == 0), stop=False), waits=w)
                    S.emit("t", lambda h, e8=e8, st=st, pp=pp: h.matmul(
                        pp[:, 512:1024],
                        lhsT=h_blk[:, e8 * 1024 + st * 128: e8 * 1024 + (st + 1) * 128],
                        rhs=dw_s[e8][:, 512:1024],
                        start=(e8 == 0), stop=(e8 == 7)),
                        inc="P" if e8 == 7 else None)
                ev_dn = ("P", S.cnt["P"])
                if blk == 0:
                    ev_mo = S.emit("v", lambda h, st=st, pp=pp: h.tensor_copy(
                        moe[:, st * 1024:(st + 1) * 1024], pp[:, :]),
                        waits=[ev_dn, ev_moe[st]], inc="V")
                else:
                    ev_mo = S.emit("v", lambda h, st=st, pp=pp: h.tensor_tensor(
                        out=moe[:, st * 1024:(st + 1) * 1024],
                        in0=moe[:, st * 1024:(st + 1) * 1024], in1=pp[:, :],
                        op=ALU.add), waits=[ev_dn, ev_moe[st]], inc="V")
                ev_moe[st] = ev_mo
                pfree[r0] = pfree[r1] = ev_mo
            ev_dn_done = ("P", S.cnt["P"])
            for e8 in range(8):
                h_blk_free[e8] = ev_dn_done
                dw_free[e8] = ev_dn_done
        for hc in range(0 if DEBUG_NO_SHARED else 4):
            for ic in range(8):
                for hf in range(2):
                    w = [ev_sg, ev_l2_last] + (
                        [pfree["A0"], pfree["A1"]] if (ic == 0 and hf == 0) else [])
                    S.emit("t", lambda h, ic=ic, hf=hf, hc=hc, G=G: h.matmul(
                        pA[:, hf * 512:(hf + 1) * 512],
                        lhsT=sgw_s[:, ic * 512 + hc * 128: ic * 512 + (hc + 1) * 128],
                        rhs=l2rhs(G, ic, hf), start=(ic == 0), stop=(ic == 7)),
                        waits=w, inc="P" if (ic == 7 and hf == 1) else None)
            ev_g = ("P", S.cnt["P"])
            for ic in range(8):
                for hf in range(2):
                    w = [ev_su, ev_l2_last] + (
                        [pfree["B0"], pfree["B1"]] if (ic == 0 and hf == 0) else [])
                    S.emit("t", lambda h, ic=ic, hf=hf, hc=hc, G=G: h.matmul(
                        pB[:, hf * 512:(hf + 1) * 512],
                        lhsT=suw_s[:, ic * 512 + hc * 128: ic * 512 + (hc + 1) * 128],
                        rhs=l2rhs(G, ic, hf), start=(ic == 0), stop=(ic == 7)),
                        waits=w, inc="P" if (ic == 7 and hf == 1) else None)
            ev_u = ("P", S.cnt["P"])
            ev_sgb = S.emit("a", lambda h: h.activation(
                out=sg_b[:, :], in_=pA[:, :], func=AF.Silu),
                waits=[ev_g, sg_free], inc="A")
            pfree["A0"] = pfree["A1"] = ev_sgb
            ev_h = S.emit("v", lambda h, hc=hc: h.tensor_tensor(
                out=hsh[:, hc * 1024:(hc + 1) * 1024], in0=sg_b[:, :], in1=pB[:, :],
                op=ALU.mult), waits=[ev_sgb, ev_u], inc="V")
            sg_free = ev_h
            pfree["B0"] = pfree["B1"] = ev_h
        if DEBUG_NO_SHARED:
            for hc in []: pass
        ev_hsh = ("V", S.cnt["V"])
        dn_regs = [("C0", "C1", pC), ("A0", "A1", pA)]
        for st in range(0 if DEBUG_NO_SHARED else 8):
            r0, r1, pp = dn_regs[st % 2]
            for hc in range(4):
                w = [ev_hsh, ev_sd] + ([pfree[r0], pfree[r1]] if hc == 0 else [])
                S.emit("t", lambda h, hc=hc, st=st, pp=pp: h.matmul(
                    pp[:, 0:512],
                    lhsT=hsh[:, hc * 1024 + st * 128: hc * 1024 + (st + 1) * 128],
                    rhs=sdw_s[:, hc * 1024: hc * 1024 + 512],
                    start=(hc == 0), stop=False), waits=w)
                S.emit("t", lambda h, hc=hc, st=st, pp=pp: h.matmul(
                    pp[:, 512:1024],
                    lhsT=hsh[:, hc * 1024 + st * 128: hc * 1024 + (st + 1) * 128],
                    rhs=sdw_s[:, hc * 1024 + 512: (hc + 1) * 1024],
                    start=(hc == 0), stop=(hc == 3)),
                    inc="P" if hc == 3 else None)
            ev_dn = ("P", S.cnt["P"])
            ev_mo = S.emit("v", lambda h, st=st, pp=pp: h.tensor_tensor(
                out=moe[:, st * 1024:(st + 1) * 1024],
                in0=moe[:, st * 1024:(st + 1) * 1024], in1=pp[:, :], op=ALU.add),
                waits=[ev_dn, ev_moe[st]], inc="V")
            ev_moe[st] = ev_mo
            pfree[r0] = pfree[r1] = ev_mo
        for st in range(8):
            j = G * 8 + st
            pb = st % 2
            if DEBUG_FFN_RAW:
                out_free[pb] = S.dma("s", out_d[j * 128:(j + 1) * 128, :],
                                     moe[:, st * 1024:(st + 1) * 1024],
                                     waits=[ev_moe[st], out_free[pb]], sem="O")
                continue
            ev_x1l = S.dma("s", x1e[pb][:, :], x1_d[j], waits=[x1e_free[pb]],
                           sem="xq%d" % pb)
            ev_ln = emit_ln(moe, st * 1024, tmp_f[:, :], ev_moe[st])
            ev_moe[st] = ev_ln
            S.drain("v")
            ev_o = S.emit("v", lambda h, pb=pb: h.tensor_tensor(
                out=out_f[pb][:, :], in0=tmp_f[:, :], in1=x1e[pb][:, :], op=ALU.add),
                waits=[ev_x1l, out_free[pb]], inc="V")
            x1e_free[pb] = ev_o
            out_free[pb] = S.dma("s", out_d[j * 128:(j + 1) * 128, :], out_f[pb][:, :],
                                 waits=[ev_o], sem="O")
    S.emit("s", lambda h: None, waits=[("O", S.cnt["O"])])

    c_ctx.close()
    with octx, nc.Block() as block:
        S.replay(block)
    return nc


_NC_CACHE = {}


def _chunked(w, cols):
    return np.ascontiguousarray(
        w.reshape(8, 128, cols).transpose(1, 0, 2).reshape(128, 8 * cols))


def kernel(L0_attn_x, L0_attn_Wq, L0_attn_Aq, L0_attn_Bq, L0_attn_Wk, L0_attn_Ak,
           L0_attn_Bk, L0_attn_Wv, L0_attn_Av, L0_attn_Bv, L0_attn_Wp,
           L0_ffn_router_w, L0_ffn_router_bias, L0_ffn_gate_w, L0_ffn_up_w,
           L0_ffn_down_w, L0_ffn_sgate_w, L0_ffn_sup_w, L0_ffn_sdown_w):
    f = lambda a: np.ascontiguousarray(np.asarray(a, dtype=np.float32))
    X = f(L0_attn_x)
    wq = _chunked(f(L0_attn_Wq).T, 1024)
    wk = _chunked(f(L0_attn_Wk).T, 1024)
    wv = _chunked(f(L0_attn_Wv).T, 1024)
    wp = _chunked(f(L0_attn_Wp).T, 1024)
    Mkv = np.zeros((1024, 128), np.float32)
    Mkv[:, 32:40] = f(L0_attn_Bk).T
    Mkv[:, 64:72] = f(L0_attn_Bv).T
    bqkv = _chunked(Mkv, 128)
    bq = _chunked(f(L0_attn_Bq).T, 8)
    aT = np.zeros((128, 1024), np.float32)
    aT[0:8] = f(L0_attn_Aq).T
    aT[32:40] = f(L0_attn_Ak).T
    aT[64:72] = f(L0_attn_Av).T
    rw = _chunked(f(L0_ffn_router_w).T, 32)
    rb = f(L0_ffn_router_bias).reshape(1, 32)
    gw = np.ascontiguousarray(
        f(L0_ffn_gate_w).reshape(32, 128, 8, 128).transpose(0, 3, 2, 1).reshape(32, 128, 1024))
    uw = np.ascontiguousarray(
        f(L0_ffn_up_w).reshape(32, 128, 8, 128).transpose(0, 3, 2, 1).reshape(32, 128, 1024))
    dw = np.ascontiguousarray(f(L0_ffn_down_w).transpose(0, 2, 1))
    sg = _chunked(f(L0_ffn_sgate_w).T, 512)
    su = _chunked(f(L0_ffn_sup_w).T, 512)
    sd = np.ascontiguousarray(
        f(L0_ffn_sdown_w).T.reshape(4, 128, 1024).transpose(1, 0, 2).reshape(128, 4096))
    ident = np.eye(128, dtype=np.float32)
    ones = np.ones((128, 128), np.float32)
    scal = np.zeros((128, 8), np.float32)
    scal[:, 0] = EPS
    scal[:, 1] = 2.0
    scal[:, 2] = 0.125
    scal[:, 3] = -1.0
    scal[:, 4] = -0.5
    ishift = np.zeros((64, 256), np.float32)
    for d in range(64):
        ishift[d, d] = 1.0
        ishift[d, 128 + 64 + d] = 1.0
    slopes = np.array([2.0 ** (-(i + 1) / 2.0) for i in range(H)], np.float64)
    biastab = np.zeros((128, 512), np.float32)
    p = np.arange(128, dtype=np.float64)
    for hh in range(H):
        for kt in range(32):
            biastab[:, hh * 32 + kt] = (-slopes[hh] * (kt * 128 + p)).astype(np.float32)
    tri = np.where(np.arange(128)[:, None] > np.arange(128)[None, :],
                   NEGBIG, 0.0).astype(np.float32)
    full = np.full((128, 128), NEGBIG, np.float32)
    zero = np.zeros((128, 128), np.float32)
    maskr = [np.concatenate([tri, full], axis=1), np.concatenate([zero, tri], axis=1)]

    if "nc" not in _NC_CACHE:
        _NC_CACHE["nc"] = build_program()
    nc = _NC_CACHE["nc"]

    in_maps = []
    for c in range(8):
        b, r = c // 2, c % 2
        xs = np.ascontiguousarray(X[b])
        own_rows = np.concatenate(
            [np.arange((2 * j + r) * 128, (2 * j + r + 1) * 128) for j in range(NT)])
        xo = np.ascontiguousarray(xs[own_rows])
        in_maps.append(dict(
            xseq=xs, xown=xo, maskp=maskr[r], biastab=biastab, identc=ident,
            ishift=ishift, onesc=ones, scal=scal, wq=wq, wk=wk, wv=wv, wp=wp, bqkv=bqkv,
            bq=bq, at=aT, rw=rw, rb=rb, gw=gw, uw=uw, dw=dw, sg=sg, su=su, sd=sd))
    res = run_bass_kernel_spmd(nc, in_maps, core_ids=list(range(8)))
    out = np.empty((B, T, E), np.float32)
    for c in range(8):
        b, r = c // 2, c % 2
        oc = res.results[c]["out"]
        for j in range(NT):
            out[b, (2 * j + r) * 128:(2 * j + r + 1) * 128, :] = oc[j * 128:(j + 1) * 128, :]
    return out


# version 09
# speedup vs baseline: 1.8556x; 1.1006x over previous
import dataclasses
import math
from contextlib import ExitStack

import numpy as np
import concourse.bass as bass
import concourse.mybir as mybir
from concourse.bass_utils import run_bass_kernel_spmd

F32 = mybir.dt.float32
BF16 = mybir.dt.bfloat16
AF = mybir.ActivationFunctionType
ALU = mybir.AluOpType
AX = mybir.AxisListType

B, T, E, H, DH = 4, 4096, 1024, 16, 64
NEXP, HS = 32, 512
NT, NG, SGRP = 16, 4, 8
EPS = 1e-5
NEGBIG = -1e30
DEBUG_STOP_AT_X1 = False
DEBUG_FFN_RAW = False
DEBUG_FFN_W = False
DEBUG_NBLK = 4
DEBUG_NO_SHARED = False


def mkap(handle, off, pattern):
    a = handle[:, :] if len(handle.shape) == 2 else handle[:]
    return dataclasses.replace(a, offset=a.offset + off, ap=pattern)


class Sched:
    ENGS = ("s", "g", "t", "a", "v")

    def __init__(self, nc):
        self.nc = nc
        self.q = {e: [] for e in self.ENGS}
        self.sems = {}
        self.cnt = {}
        self.floor = {}

    def sem(self, ctx, name):
        self.sems[name] = ctx.enter_context(self.nc.semaphore(name))
        self.cnt[name] = 0

    def set_floor(self):
        self.floor = dict(self.cnt)

    def emit(self, eng, fn, waits=(), inc=None, amt=1):
        ws = dict(self.floor)
        for w in waits:
            if w is None:
                continue
            sname, val = w
            assert val <= self.cnt[sname], (sname, val, self.cnt[sname])
            if ws.get(sname, 0) < val:
                ws[sname] = val
        evt = None
        if inc is not None:
            self.cnt[inc] += amt
            evt = (inc, self.cnt[inc])
        self.q[eng].append((fn, tuple(ws.items()), (inc, amt) if inc else None))
        return evt

    def dma(self, eng, out, in_, waits=(), sem="dS"):
        return self.emit(eng, lambda h, o=out, i=in_: h.dma_start(out=o, in_=i),
                         waits=waits, inc=sem, amt=16)

    def drain(self, eng):
        self.emit(eng, lambda h: h.drain())

    def replay(self, block):
        decos = {"s": block.sync, "g": block.gpsimd, "t": block.tensor,
                 "a": block.scalar, "v": block.vector}
        for ek in self.ENGS:
            ops = self.q[ek]
            if not ops:
                continue
            def body(h, ops=ops):
                hwm = {}
                for fn, waits, inc in ops:
                    for sname, val in waits:
                        if hwm.get(sname, 0) < val:
                            h.wait_ge(self.sems[sname], val)
                            hwm[sname] = val
                    ins = fn(h)
                    if inc is not None and ins is not None:
                        ins.then_inc(self.sems[inc[0]], inc[1])
            decos[ek](body)


def build_program():
    nc = bass.Bass("TRN2", target_bir_lowering=False)
    dt = nc.dram_tensor
    xseq = dt("xseq", [T, E], F32, kind="ExternalInput").ap()
    xown = dt("xown", [NT * 128, E], F32, kind="ExternalInput").ap()
    maskp_d = dt("maskp", [128, 256], F32, kind="ExternalInput").ap()
    bias_d = dt("biastab", [128, 512], F32, kind="ExternalInput").ap()
    ident_d = dt("identc", [128, 128], F32, kind="ExternalInput").ap()
    ishift_d = dt("ishift", [64, 256], F32, kind="ExternalInput").ap()
    ones_d = dt("onesc", [128, 128], F32, kind="ExternalInput").ap()
    scal_d = dt("scal", [128, 8], F32, kind="ExternalInput").ap()
    wq_h = dt("wq", [128, 8 * 1024], F32, kind="ExternalInput").ap()
    wk_h = dt("wk", [128, 8 * 1024], F32, kind="ExternalInput").ap()
    wv_h = dt("wv", [128, 8 * 1024], F32, kind="ExternalInput").ap()
    wp_h = dt("wp", [128, 8 * 1024], F32, kind="ExternalInput").ap()
    bqkv_h = dt("bqkv", [128, 8 * 128], F32, kind="ExternalInput").ap()
    bq_h = dt("bq", [128, 8 * 8], F32, kind="ExternalInput").ap()
    at_h = dt("at", [128, 1024], F32, kind="ExternalInput").ap()
    rw_h = dt("rw", [128, 8 * 32], F32, kind="ExternalInput").ap()
    rb_h = dt("rb", [1, 32], F32, kind="ExternalInput").ap()
    gw_h = dt("gw", [NEXP, 128, 1024], F32, kind="ExternalInput").ap()
    uw_h = dt("uw", [NEXP, 128, 1024], F32, kind="ExternalInput").ap()
    dw_h = dt("dw", [NEXP, 128, 1024], F32, kind="ExternalInput").ap()
    sg_h = dt("sg", [128, 8 * 512], F32, kind="ExternalInput").ap()
    su_h = dt("su", [128, 8 * 512], F32, kind="ExternalInput").ap()
    sd_h = dt("sd", [128, 4 * 1024], F32, kind="ExternalInput").ap()
    out_d = dt("out", [NT * 128, E], F32, kind="ExternalOutput").ap()
    l1s_d = dt("l1scr", [SGRP, 128, 4096], BF16, kind="Internal").ap()
    l1o_d = dt("l1oscr", [NG, 128, 4096], BF16, kind="Internal").ap()
    x1_d = dt("x1scr", [NT, 128, 1024], F32, kind="Internal").ap()
    wg_d = dt("wgscr", [2, 32, 1024], BF16, kind="Internal").ap()

    octx = ExitStack()
    S = Sched(nc)
    for s in ("dS", "dC", "dC2", "dW0", "dW1", "dP", "dF", "g0", "g1",
              "u0", "u1", "wr0", "wr1", "l10", "l11", "xq0", "xq1",
              "P", "A", "V", "O", "oc",
              "d0", "d1", "d2", "d3", "d4", "d5", "d6", "d7"):
        S.sem(octx, s)

    def SB(ctx, n, shp, d=F32):
        return ctx.enter_context(nc.sbuf_tensor(n, shp, d))

    ident_b = SB(octx, "ident_b", [128, 128], BF16)
    ishift_b = SB(octx, "ishift_b", [64, 256], BF16)
    ones_f = SB(octx, "ones_f", [128, 128], F32)
    ones_b = SB(octx, "ones_b", [128, 128], BF16)
    maskp_s = SB(octx, "maskp_s", [128, 256], F32)
    bias_s = SB(octx, "bias_s", [128, 512], F32)
    rb_s = SB(octx, "rb_s", [1, 32], F32)
    scal_s = SB(octx, "scal_s", [128, 8], F32)
    at_b = SB(octx, "at_b", [128, 1024], BF16)
    bqkv_b = SB(octx, "bqkv_b", [128, 8 * 128], BF16)
    bq_b = SB(octx, "bq_b", [128, 8 * 8], BF16)
    xt = [SB(octx, "xt0", [128, 1024]), SB(octx, "xt1", [128, 1024])]
    lnb = [SB(octx, "lnb0", [128, 1024], BF16), SB(octx, "lnb1", [128, 1024], BF16)]
    tmp_f = SB(octx, "tmpf", [128, 1024])
    lnscr = [dict(bn6=SB(octx, f"bn6_{i}", [128, 12]), mv=SB(octx, f"mv_{i}", [128, 2]),
                  sd=SB(octx, f"sd_{i}", [128, 1]), rs=SB(octx, f"rs_{i}", [128, 1]),
                  busy_v=None, busy_a=None) for i in range(2)]
    pA = octx.enter_context(nc.psum_tensor("pA", [128, 1024], F32))
    pB = octx.enter_context(nc.psum_tensor("pB", [128, 1024], F32))
    pC = octx.enter_context(nc.psum_tensor("pC", [128, 1024], F32))
    pT = octx.enter_context(nc.psum_tensor("pT", [128, 2048], BF16))

    ev_c = {}
    for key, dst, src in [("ident", ident_b, ident_d), ("ishift", ishift_b, ishift_d),
                          ("onesb", ones_b, ones_d), ("at", at_b, at_h),
                          ("bqkv", bqkv_b, bqkv_h), ("bq", bq_b, bq_h)]:
        ev_c[key] = S.dma("g", dst[:, :], src, sem="dC")
    for key, dst, src in [("ones", ones_f, ones_d), ("maskp", maskp_s, maskp_d),
                          ("bias", bias_s, bias_d), ("rb", rb_s, rb_h),
                          ("scal", scal_s, scal_d)]:
        ev_c[key] = S.dma("s", dst[:, :], src, sem="dC2")

    xt_free = [None, None]
    lnb_free = [None, None]
    pT_free = [None, None]
    ln_i = [0]

    def emit_ln(x_tensor, x_off, out_ap, x_evt, out_wait=()):
        x_ap = x_tensor[:, x_off:x_off + 1024]
        sc = lnscr[ln_i[0] % 2]
        ln_i[0] += 1
        S.emit("v", lambda h, a=sc["bn6"], b=x_tensor, o=x_off: h.bn_stats(
            a[:, 0:6], b[:, o:o + 512]),
            waits=[x_evt, sc["busy_v"], sc["busy_a"]])
        S.emit("v", lambda h, a=sc["bn6"], b=x_tensor, o=x_off: h.bn_stats(
            a[:, 6:12], b[:, o + 512:o + 1024]))
        S.drain("v")
        ev_mv = S.emit("v", lambda h, a=sc["mv"], b=sc["bn6"]: h.bn_aggr(a[:, :], b[:, :]), inc="V")
        S.emit("a", lambda h, a=sc["sd"], b=sc["mv"]: h.activation(
            out=a[:, :], in_=b[:, 1:2], func=AF.Ln, bias=scal_s[:, 0:1]),
            waits=[ev_mv, ev_c["scal"]])
        S.drain("a")
        ev_sd = S.emit("a", lambda h, a=sc["rs"], b=sc["sd"]: h.activation(
            out=a[:, :], in_=b[:, :], func=AF.Exp, scale=scal_s[:, 4:5]), inc="A")
        ev = S.emit("v", lambda h, o=out_ap, x=x_ap, m=sc["mv"], r=sc["rs"]: h.tensor_scalar(
            out=o, in0=x, scalar1=m[:, 0:1], scalar2=r[:, 0:1],
            op0=ALU.subtract, op1=ALU.mult), waits=list(out_wait) + [ev_sd], inc="V")
        sc["busy_v"] = ev
        sc["busy_a"] = ev_sd
        return ev

    def emit_ln_transpose(src_dram_row, dst_tensor, dst_off):
        i = ln_i[0] % 2
        ev_x = S.dma("s", xt[i][:, :], src_dram_row, waits=[xt_free[i]],
                     sem="xq%d" % i)
        ev_ln = emit_ln(xt[i], 0, lnb[i][:, :], ev_x, out_wait=[lnb_free[i]])
        xt_free[i] = ev_ln
        th = ln_i[0] % 2
        ev_tr = None
        for c in range(8):
            w = [ev_ln, ev_c["ident"]] + ([pT_free[th]] if c == 0 else [])
            ev_tr = S.emit("t", lambda h, c=c, i=i, th=th: h.transpose(
                pT[:, th * 1024 + c * 128: th * 1024 + (c + 1) * 128],
                in_=lnb[i][:, c * 128:(c + 1) * 128], identity=ident_b[:, :]),
                waits=w, inc="P")
        lnb_free[i] = ev_tr
        ev_cp = S.emit("a", lambda h, th=th, dst=mkap(dst_tensor, dst_off,
                       [[dst_tensor.shape[1], 128],
                        [dst_tensor.shape[1] // 8, 8], [1, 128]]): h.activation(
            out=dst, in_=pT[:, th * 1024:(th + 1) * 1024].rearrange("p (c n) -> p c n", n=128),
            func=AF.Copy), waits=[ev_tr], inc="A")
        pT_free[th] = ev_cp
        return ev_cp

    ab_ctx = ExitStack()
    O_sb = SB(ab_ctx, "O_sb", [128, 8 * 2048], BF16)
    a_ctx = ExitStack()
    t1kv = SB(a_ctx, "t1kv", [128, SGRP * 512], BF16)
    t1q = SB(a_ctx, "t1q", [8, NG * 512], BF16)
    l1t = [SB(a_ctx, "l1t0", [128, 4096], BF16)]
    wq_s = SB(a_ctx, "wq_s", [128, 8 * 512], BF16)
    wk_s = SB(a_ctx, "wk_s", [128, 8 * 512], BF16)
    wv_s = SB(a_ctx, "wv_s", [128, 8 * 512], BF16)
    K_sb = SB(a_ctx, "K_sb", [128, 4 * 4096], BF16)
    V_sb = SB(a_ctx, "V_sb", [128, 8 * 32 * 66], BF16)
    Q_sb = SB(a_ctx, "Q_sb", [128, 8 * 2048], BF16)
    Pt = [SB(a_ctx, "Pt%d" % i, [128, 512], BF16) for i in range(4)]
    rec_s = SB(a_ctx, "rec_s", [128, 512], BF16)
    lnd_s = SB(a_ctx, "lnd_s", [128, 512])
    bc_s = SB(a_ctx, "bc_s", [64, 512], BF16)
    On_s = SB(a_ctx, "On_s", [64, 512], BF16)

    l1t_free = [None, None]
    ev_t1_seq = [None] * SGRP
    ev_t1_own = [None] * NG
    pfree = {k: None for k in ("A0", "A1", "B0", "B1", "C0", "C1")}
    PREG = {"A0": (pA, 0), "A1": (pA, 512), "B0": (pB, 0), "B1": (pB, 512),
            "C0": (pC, 0), "C1": (pC, 512)}

    ev_qz = S.emit("v", lambda h: h.memset(Q_sb[:, :], 0.0), inc="V")
    ev_vones = S.emit("v", lambda h: h.memset(
        V_sb[:, :].rearrange("p (s n) -> p s n", n=66)[:, :, 64:66], 1.0), inc="V")

    def emit_weight_half(hh, extra_wait):
        ev = None
        for dst, src in ((wq_s, wq_h), (wk_s, wk_h), (wv_s, wv_h)):
            for c in range(8):
                ev = S.dma("g", dst[:, c * 512:(c + 1) * 512],
                           src[:, c * 1024 + hh * 512: c * 1024 + hh * 512 + 512],
                           waits=extra_wait, sem="dW%d" % hh)
        return ev

    def emit_qkv_half(hh, ev_w, first):
        kv_last, q_last = None, None
        acc_regs = ["A0", "A1", "B0", "B1"]
        for g in range(SGRP):
            gb = 0
            if first:
                subev = []
                for s in range(4):
                    subev.append(emit_ln_transpose(
                        xseq[(g * 4 + s) * 128:(g * 4 + s + 1) * 128, :],
                        l1t[gb], s * 128))
                ev_l1 = subev[-1]
                S.dma("g", l1s_d[g], l1t[gb][:, :], waits=subev)
                for ic in range(8):
                    w = [ev_l1, ev_c["bqkv"]] + ([pfree["C0"]] if ic == 0 else [])
                    S.emit("t", lambda h, ic=ic, gb=gb: h.matmul(
                        pC[:, 0:512], lhsT=bqkv_b[:, ic * 128:(ic + 1) * 128],
                        rhs=l1t[gb][:, ic * 512:(ic + 1) * 512],
                        start=(ic == 0), stop=(ic == 7)), waits=w,
                        inc="P" if ic == 7 else None)
                ev_t1p = ("P", S.cnt["P"])
                S.emit("a", lambda h, g=g: h.activation(
                    out=t1kv[32:40, g * 512:(g + 1) * 512], in_=pC[32:40, 0:512],
                    func=AF.Copy, scale=scal_s[32:40, 1:2]),
                    waits=[ev_t1p, ev_c["scal"]], inc="A")
                ev_t1_seq[g] = S.emit("a", lambda h, g=g: h.activation(
                    out=t1kv[64:72, g * 512:(g + 1) * 512], in_=pC[64:72, 0:512],
                    func=AF.Copy, scale=scal_s[64:72, 1:2]), inc="A")
                pfree["C0"] = ev_t1_seq[g]
            else:
                ev_l1 = S.dma("s", l1t[gb][:, :], l1s_d[g], waits=[l1t_free[gb]],
                              sem="l10")
            for c in range(4):
                pp, off = PREG[acc_regs[c]]
                for ic in range(8):
                    w = [ev_l1, ev_w] + ([pfree[acc_regs[c]]] if ic == 0 else [])
                    S.emit("t", lambda h, c=c, ic=ic, gb=gb, pp=pp, off=off: h.matmul(
                        pp[:, off:off + 512],
                        lhsT=wk_s[:, ic * 512 + c * 128: ic * 512 + (c + 1) * 128],
                        rhs=l1t[gb][:, ic * 512:(ic + 1) * 512],
                        start=(ic == 0), stop=False), waits=w)
                S.emit("t", lambda h, c=c, g=g, hh=hh, pp=pp, off=off: h.matmul(
                    pp[:, off:off + 512],
                    lhsT=at_b[32:40, hh * 512 + c * 128: hh * 512 + (c + 1) * 128],
                    rhs=t1kv[32:40, g * 512:(g + 1) * 512],
                    start=False, stop=True),
                    waits=[ev_t1_seq[g], ev_c["at"]], inc="P")
                ev_kp = ("P", S.cnt["P"])
                kv_last = S.emit("a", lambda h, c=c, g=g, pp=pp, off=off: h.activation(
                    out=K_sb[:, c * 4096 + g * 512: c * 4096 + (g + 1) * 512],
                    in_=pp[:, off:off + 512], func=AF.Copy), waits=[ev_kp], inc="A")
                pfree[acc_regs[c]] = kv_last
            for s in range(4):
                kt = g * 4 + s
                vr = "C0" if s % 2 == 0 else "C1"
                pp, off = PREG[vr]
                for ic in range(8):
                    w = [ev_l1, ev_w] + ([pfree[vr]] if ic == 0 else [])
                    S.emit("t", lambda h, s=s, ic=ic, gb=gb, pp=pp, off=off: h.matmul(
                        pp[:, off:off + 512],
                        lhsT=l1t[gb][:, ic * 512 + s * 128: ic * 512 + (s + 1) * 128],
                        rhs=wv_s[:, ic * 512:(ic + 1) * 512],
                        start=(ic == 0), stop=False), waits=w)
                S.emit("t", lambda h, g=g, s=s, hh=hh, pp=pp, off=off: h.matmul(
                    pp[:, off:off + 512],
                    lhsT=t1kv[64:72, (g * 4 + s) * 128:(g * 4 + s + 1) * 128],
                    rhs=at_b[64:72, hh * 512:(hh + 1) * 512],
                    start=False, stop=True),
                    waits=[ev_t1_seq[g], ev_c["at"]], inc="P")
                ev_vp = ("P", S.cnt["P"])
                dst = mkap(V_sb, kt * 66, [[8 * 32 * 66, 128], [32 * 66, 8], [1, 64]])
                kv_last = S.emit("a", lambda h, pp=pp, off=off, dst=dst: h.activation(
                    out=dst, in_=pp[:, off:off + 512].rearrange("p (c n) -> p c n", n=64),
                    func=AF.Copy), waits=[ev_vp, ev_vones], inc="A")
                pfree[vr] = kv_last
            l1t_free[gb] = ("P", S.cnt["P"])
        for og in range(NG):
            gb = 0
            if first:
                subev = []
                for s in range(4):
                    subev.append(emit_ln_transpose(
                        xown[(og * 4 + s) * 128:(og * 4 + s + 1) * 128, :],
                        l1t[gb], s * 128))
                ev_l1 = subev[-1]
                S.dma("g", l1o_d[og], l1t[gb][:, :], waits=subev)
                for ic in range(8):
                    w = [ev_l1, ev_c["bq"]] + ([pfree["C0"]] if ic == 0 else [])
                    S.emit("t", lambda h, ic=ic, gb=gb: h.matmul(
                        pC[0:8, 0:512], lhsT=bq_b[:, ic * 8:(ic + 1) * 8],
                        rhs=l1t[gb][:, ic * 512:(ic + 1) * 512],
                        start=(ic == 0), stop=(ic == 7)), waits=w,
                        inc="P" if ic == 7 else None)
                ev_qt1p = ("P", S.cnt["P"])
                ev_t1_own[og] = S.emit("a", lambda h, og=og: h.activation(
                    out=t1q[0:8, og * 512:(og + 1) * 512], in_=pC[0:8, 0:512],
                    func=AF.Copy, scale=scal_s[0:8, 1:2]),
                    waits=[ev_qt1p, ev_c["scal"]], inc="A")
                pfree["C0"] = ev_t1_own[og]
            else:
                ev_l1 = S.dma("s", l1t[gb][:, :], l1o_d[og], waits=[l1t_free[gb]],
                              sem="l10")
            for c in range(4):
                pp, off = PREG[acc_regs[c]]
                for ic in range(8):
                    w = [ev_l1, ev_w] + ([pfree[acc_regs[c]]] if ic == 0 else [])
                    S.emit("t", lambda h, c=c, ic=ic, gb=gb, pp=pp, off=off: h.matmul(
                        pp[:, off:off + 512],
                        lhsT=wq_s[:, ic * 512 + c * 128: ic * 512 + (c + 1) * 128],
                        rhs=l1t[gb][:, ic * 512:(ic + 1) * 512],
                        start=(ic == 0), stop=False), waits=w)
                S.emit("t", lambda h, c=c, og=og, hh=hh, pp=pp, off=off: h.matmul(
                    pp[:, off:off + 512],
                    lhsT=at_b[0:8, hh * 512 + c * 128: hh * 512 + (c + 1) * 128],
                    rhs=t1q[0:8, og * 512:(og + 1) * 512],
                    start=False, stop=True),
                    waits=[ev_t1_own[og], ev_c["at"]], inc="P")
                ev_qp = ("P", S.cnt["P"])
                S.emit("a", lambda h, c=c, og=og, pp=pp, off=off: h.activation(
                    out=Q_sb[0:64, (2 * c) * 2048 + og * 512: (2 * c) * 2048 + (og + 1) * 512],
                    in_=pp[0:64, off:off + 512], func=AF.Copy),
                    waits=[ev_qp, ev_qz], inc="A")
                q_last = S.emit("a", lambda h, c=c, og=og, pp=pp, off=off: h.activation(
                    out=Q_sb[64:128, (2 * c + 1) * 2048 + og * 512: (2 * c + 1) * 2048 + (og + 1) * 512],
                    in_=pp[64:128, off:off + 512], func=AF.Copy), inc="A")
                pfree[acc_regs[c]] = q_last
            l1t_free[gb] = ("P", S.cnt["P"])
        return kv_last, q_last

    def emit_attn_half(hh, ev_ready):
        Pt_free = [None, None, None, None]
        SREG = [("C0", pC, 0), ("C1", pC, 512), ("B0", pB, 0)]
        state = dict(rec_free=None, bc_free=None, On_free=None, chunk_free=None)

        def emit_norm(gq, hl, ev_pv):
            cl, hp = hl // 2, hl % 2
            o_reg = "A0" if hp == 0 else "A1"
            ocol = (hh * 4 + cl) * 2048 + gq * 512
            ev_ld = S.emit("a", lambda h, hp=hp: h.activation(
                out=lnd_s[64:65, :], in_=pA[64:65, hp * 512:(hp + 1) * 512],
                func=AF.Ln), waits=[ev_pv, state["rec_free"]], inc="A")
            ev_rec = S.emit("a", lambda h: h.activation(
                out=rec_s[64:65, :], in_=lnd_s[64:65, :], func=AF.Exp,
                scale=scal_s[64:65, 3:4]), waits=[ev_c["scal"]], inc="A")
            ev_bc = S.emit("t", lambda h: h.matmul(
                pB[:, 512:1024], lhsT=ones_b[64:65, 0:128], rhs=rec_s[64:65, 0:512],
                start=True, stop=True),
                waits=[ev_rec, ev_c["onesb"], pfree["B1"]], inc="P")
            state["rec_free"] = ev_bc
            ev_bce = S.emit("v", lambda h: h.tensor_copy(
                bc_s[:, :], pB[0:64, 512:1024]),
                waits=[ev_bc, state["bc_free"]], inc="V")
            pfree["B1"] = ev_bce
            if hp == 0:
                ev_on = S.emit("v", lambda h, ocol=ocol: h.tensor_tensor(
                    out=O_sb[0:64, ocol:ocol + 512], in0=pA[0:64, 0:512],
                    in1=bc_s[:, :], op=ALU.mult), waits=[ev_bce], inc="V")
                state["bc_free"] = ev_on
                pfree["A0"] = ev_on
            else:
                ev_on = S.emit("v", lambda h: h.tensor_tensor(
                    out=On_s[:, :], in0=pA[0:64, 512:1024],
                    in1=bc_s[:, :], op=ALU.mult),
                    waits=[ev_bce, state["On_free"]], inc="V")
                state["bc_free"] = ev_on
                pfree["A1"] = ev_on
                state["On_free"] = S.dma(
                    "s", O_sb[64:128, ocol:ocol + 512], On_s[0:64, :],
                    waits=[ev_on], sem="oc")
            return state["On_free"]

        kt_i = [0]
        pt_i = [0]

        def emit_s(gq, hl, kt, nkt):
            h_g = hh * 8 + hl
            cl, hp = hl // 2, hl % 2
            cs = 128 * max(0, kt // 2 - 4 * gq)
            sr, sp, soff = SREG[kt_i[0] % 3]
            kt_i[0] += 1
            w = [ev_ready, pfree[sr]]
            ev_s = S.emit("t", lambda h, hl=hl, cl=cl, kt=kt, gq=gq, cs=cs, sp=sp, soff=soff: h.matmul(
                sp[:, soff + cs: soff + 512],
                lhsT=K_sb[:, cl * 4096 + kt * 128: cl * 4096 + (kt + 1) * 128],
                rhs=Q_sb[:, hl * 2048 + gq * 512 + cs: hl * 2048 + (gq + 1) * 512],
                start=True, stop=True), waits=w, inc="P")
            ev_m = None
            if kt >= 8 * gq:
                jm = kt // 2 - 4 * gq
                ev_m = S.emit("v", lambda h, sp=sp, soff=soff, jm=jm, kt=kt: h.tensor_tensor(
                    out=sp[:, soff + jm * 128: soff + (jm + 1) * 128],
                    in0=sp[:, soff + jm * 128: soff + (jm + 1) * 128],
                    in1=maskp_s[:, (kt % 2) * 128:(kt % 2 + 1) * 128],
                    op=ALU.add), waits=[ev_s, ev_c["maskp"]], inc="V")
            pi = pt_i[0] % 4
            pt_i[0] += 1
            ev_e = S.emit("a", lambda h, pi=pi, cs=cs, h_g=h_g, kt=kt, sp=sp, soff=soff: h.activation(
                out=Pt[pi][:, cs:512], in_=sp[:, soff + cs: soff + 512],
                func=AF.Exp, bias=bias_s[:, h_g * 32 + kt: h_g * 32 + kt + 1],
                scale=scal_s[:, 2:3]),
                waits=[ev_s, ev_m, ev_c["bias"], ev_c["scal"], Pt_free[pi]], inc="A")
            pfree[sr] = ev_e
            return (ev_e, pi)

        def emit_pv(gq, hl, kt, nkt, es):
            ev_e, pi = es
            hp = hl % 2
            o_reg = "A0" if hp == 0 else "A1"
            cs = 128 * max(0, kt // 2 - 4 * gq)
            w2 = [ev_e] + ([pfree[o_reg]] if kt == 0 else [])
            ev_pv = S.emit("t", lambda h, hl=hl, kt=kt, cs=cs, pi=pi, hp=hp, nkt=nkt: h.matmul(
                pA[0:65, hp * 512 + cs:(hp + 1) * 512],
                lhsT=V_sb[:, (hl * 32 + kt) * 66: (hl * 32 + kt) * 66 + 65],
                rhs=Pt[pi][:, cs:512],
                start=(kt == 0), stop=(kt == nkt - 1)), waits=w2, inc="P")
            Pt_free[pi] = ev_pv
            return ev_pv

        LAG = 3
        visits = []
        for gq in range(NG):
            for hl in range(8):
                nkt = 8 * gq + 8
                for kt in range(nkt):
                    visits.append((gq, hl, kt, nkt))
        ev_es = {}
        norm_due = []
        ev_oc = None
        n = len(visits)
        for i in range(n + LAG):
            if i < n:
                ev_es[i] = emit_s(*visits[i])
            j = i - LAG
            if j >= 0:
                gq, hl, kt, nkt = visits[j]
                ev_pv = emit_pv(gq, hl, kt, nkt, ev_es.pop(j))
                if kt == nkt - 1:
                    norm_due.append((i + 2, (gq, hl, ev_pv)))
            while norm_due and norm_due[0][0] <= i:
                _, args = norm_due.pop(0)
                ev_oc = emit_norm(*args) or ev_oc
        while norm_due:
            _, args = norm_due.pop(0)
            ev_oc = emit_norm(*args) or ev_oc
        return [ev_oc, ("V", S.cnt["V"])]

    ev_w0 = emit_weight_half(0, [])
    kv0, q0 = emit_qkv_half(0, ev_w0, first=True)
    ev_oc0 = emit_attn_half(0, q0)
    ev_w1 = emit_weight_half(1, [("P", S.cnt["P"])])
    kv1, q1 = emit_qkv_half(1, ev_w1, first=False)
    ev_oc1 = emit_attn_half(1, q1)

    a_ctx.close()
    S.set_floor()
    b_ctx = ExitStack()
    wp_s = SB(b_ctx, "wp_s", [128, 8 * 1024], BF16)
    at_f = [SB(b_ctx, "atf0", [128, 1024]), SB(b_ctx, "atf1", [128, 1024])]
    xo_f = [SB(b_ctx, "xof0", [128, 1024]), SB(b_ctx, "xof1", [128, 1024])]
    x1t = [SB(b_ctx, "x1t0", [128, 1024]), SB(b_ctx, "x1t1", [128, 1024])]
    ev_wp = None
    for ic in range(8):
        ev_wp = S.dma("g", wp_s[:, ic * 1024:(ic + 1) * 1024],
                      wp_h[:, ic * 1024:(ic + 1) * 1024], sem="dP")
    pacc_free = [None, None]
    at_free = [None, None]
    xo_free = [None, None]
    x1t_free = [None, None]
    ev_x1 = [None] * NT
    for j in range(NT):
        pb = j % 2
        pp = pA if pb == 0 else pB
        ev_xo = S.dma("s", xo_f[pb][:, :], xown[j * 128:(j + 1) * 128, :],
                      waits=[xo_free[pb]], sem="xq%d" % pb)
        for ic in range(8):
            for hf in range(2):
                w = list(ev_oc1) + [ev_wp] + ([pacc_free[pb]] if (ic == 0 and hf == 0) else [])
                S.emit("t", lambda h, ic=ic, hf=hf, j=j, pp=pp: h.matmul(
                    pp[:, hf * 512:(hf + 1) * 512],
                    lhsT=O_sb[:, ic * 2048 + j * 128:ic * 2048 + (j + 1) * 128],
                    rhs=wp_s[:, ic * 1024 + hf * 512: ic * 1024 + (hf + 1) * 512],
                    start=(ic == 0), stop=(ic == 7)),
                    waits=w, inc="P" if (ic == 7 and hf == 1) else None)
        ev_wpp = ("P", S.cnt["P"])
        ev_ae = S.emit("a", lambda h, pb=pb, pp=pp: h.activation(
            out=at_f[pb][:, :], in_=pp[:, :], func=AF.Copy),
            waits=[ev_wpp, at_free[pb]], inc="A")
        pacc_free[pb] = ev_ae
        ev_ln = emit_ln(at_f[pb], 0, tmp_f[:, :], ev_ae)
        at_free[pb] = ev_ln
        S.drain("v")
        ev_x1t = S.emit("v", lambda h, pb=pb: h.tensor_tensor(
            out=x1t[pb][:, :], in0=tmp_f[:, :], in1=xo_f[pb][:, :], op=ALU.add),
            waits=[ev_xo, x1t_free[pb]], inc="V")
        xo_free[pb] = ev_x1t
        ev_x1[j] = S.dma("g", x1_d[j], x1t[pb][:, :], waits=[ev_x1t])
        x1t_free[pb] = ev_x1[j]

    if DEBUG_STOP_AT_X1:
        for j in range(NT):
            S.dma("s", out_d[j * 128:(j + 1) * 128, :], x1_d[j],
                  waits=[ev_x1[j]], sem="O")
        S.emit("s", lambda h: None, waits=[("O", S.cnt["O"])])
        b_ctx.close(); ab_ctx.close()
        with octx, nc.Block() as block:
            S.replay(block)
        return nc

    b_ctx.close()
    ab_ctx.close()
    S.set_floor()
    c_ctx = ExitStack()
    ln2T = [SB(c_ctx, "ln2T0", [128, 8 * 1024], BF16),
            SB(c_ctx, "ln2T1", [128, 8 * 1024], BF16)]
    rw_s = SB(c_ctx, "rw_s", [128, 8 * 32], BF16)
    sgw_s = SB(c_ctx, "sgw_s", [128, 8 * 512], BF16)
    suw_s = SB(c_ctx, "suw_s", [128, 8 * 512], BF16)
    sdw_s = SB(c_ctx, "sdw_s", [128, 4 * 1024], BF16)
    gw_s = [SB(c_ctx, "gw_s0", [128, 1024], BF16), SB(c_ctx, "gw_s1", [128, 1024], BF16)]
    uw_s = [SB(c_ctx, "uw_s0", [128, 1024], BF16), SB(c_ctx, "uw_s1", [128, 1024], BF16)]
    dw_s = [SB(c_ctx, f"dw_s{i}", [128, 1024], BF16) for i in range(8)]
    sc_f = SB(c_ctx, "sc_f", [128, 32])
    cur_f = SB(c_ctx, "cur_f", [128, 32])
    msk_f = SB(c_ctx, "msk_f", [128, 32])
    m_f = SB(c_ctx, "m_f", [128, 1])
    wu_f = SB(c_ctx, "wu_f", [128, 32])
    w_b = SB(c_ctx, "w_b", [128, 32], BF16)
    wfm = SB(c_ctx, "wfm", [32, 1024], BF16)
    wrow = [SB(c_ctx, "wrow0", [1, 1024], BF16), SB(c_ctx, "wrow1", [1, 1024], BF16)]
    sg_b = SB(c_ctx, "sg_b", [128, 1024], BF16)
    wb_b = SB(c_ctx, "wb_b", [128, 1024], BF16)
    h1_b = SB(c_ctx, "h1_b", [128, 1024], BF16)
    h_blk = SB(c_ctx, "h_blk", [128, 8 * 1024], BF16)
    hsh = SB(c_ctx, "hsh", [128, 4 * 1024], BF16)
    moe = SB(c_ctx, "moe", [128, 8 * 1024])
    out_f = [SB(c_ctx, "outf0", [128, 1024]), SB(c_ctx, "outf1", [128, 1024])]
    wdbg = SB(c_ctx, "wdbg", [128, 32])
    x1e = [SB(c_ctx, "x1e0", [128, 1024]), SB(c_ctx, "x1e1", [128, 1024])]

    ev_rw = S.dma("g", rw_s[:, :], rw_h, sem="dF")
    ev_sg = S.dma("g", sgw_s[:, :], sg_h, sem="dF")
    ev_su = S.dma("g", suw_s[:, :], su_h, sem="dF")
    ev_sd = S.dma("g", sdw_s[:, :], sd_h, sem="dF")

    def l2rhs(G, ic, hf):
        return mkap(ln2T[G], ic * 1024 + hf * 512, [[8192, 128], [1, 512]])

    ev_moe = [None] * 8
    w_b_free = [None]
    gw_free = [None, None]
    uw_free = [None, None]
    dw_free = [None] * 8
    wrow_free = [None, None]
    h_blk_free = [None] * 8
    sg_free, wb_free, h1_free = None, None, None
    x1e_free = [None, None]
    out_free = [None, None]
    topk_free = None
    for G in range(2):
        ev_l2_last = None
        ev_wfm = []
        pfree["C0"] = pfree.get("C0")
        ev_lnst = [None] * 8
        for st in range(8):
            j = G * 8 + st
            pb = st % 2
            ev_x1l = S.dma("s", x1e[pb][:, :], x1_d[j], waits=[ev_x1[j], x1e_free[pb]],
                           sem="xq%d" % pb)
            ev_ln = emit_ln(x1e[pb], 0, lnb[pb][:, :], ev_x1l,
                            out_wait=[lnb_free[pb]])
            x1e_free[pb] = ev_ln
            th = st % 2
            ev_tr = None
            for c in range(8):
                w = [ev_ln] + ([pT_free[th]] if c == 0 else [])
                ev_tr = S.emit("t", lambda h, c=c, pb=pb, th=th: h.transpose(
                    pT[:, th * 1024 + c * 128: th * 1024 + (c + 1) * 128],
                    in_=lnb[pb][:, c * 128:(c + 1) * 128], identity=ident_b[:, :]),
                    waits=w, inc="P")
            lnb_free[pb] = ev_tr
            ev_l2_last = S.emit("a", lambda h, th=th, dst=mkap(
                ln2T[G], st * 128, [[8192, 128], [1024, 8], [1, 128]]): h.activation(
                out=dst, in_=pT[:, th * 1024:(th + 1) * 1024].rearrange("p (c n) -> p c n", n=128),
                func=AF.Copy), waits=[ev_tr], inc="A")
            ev_lnst[st] = ev_l2_last
            pT_free[th] = ev_l2_last
        for st in range(8):
            th = st % 2
            for ic in range(8):
                w = [ev_lnst[st], ev_rw] + ([pfree["C0"]] if ic == 0 else [])
                S.emit("t", lambda h, ic=ic, st=st, G=G: h.matmul(
                    pC[:, 0:32],
                    lhsT=mkap(ln2T[G], ic * 1024 + st * 128, [[8192, 128], [1, 128]]),
                    rhs=rw_s[:, ic * 32:(ic + 1) * 32],
                    start=(ic == 0), stop=False), waits=w)
            S.emit("t", lambda h: h.matmul(
                pC[:, 0:32], lhsT=ones_f[0:1, 0:128], rhs=rb_s[0:1, :],
                start=False, stop=True), waits=[ev_c["rb"], ev_c["ones"]], inc="P")
            ev_rt = ("P", S.cnt["P"])
            ev_nm = S.emit("v", lambda h: h.reduce_max(
                m_f[:, :], pC[:, 0:32], axis=AX.X, negate=True),
                waits=[ev_rt, topk_free], inc="V")
            ev_sc = S.emit("a", lambda h: h.activation(
                out=sc_f[:, :], in_=pC[:, 0:32], func=AF.Exp, bias=m_f[:, 0:1]),
                waits=[ev_nm], inc="A")
            pfree["C0"] = ev_sc
            S.emit("v", lambda h: h.tensor_copy(cur_f[:, :], sc_f[:, :]), waits=[ev_sc])
            for it in range(3):
                S.drain("v")
                S.emit("v", lambda h: h.reduce_max(m_f[:, :], cur_f[:, :], axis=AX.X),
                       waits=[ev_sc])
                S.drain("v")
                S.emit("v", lambda h: h.tensor_scalar(
                    out=msk_f[:, :], in0=cur_f[:, :], scalar1=m_f[:, 0:1],
                    scalar2=1e30, op0=ALU.is_ge, op1=ALU.mult))
                S.drain("v")
                S.emit("v", lambda h: h.tensor_sub(cur_f[:, :], cur_f[:, :], msk_f[:, :]))
            S.drain("v")
            S.emit("v", lambda h: h.reduce_max(m_f[:, :], cur_f[:, :], axis=AX.X))
            S.drain("v")
            S.emit("v", lambda h: h.tensor_scalar(
                out=msk_f[:, :], in0=sc_f[:, :], scalar1=m_f[:, 0:1],
                scalar2=None, op0=ALU.is_ge))
            S.drain("v")
            S.emit("v", lambda h: h.tensor_mul(wu_f[:, :], sc_f[:, :], msk_f[:, :]))
            S.drain("v")
            S.emit("v", lambda h: h.reduce_sum(m_f[:, :], wu_f[:, :], axis=AX.X))
            S.drain("v")
            S.emit("v", lambda h: h.reciprocal(m_f[:, :], m_f[:, :]))
            S.drain("v")
            ev_wb = S.emit("v", lambda h: h.tensor_scalar(
                out=w_b[:, :], in0=wu_f[:, :], scalar1=m_f[:, 0:1], scalar2=None,
                op0=ALU.mult), waits=[w_b_free[0]], inc="V")
            topk_free = ev_wb
            ev_wt = S.emit("t", lambda h, th=th: h.transpose(
                pT[0:32, th * 1024: th * 1024 + 128],
                in_=w_b[:, :], identity=ident_b[:, :]),
                waits=[ev_wb, pT_free[th]], inc="P")
            ev_wf = S.emit("a", lambda h, st=st, th=th: h.activation(
                out=wfm[0:32, st * 128:(st + 1) * 128],
                in_=pT[0:32, th * 1024: th * 1024 + 128],
                func=AF.Copy), waits=[ev_wt], inc="A")
            ev_wfm.append(ev_wf)
            pT_free[th] = ev_wf
        ev_wsc = S.dma("g", wg_d[G], wfm[:, :], waits=ev_wfm)
        if DEBUG_FFN_W:
            continue
        for blk in range(DEBUG_NBLK):
            ev_h_all = None
            for e8 in range(8):
                e = blk * 8 + e8
                eb = e % 2
                ev_gw = S.dma("g", gw_s[eb][:, :], gw_h[e],
                              waits=[gw_free[eb]], sem="g%d" % eb)
                ev_uw = S.dma("g", uw_s[eb][:, :], uw_h[e],
                              waits=[uw_free[eb]], sem="u%d" % eb)
                ev_dwl = S.dma("g", dw_s[e8][:, :], dw_h[e],
                               waits=[dw_free[e8]], sem="d%d" % e8)
                dw_free[e8] = ev_dwl
                ev_wr = S.dma("s", wrow[eb][0:1, :], wg_d[G][e:e + 1, :],
                              waits=[ev_wsc, wrow_free[eb]], sem="wr%d" % eb)
                for ic in range(8):
                    for hf in range(2):
                        w = [ev_gw, ev_l2_last] + (
                            [pfree["A0"], pfree["A1"]] if (ic == 0 and hf == 0) else [])
                        S.emit("t", lambda h, ic=ic, hf=hf, eb=eb, G=G: h.matmul(
                            pA[:, hf * 512:(hf + 1) * 512],
                            lhsT=gw_s[eb][:, ic * 128:(ic + 1) * 128],
                            rhs=l2rhs(G, ic, hf), start=(ic == 0), stop=(ic == 7)),
                            waits=w, inc="P" if (ic == 7 and hf == 1) else None)
                ev_g = ("P", S.cnt["P"])
                gw_free[eb] = ev_g
                for ic in range(8):
                    for hf in range(2):
                        w = [ev_uw, ev_l2_last] + (
                            [pfree["B0"], pfree["B1"]] if (ic == 0 and hf == 0) else [])
                        S.emit("t", lambda h, ic=ic, hf=hf, eb=eb, G=G: h.matmul(
                            pB[:, hf * 512:(hf + 1) * 512],
                            lhsT=uw_s[eb][:, ic * 128:(ic + 1) * 128],
                            rhs=l2rhs(G, ic, hf), start=(ic == 0), stop=(ic == 7)),
                            waits=w, inc="P" if (ic == 7 and hf == 1) else None)
                ev_u = ("P", S.cnt["P"])
                uw_free[eb] = ev_u
                for hf in range(2):
                    w = [ev_wr, ev_c["onesb"]] + (
                        [pfree["C0"], pfree["C1"]] if hf == 0 else [])
                    S.emit("t", lambda h, hf=hf, eb=eb: h.matmul(
                        pC[:, hf * 512:(hf + 1) * 512], lhsT=ones_b[0:1, 0:128],
                        rhs=wrow[eb][0:1, hf * 512:(hf + 1) * 512],
                        start=True, stop=True), waits=w,
                        inc="P" if hf == 1 else None)
                ev_wbp = ("P", S.cnt["P"])
                wrow_free[eb] = ev_wbp
                ev_sgb = S.emit("a", lambda h: h.activation(
                    out=sg_b[:, :], in_=pA[:, :], func=AF.Silu),
                    waits=[ev_g, sg_free], inc="A")
                pfree["A0"] = pfree["A1"] = ev_sgb
                ev_wbe = S.emit("a", lambda h: h.activation(
                    out=wb_b[:, :], in_=pC[:, :], func=AF.Copy),
                    waits=[ev_wbp, wb_free], inc="A")
                pfree["C0"] = pfree["C1"] = ev_wbe
                ev_h1 = S.emit("v", lambda h: h.tensor_tensor(
                    out=h1_b[:, :], in0=sg_b[:, :], in1=pB[:, :], op=ALU.mult),
                    waits=[ev_sgb, ev_u, h1_free], inc="V")
                sg_free = ev_h1
                pfree["B0"] = pfree["B1"] = ev_h1
                ev_h = S.emit("v", lambda h, e8=e8: h.tensor_tensor(
                    out=h_blk[:, e8 * 1024:(e8 + 1) * 1024], in0=h1_b[:, :],
                    in1=wb_b[:, :], op=ALU.mult),
                    waits=[ev_h1, ev_wbe, h_blk_free[e8]], inc="V")
                h1_free = ev_h
                wb_free = ev_h
                ev_h_all = ev_h
            dn_regs = [("C0", "C1", pC), ("A0", "A1", pA)]
            for st in range(8):
                r0, r1, pp = dn_regs[st % 2]
                for e8 in range(8):
                    w = [ev_h_all, dw_free[e8]] + (
                        [pfree[r0], pfree[r1]] if e8 == 0 else [])
                    S.emit("t", lambda h, e8=e8, st=st, pp=pp: h.matmul(
                        pp[:, 0:512],
                        lhsT=h_blk[:, e8 * 1024 + st * 128: e8 * 1024 + (st + 1) * 128],
                        rhs=dw_s[e8][:, 0:512],
                        start=(e8 == 0), stop=False), waits=w)
                    S.emit("t", lambda h, e8=e8, st=st, pp=pp: h.matmul(
                        pp[:, 512:1024],
                        lhsT=h_blk[:, e8 * 1024 + st * 128: e8 * 1024 + (st + 1) * 128],
                        rhs=dw_s[e8][:, 512:1024],
                        start=(e8 == 0), stop=(e8 == 7)),
                        inc="P" if e8 == 7 else None)
                ev_dn = ("P", S.cnt["P"])
                if blk == 0:
                    ev_mo = S.emit("v", lambda h, st=st, pp=pp: h.tensor_copy(
                        moe[:, st * 1024:(st + 1) * 1024], pp[:, :]),
                        waits=[ev_dn, ev_moe[st]], inc="V")
                else:
                    ev_mo = S.emit("v", lambda h, st=st, pp=pp: h.tensor_tensor(
                        out=moe[:, st * 1024:(st + 1) * 1024],
                        in0=moe[:, st * 1024:(st + 1) * 1024], in1=pp[:, :],
                        op=ALU.add), waits=[ev_dn, ev_moe[st]], inc="V")
                ev_moe[st] = ev_mo
                pfree[r0] = pfree[r1] = ev_mo
            ev_dn_done = ("P", S.cnt["P"])
            for e8 in range(8):
                h_blk_free[e8] = ev_dn_done
                dw_free[e8] = ev_dn_done
        for hc in range(0 if DEBUG_NO_SHARED else 4):
            for ic in range(8):
                for hf in range(2):
                    w = [ev_sg, ev_l2_last] + (
                        [pfree["A0"], pfree["A1"]] if (ic == 0 and hf == 0) else [])
                    S.emit("t", lambda h, ic=ic, hf=hf, hc=hc, G=G: h.matmul(
                        pA[:, hf * 512:(hf + 1) * 512],
                        lhsT=sgw_s[:, ic * 512 + hc * 128: ic * 512 + (hc + 1) * 128],
                        rhs=l2rhs(G, ic, hf), start=(ic == 0), stop=(ic == 7)),
                        waits=w, inc="P" if (ic == 7 and hf == 1) else None)
            ev_g = ("P", S.cnt["P"])
            for ic in range(8):
                for hf in range(2):
                    w = [ev_su, ev_l2_last] + (
                        [pfree["B0"], pfree["B1"]] if (ic == 0 and hf == 0) else [])
                    S.emit("t", lambda h, ic=ic, hf=hf, hc=hc, G=G: h.matmul(
                        pB[:, hf * 512:(hf + 1) * 512],
                        lhsT=suw_s[:, ic * 512 + hc * 128: ic * 512 + (hc + 1) * 128],
                        rhs=l2rhs(G, ic, hf), start=(ic == 0), stop=(ic == 7)),
                        waits=w, inc="P" if (ic == 7 and hf == 1) else None)
            ev_u = ("P", S.cnt["P"])
            ev_sgb = S.emit("a", lambda h: h.activation(
                out=sg_b[:, :], in_=pA[:, :], func=AF.Silu),
                waits=[ev_g, sg_free], inc="A")
            pfree["A0"] = pfree["A1"] = ev_sgb
            ev_h = S.emit("v", lambda h, hc=hc: h.tensor_tensor(
                out=hsh[:, hc * 1024:(hc + 1) * 1024], in0=sg_b[:, :], in1=pB[:, :],
                op=ALU.mult), waits=[ev_sgb, ev_u], inc="V")
            sg_free = ev_h
            pfree["B0"] = pfree["B1"] = ev_h
        if DEBUG_NO_SHARED:
            for hc in []: pass
        ev_hsh = ("V", S.cnt["V"])
        dn_regs = [("C0", "C1", pC), ("A0", "A1", pA)]
        for st in range(0 if DEBUG_NO_SHARED else 8):
            r0, r1, pp = dn_regs[st % 2]
            for hc in range(4):
                w = [ev_hsh, ev_sd] + ([pfree[r0], pfree[r1]] if hc == 0 else [])
                S.emit("t", lambda h, hc=hc, st=st, pp=pp: h.matmul(
                    pp[:, 0:512],
                    lhsT=hsh[:, hc * 1024 + st * 128: hc * 1024 + (st + 1) * 128],
                    rhs=sdw_s[:, hc * 1024: hc * 1024 + 512],
                    start=(hc == 0), stop=False), waits=w)
                S.emit("t", lambda h, hc=hc, st=st, pp=pp: h.matmul(
                    pp[:, 512:1024],
                    lhsT=hsh[:, hc * 1024 + st * 128: hc * 1024 + (st + 1) * 128],
                    rhs=sdw_s[:, hc * 1024 + 512: (hc + 1) * 1024],
                    start=(hc == 0), stop=(hc == 3)),
                    inc="P" if hc == 3 else None)
            ev_dn = ("P", S.cnt["P"])
            ev_mo = S.emit("v", lambda h, st=st, pp=pp: h.tensor_tensor(
                out=moe[:, st * 1024:(st + 1) * 1024],
                in0=moe[:, st * 1024:(st + 1) * 1024], in1=pp[:, :], op=ALU.add),
                waits=[ev_dn, ev_moe[st]], inc="V")
            ev_moe[st] = ev_mo
            pfree[r0] = pfree[r1] = ev_mo
        for st in range(8):
            j = G * 8 + st
            pb = st % 2
            if DEBUG_FFN_RAW:
                out_free[pb] = S.dma("s", out_d[j * 128:(j + 1) * 128, :],
                                     moe[:, st * 1024:(st + 1) * 1024],
                                     waits=[ev_moe[st], out_free[pb]], sem="O")
                continue
            ev_x1l = S.dma("s", x1e[pb][:, :], x1_d[j], waits=[x1e_free[pb]],
                           sem="xq%d" % pb)
            ev_ln = emit_ln(moe, st * 1024, tmp_f[:, :], ev_moe[st])
            ev_moe[st] = ev_ln
            S.drain("v")
            ev_o = S.emit("v", lambda h, pb=pb: h.tensor_tensor(
                out=out_f[pb][:, :], in0=tmp_f[:, :], in1=x1e[pb][:, :], op=ALU.add),
                waits=[ev_x1l, out_free[pb]], inc="V")
            x1e_free[pb] = ev_o
            out_free[pb] = S.dma("s", out_d[j * 128:(j + 1) * 128, :], out_f[pb][:, :],
                                 waits=[ev_o], sem="O")
    S.emit("s", lambda h: None, waits=[("O", S.cnt["O"])])

    c_ctx.close()
    with octx, nc.Block() as block:
        S.replay(block)
    return nc


_NC_CACHE = {}


def _chunked(w, cols):
    return np.ascontiguousarray(
        w.reshape(8, 128, cols).transpose(1, 0, 2).reshape(128, 8 * cols))


def kernel(L0_attn_x, L0_attn_Wq, L0_attn_Aq, L0_attn_Bq, L0_attn_Wk, L0_attn_Ak,
           L0_attn_Bk, L0_attn_Wv, L0_attn_Av, L0_attn_Bv, L0_attn_Wp,
           L0_ffn_router_w, L0_ffn_router_bias, L0_ffn_gate_w, L0_ffn_up_w,
           L0_ffn_down_w, L0_ffn_sgate_w, L0_ffn_sup_w, L0_ffn_sdown_w):
    f = lambda a: np.ascontiguousarray(np.asarray(a, dtype=np.float32))
    X = f(L0_attn_x)
    wq = _chunked(f(L0_attn_Wq).T, 1024)
    wk = _chunked(f(L0_attn_Wk).T, 1024)
    wv = _chunked(f(L0_attn_Wv).T, 1024)
    wp = _chunked(f(L0_attn_Wp).T, 1024)
    Mkv = np.zeros((1024, 128), np.float32)
    Mkv[:, 32:40] = f(L0_attn_Bk).T
    Mkv[:, 64:72] = f(L0_attn_Bv).T
    bqkv = _chunked(Mkv, 128)
    bq = _chunked(f(L0_attn_Bq).T, 8)
    aT = np.zeros((128, 1024), np.float32)
    aT[0:8] = f(L0_attn_Aq).T
    aT[32:40] = f(L0_attn_Ak).T
    aT[64:72] = f(L0_attn_Av).T
    rw = _chunked(f(L0_ffn_router_w).T, 32)
    rb = f(L0_ffn_router_bias).reshape(1, 32)
    gw = np.ascontiguousarray(
        f(L0_ffn_gate_w).reshape(32, 128, 8, 128).transpose(0, 3, 2, 1).reshape(32, 128, 1024))
    uw = np.ascontiguousarray(
        f(L0_ffn_up_w).reshape(32, 128, 8, 128).transpose(0, 3, 2, 1).reshape(32, 128, 1024))
    dw = np.ascontiguousarray(f(L0_ffn_down_w).transpose(0, 2, 1))
    sg = _chunked(f(L0_ffn_sgate_w).T, 512)
    su = _chunked(f(L0_ffn_sup_w).T, 512)
    sd = np.ascontiguousarray(
        f(L0_ffn_sdown_w).T.reshape(4, 128, 1024).transpose(1, 0, 2).reshape(128, 4096))
    ident = np.eye(128, dtype=np.float32)
    ones = np.ones((128, 128), np.float32)
    scal = np.zeros((128, 8), np.float32)
    scal[:, 0] = EPS
    scal[:, 1] = 2.0
    scal[:, 2] = 0.125
    scal[:, 3] = -1.0
    scal[:, 4] = -0.5
    ishift = np.zeros((64, 256), np.float32)
    for d in range(64):
        ishift[d, d] = 1.0
        ishift[d, 128 + 64 + d] = 1.0
    slopes = np.array([2.0 ** (-(i + 1) / 2.0) for i in range(H)], np.float64)
    biastab = np.zeros((128, 512), np.float32)
    p = np.arange(128, dtype=np.float64)
    for hh in range(H):
        for kt in range(32):
            biastab[:, hh * 32 + kt] = (-slopes[hh] * (kt * 128 + p)).astype(np.float32)
    tri = np.where(np.arange(128)[:, None] > np.arange(128)[None, :],
                   NEGBIG, 0.0).astype(np.float32)
    full = np.full((128, 128), NEGBIG, np.float32)
    zero = np.zeros((128, 128), np.float32)
    maskr = [np.concatenate([tri, full], axis=1), np.concatenate([zero, tri], axis=1)]

    if "nc" not in _NC_CACHE:
        _NC_CACHE["nc"] = build_program()
    nc = _NC_CACHE["nc"]

    in_maps = []
    for c in range(8):
        b, r = c // 2, c % 2
        xs = np.ascontiguousarray(X[b])
        own_rows = np.concatenate(
            [np.arange((2 * j + r) * 128, (2 * j + r + 1) * 128) for j in range(NT)])
        xo = np.ascontiguousarray(xs[own_rows])
        in_maps.append(dict(
            xseq=xs, xown=xo, maskp=maskr[r], biastab=biastab, identc=ident,
            ishift=ishift, onesc=ones, scal=scal, wq=wq, wk=wk, wv=wv, wp=wp, bqkv=bqkv,
            bq=bq, at=aT, rw=rw, rb=rb, gw=gw, uw=uw, dw=dw, sg=sg, su=su, sd=sd))
    res = run_bass_kernel_spmd(nc, in_maps, core_ids=list(range(8)))
    out = np.empty((B, T, E), np.float32)
    for c in range(8):
        b, r = c // 2, c % 2
        oc = res.results[c]["out"]
        for j in range(NT):
            out[b, (2 * j + r) * 128:(2 * j + r + 1) * 128, :] = oc[j * 128:(j + 1) * 128, :]
    return out


# version 11
# speedup vs baseline: 1.8915x; 1.0194x over previous
import dataclasses
import math
from contextlib import ExitStack

import numpy as np
import concourse.bass as bass
import concourse.mybir as mybir
from concourse.bass_utils import run_bass_kernel_spmd

F32 = mybir.dt.float32
BF16 = mybir.dt.bfloat16
AF = mybir.ActivationFunctionType
ALU = mybir.AluOpType
AX = mybir.AxisListType

B, T, E, H, DH = 4, 4096, 1024, 16, 64
NEXP, HS = 32, 512
NT, NG, SGRP = 16, 4, 8
EPS = 1e-5
NEGBIG = -1e30
DEBUG_STOP_AT_X1 = False
DEBUG_FFN_RAW = False
DEBUG_FFN_W = False
DEBUG_NBLK = 4
DEBUG_NO_SHARED = False


def mkap(handle, off, pattern):
    a = handle[:, :] if len(handle.shape) == 2 else handle[:]
    return dataclasses.replace(a, offset=a.offset + off, ap=pattern)


class Sched:
    ENGS = ("s", "g", "t", "a", "v")

    def __init__(self, nc):
        self.nc = nc
        self.q = {e: [] for e in self.ENGS}
        self.sems = {}
        self.cnt = {}
        self.floor = {}

    def sem(self, ctx, name):
        self.sems[name] = ctx.enter_context(self.nc.semaphore(name))
        self.cnt[name] = 0

    def set_floor(self):
        self.floor = dict(self.cnt)

    def emit(self, eng, fn, waits=(), inc=None, amt=1):
        ws = dict(self.floor)
        for w in waits:
            if w is None:
                continue
            sname, val = w
            assert val <= self.cnt[sname], (sname, val, self.cnt[sname])
            if ws.get(sname, 0) < val:
                ws[sname] = val
        evt = None
        if inc is not None:
            self.cnt[inc] += amt
            evt = (inc, self.cnt[inc])
        self.q[eng].append((fn, tuple(ws.items()), (inc, amt) if inc else None))
        return evt

    def dma(self, eng, out, in_, waits=(), sem="dS"):
        return self.emit(eng, lambda h, o=out, i=in_: h.dma_start(out=o, in_=i),
                         waits=waits, inc=sem, amt=16)

    def drain(self, eng):
        self.emit(eng, lambda h: h.drain())

    def replay(self, block):
        decos = {"s": block.sync, "g": block.gpsimd, "t": block.tensor,
                 "a": block.scalar, "v": block.vector}
        for ek in self.ENGS:
            ops = self.q[ek]
            if not ops:
                continue
            def body(h, ops=ops):
                hwm = {}
                for fn, waits, inc in ops:
                    for sname, val in waits:
                        if hwm.get(sname, 0) < val:
                            h.wait_ge(self.sems[sname], val)
                            hwm[sname] = val
                    ins = fn(h)
                    if inc is not None and ins is not None:
                        ins.then_inc(self.sems[inc[0]], inc[1])
            decos[ek](body)


def build_program():
    nc = bass.Bass("TRN2", target_bir_lowering=False)
    dt = nc.dram_tensor
    xseq = dt("xseq", [T, E], F32, kind="ExternalInput").ap()
    xown = dt("xown", [NT * 128, E], F32, kind="ExternalInput").ap()
    maskp_d = dt("maskp", [128, 256], F32, kind="ExternalInput").ap()
    bias_d = dt("biastab", [128, 512], F32, kind="ExternalInput").ap()
    ident_d = dt("identc", [128, 128], F32, kind="ExternalInput").ap()
    ishift_d = dt("ishift", [64, 256], F32, kind="ExternalInput").ap()
    ones_d = dt("onesc", [128, 128], F32, kind="ExternalInput").ap()
    scal_d = dt("scal", [128, 8], F32, kind="ExternalInput").ap()
    wq_h = dt("wq", [128, 8 * 1024], F32, kind="ExternalInput").ap()
    wk_h = dt("wk", [128, 8 * 1024], F32, kind="ExternalInput").ap()
    wv_h = dt("wv", [128, 8 * 1024], F32, kind="ExternalInput").ap()
    wp_h = dt("wp", [128, 8 * 1024], F32, kind="ExternalInput").ap()
    bqkv_h = dt("bqkv", [128, 8 * 128], F32, kind="ExternalInput").ap()
    bq_h = dt("bq", [128, 8 * 8], F32, kind="ExternalInput").ap()
    at_h = dt("at", [128, 1024], F32, kind="ExternalInput").ap()
    rw_h = dt("rw", [128, 8 * 32], F32, kind="ExternalInput").ap()
    rb_h = dt("rb", [1, 32], F32, kind="ExternalInput").ap()
    gw_h = dt("gw", [NEXP, 128, 1024], F32, kind="ExternalInput").ap()
    uw_h = dt("uw", [NEXP, 128, 1024], F32, kind="ExternalInput").ap()
    dw_h = dt("dw", [NEXP, 128, 1024], F32, kind="ExternalInput").ap()
    sg_h = dt("sg", [128, 8 * 512], F32, kind="ExternalInput").ap()
    su_h = dt("su", [128, 8 * 512], F32, kind="ExternalInput").ap()
    sd_h = dt("sd", [128, 4 * 1024], F32, kind="ExternalInput").ap()
    out_d = dt("out", [NT * 128, E], F32, kind="ExternalOutput").ap()
    l1s_d = dt("l1scr", [SGRP, 128, 4096], BF16, kind="Internal").ap()
    l1o_d = dt("l1oscr", [NG, 128, 4096], BF16, kind="Internal").ap()
    x1_d = dt("x1scr", [NT, 128, 1024], F32, kind="Internal").ap()
    wg_d = dt("wgscr", [2, 32, 1024], BF16, kind="Internal").ap()

    octx = ExitStack()
    S = Sched(nc)
    for s in ("dS", "dC", "dC2", "dW0", "dW1", "dP", "dF", "g0", "g1",
              "u0", "u1", "wr0", "wr1", "l10", "l11", "xq0", "xq1",
              "P", "A", "V", "O", "oc",
              "d0", "d1", "d2", "d3", "d4", "d5", "d6", "d7"):
        S.sem(octx, s)

    def SB(ctx, n, shp, d=F32):
        return ctx.enter_context(nc.sbuf_tensor(n, shp, d))

    ident_b = SB(octx, "ident_b", [128, 128], BF16)
    ishift_b = SB(octx, "ishift_b", [64, 256], BF16)
    ones_f = SB(octx, "ones_f", [128, 128], F32)
    ones_b = SB(octx, "ones_b", [128, 128], BF16)
    maskp_s = SB(octx, "maskp_s", [128, 256], F32)
    bias_s = SB(octx, "bias_s", [128, 512], F32)
    rb_s = SB(octx, "rb_s", [1, 32], F32)
    scal_s = SB(octx, "scal_s", [128, 8], F32)
    at_b = SB(octx, "at_b", [128, 1024], BF16)
    bqkv_b = SB(octx, "bqkv_b", [128, 8 * 128], BF16)
    bq_b = SB(octx, "bq_b", [128, 8 * 8], BF16)
    xt = [SB(octx, "xt0", [128, 1024]), SB(octx, "xt1", [128, 1024])]
    lnb = [SB(octx, "lnb0", [128, 1024], BF16), SB(octx, "lnb1", [128, 1024], BF16)]
    tmp_f = SB(octx, "tmpf", [128, 1024])
    lnscr = [dict(bn6=SB(octx, f"bn6_{i}", [128, 12]), mv=SB(octx, f"mv_{i}", [128, 2]),
                  sd=SB(octx, f"sd_{i}", [128, 1]), rs=SB(octx, f"rs_{i}", [128, 1]),
                  busy_v=None, busy_a=None) for i in range(2)]
    pA = octx.enter_context(nc.psum_tensor("pA", [128, 1024], F32))
    pB = octx.enter_context(nc.psum_tensor("pB", [128, 1024], F32))
    pC = octx.enter_context(nc.psum_tensor("pC", [128, 1024], F32))
    pT = octx.enter_context(nc.psum_tensor("pT", [128, 2048], BF16))

    ev_c = {}
    for key, dst, src in [("ident", ident_b, ident_d), ("ishift", ishift_b, ishift_d),
                          ("onesb", ones_b, ones_d), ("at", at_b, at_h),
                          ("bqkv", bqkv_b, bqkv_h), ("bq", bq_b, bq_h)]:
        S.dma("g", dst[:, :], src, sem="dC")
    for key, dst, src in [("ones", ones_f, ones_d), ("maskp", maskp_s, maskp_d),
                          ("bias", bias_s, bias_d), ("rb", rb_s, rb_h),
                          ("scal", scal_s, scal_d)]:
        S.dma("s", dst[:, :], src, sem="dC2")
    for key in ("ident", "ishift", "onesb", "at", "bqkv", "bq"):
        ev_c[key] = ("dC", S.cnt["dC"])
    for key in ("ones", "maskp", "bias", "rb", "scal"):
        ev_c[key] = ("dC2", S.cnt["dC2"])

    xt_free = [None, None]
    lnb_free = [None, None]
    pT_free = [None, None]
    ln_i = [0]

    def emit_ln(x_tensor, x_off, out_ap, x_evt, out_wait=()):
        x_ap = x_tensor[:, x_off:x_off + 1024]
        sc = lnscr[ln_i[0] % 2]
        ln_i[0] += 1
        S.emit("v", lambda h, a=sc["bn6"], b=x_tensor, o=x_off: h.bn_stats(
            a[:, 0:6], b[:, o:o + 512]),
            waits=[x_evt, sc["busy_v"], sc["busy_a"]])
        S.emit("v", lambda h, a=sc["bn6"], b=x_tensor, o=x_off: h.bn_stats(
            a[:, 6:12], b[:, o + 512:o + 1024]))
        S.drain("v")
        ev_mv = S.emit("v", lambda h, a=sc["mv"], b=sc["bn6"]: h.bn_aggr(a[:, :], b[:, :]), inc="V")
        S.emit("a", lambda h, a=sc["sd"], b=sc["mv"]: h.activation(
            out=a[:, :], in_=b[:, 1:2], func=AF.Ln, bias=scal_s[:, 0:1]),
            waits=[ev_mv, ev_c["scal"]])
        S.drain("a")
        ev_sd = S.emit("a", lambda h, a=sc["rs"], b=sc["sd"]: h.activation(
            out=a[:, :], in_=b[:, :], func=AF.Exp, scale=scal_s[:, 4:5]), inc="A")
        ev = S.emit("v", lambda h, o=out_ap, x=x_ap, m=sc["mv"], r=sc["rs"]: h.tensor_scalar(
            out=o, in0=x, scalar1=m[:, 0:1], scalar2=r[:, 0:1],
            op0=ALU.subtract, op1=ALU.mult), waits=list(out_wait) + [ev_sd], inc="V")
        sc["busy_v"] = ev
        sc["busy_a"] = ev_sd
        return ev

    def emit_ln_transpose(src_dram_row, dst_tensor, dst_off):
        i = ln_i[0] % 2
        ev_x = S.dma("s", xt[i][:, :], src_dram_row, waits=[xt_free[i]],
                     sem="xq%d" % i)
        ev_ln = emit_ln(xt[i], 0, lnb[i][:, :], ev_x, out_wait=[lnb_free[i]])
        xt_free[i] = ev_ln
        th = ln_i[0] % 2
        ev_tr = None
        for c in range(8):
            w = [ev_ln, ev_c["ident"]] + ([pT_free[th]] if c == 0 else [])
            ev_tr = S.emit("t", lambda h, c=c, i=i, th=th: h.transpose(
                pT[:, th * 1024 + c * 128: th * 1024 + (c + 1) * 128],
                in_=lnb[i][:, c * 128:(c + 1) * 128], identity=ident_b[:, :]),
                waits=w, inc="P")
        lnb_free[i] = ev_tr
        ev_cp = S.emit("a", lambda h, th=th, dst=mkap(dst_tensor, dst_off,
                       [[dst_tensor.shape[1], 128],
                        [dst_tensor.shape[1] // 8, 8], [1, 128]]): h.activation(
            out=dst, in_=pT[:, th * 1024:(th + 1) * 1024].rearrange("p (c n) -> p c n", n=128),
            func=AF.Copy), waits=[ev_tr], inc="A")
        pT_free[th] = ev_cp
        return ev_cp

    ab_ctx = ExitStack()
    O_sb = SB(ab_ctx, "O_sb", [128, 8 * 2048], BF16)
    a_ctx = ExitStack()
    t1kv = SB(a_ctx, "t1kv", [128, SGRP * 512], BF16)
    t1q = SB(a_ctx, "t1q", [8, NG * 512], BF16)
    l1t = [SB(a_ctx, "l1t0", [128, 4096], BF16), SB(a_ctx, "l1t1", [128, 4096], BF16)]
    wk_s = SB(a_ctx, "wk_s", [128, 8 * 512], BF16)
    wq_s = wk_s
    wv_s = SB(a_ctx, "wv_s", [128, 8 * 512], BF16)
    K_sb = SB(a_ctx, "K_sb", [128, 4 * 4096], BF16)
    V_sb = SB(a_ctx, "V_sb", [128, 8 * 32 * 66], BF16)
    Q_sb = SB(a_ctx, "Q_sb", [128, 8 * 2048], BF16)
    Pt = [SB(a_ctx, "Pt%d" % i, [128, 512], BF16) for i in range(4)]
    rec_s = SB(a_ctx, "rec_s", [128, 512], BF16)
    lnd_s = SB(a_ctx, "lnd_s", [128, 512])
    bc_s = SB(a_ctx, "bc_s", [64, 512], BF16)
    On_s = SB(a_ctx, "On_s", [64, 512], BF16)

    l1t_free = [None, None]
    ev_t1_seq = [None] * SGRP
    ev_t1_own = [None] * NG
    pfree = {k: None for k in ("A0", "A1", "B0", "B1", "C0", "C1")}
    PREG = {"A0": (pA, 0), "A1": (pA, 512), "B0": (pB, 0), "B1": (pB, 512),
            "C0": (pC, 0), "C1": (pC, 512)}

    ev_qz = S.emit("v", lambda h: h.memset(Q_sb[:, :], 0.0), inc="V")
    ev_vones = S.emit("v", lambda h: h.memset(
        V_sb[:, :].rearrange("p (s n) -> p s n", n=66)[:, :, 64:66], 1.0), inc="V")

    def emit_weight_half(hh, extra_wait):
        ev = None
        for dst, src in ((wk_s, wk_h), (wv_s, wv_h)):
            for c in range(8):
                ev = S.dma("g", dst[:, c * 512:(c + 1) * 512],
                           src[:, c * 1024 + hh * 512: c * 1024 + hh * 512 + 512],
                           waits=extra_wait, sem="dW%d" % hh)
        return ev

    def emit_qkv_half(hh, ev_w, first):
        kv_last, q_last = None, None
        acc_regs = ["A0", "A1", "B0", "B1"]
        for g in range(SGRP):
            gb = g % 2
            if first:
                subev = []
                for s in range(4):
                    subev.append(emit_ln_transpose(
                        xseq[(g * 4 + s) * 128:(g * 4 + s + 1) * 128, :],
                        l1t[gb], s * 128))
                ev_l1 = subev[-1]
                S.dma("g", l1s_d[g], l1t[gb][:, :], waits=subev)
                for ic in range(8):
                    w = [ev_l1, ev_c["bqkv"]] + ([pfree["C0"]] if ic == 0 else [])
                    S.emit("t", lambda h, ic=ic, gb=gb: h.matmul(
                        pC[:, 0:512], lhsT=bqkv_b[:, ic * 128:(ic + 1) * 128],
                        rhs=l1t[gb][:, ic * 512:(ic + 1) * 512],
                        start=(ic == 0), stop=(ic == 7)), waits=w,
                        inc="P" if ic == 7 else None)
                ev_t1p = ("P", S.cnt["P"])
                S.emit("a", lambda h, g=g: h.activation(
                    out=t1kv[32:40, g * 512:(g + 1) * 512], in_=pC[32:40, 0:512],
                    func=AF.Copy, scale=scal_s[32:40, 1:2]),
                    waits=[ev_t1p, ev_c["scal"]], inc="A")
                ev_t1_seq[g] = S.emit("a", lambda h, g=g: h.activation(
                    out=t1kv[64:72, g * 512:(g + 1) * 512], in_=pC[64:72, 0:512],
                    func=AF.Copy, scale=scal_s[64:72, 1:2]), inc="A")
                pfree["C0"] = ev_t1_seq[g]
            else:
                ev_l1 = S.dma("s", l1t[gb][:, :], l1s_d[g],
                              waits=[l1t_free[gb], ("dS", scr_full[0])],
                              sem="l1%d" % gb)
            for c in range(4):
                pp, off = PREG[acc_regs[c]]
                for ic in range(8):
                    w = [ev_l1, ev_w] + ([pfree[acc_regs[c]]] if ic == 0 else [])
                    S.emit("t", lambda h, c=c, ic=ic, gb=gb, pp=pp, off=off: h.matmul(
                        pp[:, off:off + 512],
                        lhsT=wk_s[:, ic * 512 + c * 128: ic * 512 + (c + 1) * 128],
                        rhs=l1t[gb][:, ic * 512:(ic + 1) * 512],
                        start=(ic == 0), stop=False), waits=w)
                S.emit("t", lambda h, c=c, g=g, hh=hh, pp=pp, off=off: h.matmul(
                    pp[:, off:off + 512],
                    lhsT=at_b[32:40, hh * 512 + c * 128: hh * 512 + (c + 1) * 128],
                    rhs=t1kv[32:40, g * 512:(g + 1) * 512],
                    start=False, stop=True),
                    waits=[ev_t1_seq[g], ev_c["at"]], inc="P")
                ev_kp = ("P", S.cnt["P"])
                kv_last = S.emit("a", lambda h, c=c, g=g, pp=pp, off=off: h.activation(
                    out=K_sb[:, c * 4096 + g * 512: c * 4096 + (g + 1) * 512],
                    in_=pp[:, off:off + 512], func=AF.Copy), waits=[ev_kp], inc="A")
                pfree[acc_regs[c]] = kv_last
            for s in range(4):
                kt = g * 4 + s
                vr = "C0" if s % 2 == 0 else "C1"
                pp, off = PREG[vr]
                for ic in range(8):
                    w = [ev_l1, ev_w] + ([pfree[vr]] if ic == 0 else [])
                    S.emit("t", lambda h, s=s, ic=ic, gb=gb, pp=pp, off=off: h.matmul(
                        pp[:, off:off + 512],
                        lhsT=l1t[gb][:, ic * 512 + s * 128: ic * 512 + (s + 1) * 128],
                        rhs=wv_s[:, ic * 512:(ic + 1) * 512],
                        start=(ic == 0), stop=False), waits=w)
                S.emit("t", lambda h, g=g, s=s, hh=hh, pp=pp, off=off: h.matmul(
                    pp[:, off:off + 512],
                    lhsT=t1kv[64:72, (g * 4 + s) * 128:(g * 4 + s + 1) * 128],
                    rhs=at_b[64:72, hh * 512:(hh + 1) * 512],
                    start=False, stop=True),
                    waits=[ev_t1_seq[g], ev_c["at"]], inc="P")
                ev_vp = ("P", S.cnt["P"])
                dst = mkap(V_sb, kt * 66, [[8 * 32 * 66, 128], [32 * 66, 8], [1, 64]])
                kv_last = S.emit("a", lambda h, pp=pp, off=off, dst=dst: h.activation(
                    out=dst, in_=pp[:, off:off + 512].rearrange("p (c n) -> p c n", n=64),
                    func=AF.Copy), waits=[ev_vp, ev_vones], inc="A")
                pfree[vr] = kv_last
            l1t_free[gb] = ("P", S.cnt["P"])
        ev_wq = None
        for c in range(8):
            ev_wq = S.dma("g", wq_s[:, c * 512:(c + 1) * 512],
                          wq_h[:, c * 1024 + hh * 512: c * 1024 + hh * 512 + 512],
                          waits=[("P", S.cnt["P"])], sem="dW%d" % hh)
        for og in range(NG):
            gb = g % 2
            if first:
                subev = []
                for s in range(4):
                    subev.append(emit_ln_transpose(
                        xown[(og * 4 + s) * 128:(og * 4 + s + 1) * 128, :],
                        l1t[gb], s * 128))
                ev_l1 = subev[-1]
                S.dma("g", l1o_d[og], l1t[gb][:, :], waits=subev)
                for ic in range(8):
                    w = [ev_l1, ev_c["bq"]] + ([pfree["C0"]] if ic == 0 else [])
                    S.emit("t", lambda h, ic=ic, gb=gb: h.matmul(
                        pC[0:8, 0:512], lhsT=bq_b[:, ic * 8:(ic + 1) * 8],
                        rhs=l1t[gb][:, ic * 512:(ic + 1) * 512],
                        start=(ic == 0), stop=(ic == 7)), waits=w,
                        inc="P" if ic == 7 else None)
                ev_qt1p = ("P", S.cnt["P"])
                ev_t1_own[og] = S.emit("a", lambda h, og=og: h.activation(
                    out=t1q[0:8, og * 512:(og + 1) * 512], in_=pC[0:8, 0:512],
                    func=AF.Copy, scale=scal_s[0:8, 1:2]),
                    waits=[ev_qt1p, ev_c["scal"]], inc="A")
                pfree["C0"] = ev_t1_own[og]
            else:
                ev_l1 = S.dma("s", l1t[gb][:, :], l1o_d[og],
                              waits=[l1t_free[gb], ("dS", scr_full[0])],
                              sem="l1%d" % gb)
            for c in range(4):
                pp, off = PREG[acc_regs[c]]
                for ic in range(8):
                    w = [ev_l1, ev_wq] + ([pfree[acc_regs[c]]] if ic == 0 else [])
                    S.emit("t", lambda h, c=c, ic=ic, gb=gb, pp=pp, off=off: h.matmul(
                        pp[:, off:off + 512],
                        lhsT=wq_s[:, ic * 512 + c * 128: ic * 512 + (c + 1) * 128],
                        rhs=l1t[gb][:, ic * 512:(ic + 1) * 512],
                        start=(ic == 0), stop=False), waits=w)
                S.emit("t", lambda h, c=c, og=og, hh=hh, pp=pp, off=off: h.matmul(
                    pp[:, off:off + 512],
                    lhsT=at_b[0:8, hh * 512 + c * 128: hh * 512 + (c + 1) * 128],
                    rhs=t1q[0:8, og * 512:(og + 1) * 512],
                    start=False, stop=True),
                    waits=[ev_t1_own[og], ev_c["at"]], inc="P")
                ev_qp = ("P", S.cnt["P"])
                S.emit("a", lambda h, c=c, og=og, pp=pp, off=off: h.activation(
                    out=Q_sb[0:64, (2 * c) * 2048 + og * 512: (2 * c) * 2048 + (og + 1) * 512],
                    in_=pp[0:64, off:off + 512], func=AF.Copy),
                    waits=[ev_qp, ev_qz], inc="A")
                q_last = S.emit("a", lambda h, c=c, og=og, pp=pp, off=off: h.activation(
                    out=Q_sb[64:128, (2 * c + 1) * 2048 + og * 512: (2 * c + 1) * 2048 + (og + 1) * 512],
                    in_=pp[64:128, off:off + 512], func=AF.Copy), inc="A")
                pfree[acc_regs[c]] = q_last
            l1t_free[gb] = ("P", S.cnt["P"])
        return kv_last, q_last

    def emit_attn_half(hh, ev_ready):
        Pt_free = [None, None, None, None]
        SREG = [("C0", pC, 0), ("C1", pC, 512), ("B0", pB, 0)]
        state = dict(rec_free=None, bc_free=None, On_free=None, chunk_free=None)

        def emit_norm(gq, hl, ev_pv):
            cl, hp = hl // 2, hl % 2
            o_reg = "A0" if hp == 0 else "A1"
            ocol = (hh * 4 + cl) * 2048 + gq * 512
            ev_ld = S.emit("a", lambda h, hp=hp: h.activation(
                out=lnd_s[64:65, :], in_=pA[64:65, hp * 512:(hp + 1) * 512],
                func=AF.Ln), waits=[ev_pv, state["rec_free"]], inc="A")
            ev_rec = S.emit("a", lambda h: h.activation(
                out=rec_s[64:65, :], in_=lnd_s[64:65, :], func=AF.Exp,
                scale=scal_s[64:65, 3:4]), waits=[ev_c["scal"]], inc="A")
            ev_bc = S.emit("t", lambda h: h.matmul(
                pB[:, 512:1024], lhsT=ones_b[64:65, 0:128], rhs=rec_s[64:65, 0:512],
                start=True, stop=True),
                waits=[ev_rec, ev_c["onesb"], pfree["B1"]], inc="P")
            state["rec_free"] = ev_bc
            ev_bce = S.emit("v", lambda h: h.tensor_copy(
                bc_s[:, :], pB[0:64, 512:1024]),
                waits=[ev_bc, state["bc_free"]], inc="V")
            pfree["B1"] = ev_bce
            if hp == 0:
                ev_on = S.emit("v", lambda h, ocol=ocol: h.tensor_tensor(
                    out=O_sb[0:64, ocol:ocol + 512], in0=pA[0:64, 0:512],
                    in1=bc_s[:, :], op=ALU.mult), waits=[ev_bce], inc="V")
                state["bc_free"] = ev_on
                pfree["A0"] = ev_on
            else:
                ev_on = S.emit("v", lambda h: h.tensor_tensor(
                    out=On_s[:, :], in0=pA[0:64, 512:1024],
                    in1=bc_s[:, :], op=ALU.mult),
                    waits=[ev_bce, state["On_free"]], inc="V")
                state["bc_free"] = ev_on
                pfree["A1"] = ev_on
                state["On_free"] = S.dma(
                    "s", O_sb[64:128, ocol:ocol + 512], On_s[0:64, :],
                    waits=[ev_on], sem="oc")
            return state["On_free"]

        kt_i = [0]
        pt_i = [0]

        def emit_s(gq, hl, kt, nkt):
            h_g = hh * 8 + hl
            cl, hp = hl // 2, hl % 2
            cs = 128 * max(0, kt // 2 - 4 * gq)
            sr, sp, soff = SREG[kt_i[0] % 3]
            kt_i[0] += 1
            w = [ev_ready, pfree[sr]]
            ev_s = S.emit("t", lambda h, hl=hl, cl=cl, kt=kt, gq=gq, cs=cs, sp=sp, soff=soff: h.matmul(
                sp[:, soff + cs: soff + 512],
                lhsT=K_sb[:, cl * 4096 + kt * 128: cl * 4096 + (kt + 1) * 128],
                rhs=Q_sb[:, hl * 2048 + gq * 512 + cs: hl * 2048 + (gq + 1) * 512],
                start=True, stop=True), waits=w, inc="P")
            ev_m = None
            if kt >= 8 * gq:
                jm = kt // 2 - 4 * gq
                ev_m = S.emit("v", lambda h, sp=sp, soff=soff, jm=jm, kt=kt: h.tensor_tensor(
                    out=sp[:, soff + jm * 128: soff + (jm + 1) * 128],
                    in0=sp[:, soff + jm * 128: soff + (jm + 1) * 128],
                    in1=maskp_s[:, (kt % 2) * 128:(kt % 2 + 1) * 128],
                    op=ALU.add), waits=[ev_s, ev_c["maskp"]], inc="V")
            pi = pt_i[0] % 4
            pt_i[0] += 1
            ev_e = S.emit("a", lambda h, pi=pi, cs=cs, h_g=h_g, kt=kt, sp=sp, soff=soff: h.activation(
                out=Pt[pi][:, cs:512], in_=sp[:, soff + cs: soff + 512],
                func=AF.Exp, bias=bias_s[:, h_g * 32 + kt: h_g * 32 + kt + 1],
                scale=scal_s[:, 2:3]),
                waits=[ev_s, ev_m, ev_c["bias"], ev_c["scal"], Pt_free[pi]], inc="A")
            pfree[sr] = ev_e
            return (ev_e, pi)

        def emit_pv(gq, hl, kt, nkt, es):
            ev_e, pi = es
            hp = hl % 2
            o_reg = "A0" if hp == 0 else "A1"
            cs = 128 * max(0, kt // 2 - 4 * gq)
            w2 = [ev_e] + ([pfree[o_reg]] if kt == 0 else [])
            ev_pv = S.emit("t", lambda h, hl=hl, kt=kt, cs=cs, pi=pi, hp=hp, nkt=nkt: h.matmul(
                pA[0:65, hp * 512 + cs:(hp + 1) * 512],
                lhsT=V_sb[:, (hl * 32 + kt) * 66: (hl * 32 + kt) * 66 + 65],
                rhs=Pt[pi][:, cs:512],
                start=(kt == 0), stop=(kt == nkt - 1)), waits=w2, inc="P")
            Pt_free[pi] = ev_pv
            return ev_pv

        LAG = 3
        visits = []
        for gq in range(NG):
            for hl in range(8):
                nkt = 8 * gq + 8
                for kt in range(nkt):
                    visits.append((gq, hl, kt, nkt))
        ev_es = {}
        norm_due = []
        ev_oc = None
        n = len(visits)
        for i in range(n + LAG):
            if i < n:
                ev_es[i] = emit_s(*visits[i])
            j = i - LAG
            if j >= 0:
                gq, hl, kt, nkt = visits[j]
                ev_pv = emit_pv(gq, hl, kt, nkt, ev_es.pop(j))
                if kt == nkt - 1:
                    norm_due.append((i + 2, (gq, hl, ev_pv)))
            while norm_due and norm_due[0][0] <= i:
                _, args = norm_due.pop(0)
                ev_oc = emit_norm(*args) or ev_oc
        while norm_due:
            _, args = norm_due.pop(0)
            ev_oc = emit_norm(*args) or ev_oc
        return [ev_oc, ("V", S.cnt["V"])]

    scr_full = [0]
    ev_w0 = emit_weight_half(0, [])
    kv0, q0 = emit_qkv_half(0, ev_w0, first=True)
    ev_oc0 = emit_attn_half(0, q0)
    scr_full[0] = S.cnt["dS"]
    ev_w1 = emit_weight_half(1, [("P", S.cnt["P"])])
    kv1, q1 = emit_qkv_half(1, ev_w1, first=False)
    ev_oc1 = emit_attn_half(1, q1)

    a_ctx.close()
    S.set_floor()
    b_ctx = ExitStack()
    wp_s = SB(b_ctx, "wp_s", [128, 8 * 1024], BF16)
    at_f = [SB(b_ctx, "atf0", [128, 1024]), SB(b_ctx, "atf1", [128, 1024])]
    xo_f = [SB(b_ctx, "xof0", [128, 1024]), SB(b_ctx, "xof1", [128, 1024])]
    x1t = [SB(b_ctx, "x1t0", [128, 1024]), SB(b_ctx, "x1t1", [128, 1024])]
    ev_wp = None
    for ic in range(8):
        ev_wp = S.dma("g", wp_s[:, ic * 1024:(ic + 1) * 1024],
                      wp_h[:, ic * 1024:(ic + 1) * 1024], sem="dP")
    pacc_free = [None, None]
    at_free = [None, None]
    xo_free = [None, None]
    x1t_free = [None, None]
    ev_x1 = [None] * NT
    for j in range(NT):
        pb = j % 2
        pp = pA if pb == 0 else pB
        ev_xo = S.dma("s", xo_f[pb][:, :], xown[j * 128:(j + 1) * 128, :],
                      waits=[xo_free[pb]], sem="xq%d" % pb)
        for ic in range(8):
            for hf in range(2):
                w = list(ev_oc1) + [ev_wp] + ([pacc_free[pb]] if (ic == 0 and hf == 0) else [])
                S.emit("t", lambda h, ic=ic, hf=hf, j=j, pp=pp: h.matmul(
                    pp[:, hf * 512:(hf + 1) * 512],
                    lhsT=O_sb[:, ic * 2048 + j * 128:ic * 2048 + (j + 1) * 128],
                    rhs=wp_s[:, ic * 1024 + hf * 512: ic * 1024 + (hf + 1) * 512],
                    start=(ic == 0), stop=(ic == 7)),
                    waits=w, inc="P" if (ic == 7 and hf == 1) else None)
        ev_wpp = ("P", S.cnt["P"])
        ev_ae = S.emit("a", lambda h, pb=pb, pp=pp: h.activation(
            out=at_f[pb][:, :], in_=pp[:, :], func=AF.Copy),
            waits=[ev_wpp, at_free[pb]], inc="A")
        pacc_free[pb] = ev_ae
        ev_ln = emit_ln(at_f[pb], 0, tmp_f[:, :], ev_ae)
        at_free[pb] = ev_ln
        S.drain("v")
        ev_x1t = S.emit("v", lambda h, pb=pb: h.tensor_tensor(
            out=x1t[pb][:, :], in0=tmp_f[:, :], in1=xo_f[pb][:, :], op=ALU.add),
            waits=[ev_xo, x1t_free[pb]], inc="V")
        xo_free[pb] = ev_x1t
        ev_x1[j] = S.dma("g", x1_d[j], x1t[pb][:, :], waits=[ev_x1t])
        x1t_free[pb] = ev_x1[j]

    x1_full = ("dS", S.cnt["dS"])
    if DEBUG_STOP_AT_X1:
        for j in range(NT):
            S.dma("s", out_d[j * 128:(j + 1) * 128, :], x1_d[j],
                  waits=[ev_x1[j]], sem="O")
        S.emit("s", lambda h: None, waits=[("O", S.cnt["O"])])
        b_ctx.close(); ab_ctx.close()
        with octx, nc.Block() as block:
            S.replay(block)
        return nc

    b_ctx.close()
    ab_ctx.close()
    S.set_floor()
    c_ctx = ExitStack()
    ln2T = [SB(c_ctx, "ln2T0", [128, 8 * 1024], BF16),
            SB(c_ctx, "ln2T1", [128, 8 * 1024], BF16)]
    rw_s = SB(c_ctx, "rw_s", [128, 8 * 32], BF16)
    sgw_s = SB(c_ctx, "sgw_s", [128, 8 * 512], BF16)
    suw_s = SB(c_ctx, "suw_s", [128, 8 * 512], BF16)
    sdw_s = SB(c_ctx, "sdw_s", [128, 4 * 1024], BF16)
    gw_s = [SB(c_ctx, "gw_s0", [128, 1024], BF16), SB(c_ctx, "gw_s1", [128, 1024], BF16)]
    uw_s = [SB(c_ctx, "uw_s0", [128, 1024], BF16), SB(c_ctx, "uw_s1", [128, 1024], BF16)]
    dw_s = [SB(c_ctx, f"dw_s{i}", [128, 1024], BF16) for i in range(8)]
    sc_f = SB(c_ctx, "sc_f", [128, 32])
    cur_f = SB(c_ctx, "cur_f", [128, 32])
    msk_f = SB(c_ctx, "msk_f", [128, 32])
    m_f = SB(c_ctx, "m_f", [128, 1])
    wu_f = SB(c_ctx, "wu_f", [128, 32])
    w_b = SB(c_ctx, "w_b", [128, 32], BF16)
    wfm = SB(c_ctx, "wfm", [32, 1024], BF16)
    wrow = [SB(c_ctx, "wrow0", [1, 1024], BF16), SB(c_ctx, "wrow1", [1, 1024], BF16)]
    sg_b = SB(c_ctx, "sg_b", [128, 1024], BF16)
    wb_b = SB(c_ctx, "wb_b", [128, 1024], BF16)
    h1_b = SB(c_ctx, "h1_b", [128, 1024], BF16)
    h_blk = SB(c_ctx, "h_blk", [128, 8 * 1024], BF16)
    hsh = SB(c_ctx, "hsh", [128, 4 * 1024], BF16)
    moe = SB(c_ctx, "moe", [128, 8 * 1024])
    out_f = [SB(c_ctx, "outf0", [128, 1024]), SB(c_ctx, "outf1", [128, 1024])]
    wdbg = SB(c_ctx, "wdbg", [128, 32])
    x1e = [SB(c_ctx, "x1e0", [128, 1024]), SB(c_ctx, "x1e1", [128, 1024])]

    S.dma("g", rw_s[:, :], rw_h, sem="dF")
    S.dma("g", sgw_s[:, :], sg_h, sem="dF")
    S.dma("g", suw_s[:, :], su_h, sem="dF")
    S.dma("g", sdw_s[:, :], sd_h, sem="dF")
    ev_rw = ev_sg = ev_su = ev_sd = ("dF", S.cnt["dF"])

    def l2rhs(G, ic, hf):
        return mkap(ln2T[G], ic * 1024 + hf * 512, [[8192, 128], [1, 512]])

    ev_moe = [None] * 8
    w_b_free = [None]
    gw_free = [None, None]
    uw_free = [None, None]
    dw_free = [None] * 8
    wrow_free = [None, None]
    h_blk_free = [None] * 8
    sg_free, wb_free, h1_free = None, None, None
    x1e_free = [None, None]
    out_free = [None, None]
    topk_free_h = [None]
    evl2 = {0: [None] * 8, 1: [None] * 8}
    evwfm = {0: [], 1: []}
    evwsc = {}

    def prep1(G, sts):
        for st in sts:
            j = G * 8 + st
            pb = st % 2
            ev_x1l = S.dma("s", x1e[pb][:, :], x1_d[j], waits=[x1_full, x1e_free[pb]],
                           sem="xq%d" % pb)
            ev_ln = emit_ln(x1e[pb], 0, lnb[pb][:, :], ev_x1l,
                            out_wait=[lnb_free[pb]])
            x1e_free[pb] = ev_ln
            th = st % 2
            ev_tr = None
            for c in range(8):
                w = [ev_ln] + ([pT_free[th]] if c == 0 else [])
                ev_tr = S.emit("t", lambda h, c=c, pb=pb, th=th: h.transpose(
                    pT[:, th * 1024 + c * 128: th * 1024 + (c + 1) * 128],
                    in_=lnb[pb][:, c * 128:(c + 1) * 128], identity=ident_b[:, :]),
                    waits=w, inc="P")
            lnb_free[pb] = ev_tr
            ev_l2_last = S.emit("a", lambda h, th=th, dst=mkap(
                ln2T[G], st * 128, [[8192, 128], [1024, 8], [1, 128]]): h.activation(
                out=dst, in_=pT[:, th * 1024:(th + 1) * 1024].rearrange("p (c n) -> p c n", n=128),
                func=AF.Copy), waits=[ev_tr], inc="A")
            evl2[G][st] = ev_l2_last
            pT_free[th] = ev_l2_last

    def prep2(G, sts):
        for st in sts:
            th = st % 2
            for ic in range(8):
                w = [evl2[G][st], ev_rw] + ([pfree["C0"]] if ic == 0 else [])
                S.emit("t", lambda h, ic=ic, st=st, G=G: h.matmul(
                    pC[:, 0:32],
                    lhsT=mkap(ln2T[G], ic * 1024 + st * 128, [[8192, 128], [1, 128]]),
                    rhs=rw_s[:, ic * 32:(ic + 1) * 32],
                    start=(ic == 0), stop=False), waits=w)
            S.emit("t", lambda h: h.matmul(
                pC[:, 0:32], lhsT=ones_f[0:1, 0:128], rhs=rb_s[0:1, :],
                start=False, stop=True), waits=[ev_c["rb"], ev_c["ones"]], inc="P")
            ev_rt = ("P", S.cnt["P"])
            ev_nm = S.emit("v", lambda h: h.reduce_max(
                m_f[:, :], pC[:, 0:32], axis=AX.X, negate=True),
                waits=[ev_rt, topk_free_h[0]], inc="V")
            ev_sc = S.emit("a", lambda h: h.activation(
                out=sc_f[:, :], in_=pC[:, 0:32], func=AF.Exp, bias=m_f[:, 0:1]),
                waits=[ev_nm], inc="A")
            pfree["C0"] = ev_sc
            S.emit("v", lambda h: h.tensor_copy(cur_f[:, :], sc_f[:, :]), waits=[ev_sc])
            for it in range(3):
                S.drain("v")
                S.emit("v", lambda h: h.reduce_max(m_f[:, :], cur_f[:, :], axis=AX.X),
                       waits=[ev_sc])
                S.drain("v")
                S.emit("v", lambda h: h.tensor_scalar(
                    out=msk_f[:, :], in0=cur_f[:, :], scalar1=m_f[:, 0:1],
                    scalar2=1e30, op0=ALU.is_ge, op1=ALU.mult))
                S.drain("v")
                S.emit("v", lambda h: h.tensor_sub(cur_f[:, :], cur_f[:, :], msk_f[:, :]))
            S.drain("v")
            S.emit("v", lambda h: h.reduce_max(m_f[:, :], cur_f[:, :], axis=AX.X))
            S.drain("v")
            S.emit("v", lambda h: h.tensor_scalar(
                out=msk_f[:, :], in0=sc_f[:, :], scalar1=m_f[:, 0:1],
                scalar2=None, op0=ALU.is_ge))
            S.drain("v")
            S.emit("v", lambda h: h.tensor_mul(wu_f[:, :], sc_f[:, :], msk_f[:, :]))
            S.drain("v")
            S.emit("v", lambda h: h.reduce_sum(m_f[:, :], wu_f[:, :], axis=AX.X))
            S.drain("v")
            S.emit("v", lambda h: h.reciprocal(m_f[:, :], m_f[:, :]))
            S.drain("v")
            ev_wb = S.emit("v", lambda h: h.tensor_scalar(
                out=w_b[:, :], in0=wu_f[:, :], scalar1=m_f[:, 0:1], scalar2=None,
                op0=ALU.mult), waits=[w_b_free[0]], inc="V")
            topk_free_h[0] = ev_wb
            ev_wt = S.emit("t", lambda h, th=th: h.transpose(
                pT[0:32, th * 1024: th * 1024 + 128],
                in_=w_b[:, :], identity=ident_b[:, :]),
                waits=[ev_wb, pT_free[th]], inc="P")
            ev_wf = S.emit("a", lambda h, st=st, th=th: h.activation(
                out=wfm[0:32, st * 128:(st + 1) * 128],
                in_=pT[0:32, th * 1024: th * 1024 + 128],
                func=AF.Copy), waits=[ev_wt], inc="A")
            evwfm[G].append(ev_wf)
            pT_free[th] = ev_wf

    def wstore(G):
        evwsc[G] = S.dma("g", wg_d[G], wfm[:, :], waits=evwfm[G])

    prep1(0, range(8))
    prep2(0, range(8))
    wstore(0)
    if DEBUG_FFN_W:
        prep1(1, range(8))
        prep2(1, range(8))
    for G in range(2):
        if DEBUG_FFN_W:
            continue
        ev_l2_last = evl2[G][7]
        ev_wsc = evwsc[G]
        for blk in range(DEBUG_NBLK):
            ev_h_all = None
            for e8 in range(8):
                e = blk * 8 + e8
                eb = e % 2
                ev_gw = S.dma("g", gw_s[eb][:, :], gw_h[e],
                              waits=[gw_free[eb]], sem="g%d" % eb)
                ev_uw = S.dma("g", uw_s[eb][:, :], uw_h[e],
                              waits=[uw_free[eb]], sem="u%d" % eb)
                ev_dwl = S.dma("g", dw_s[e8][:, :], dw_h[e],
                               waits=[dw_free[e8]], sem="d%d" % e8)
                dw_free[e8] = ev_dwl
                ev_wr = S.dma("s", wrow[eb][0:1, :], wg_d[G][e:e + 1, :],
                              waits=[ev_wsc, wrow_free[eb]], sem="wr%d" % eb)
                for ic in range(8):
                    for hf in range(2):
                        w = [ev_gw, ev_l2_last] + (
                            [pfree["A0"], pfree["A1"]] if (ic == 0 and hf == 0) else [])
                        S.emit("t", lambda h, ic=ic, hf=hf, eb=eb, G=G: h.matmul(
                            pA[:, hf * 512:(hf + 1) * 512],
                            lhsT=gw_s[eb][:, ic * 128:(ic + 1) * 128],
                            rhs=l2rhs(G, ic, hf), start=(ic == 0), stop=(ic == 7)),
                            waits=w, inc="P" if (ic == 7 and hf == 1) else None)
                ev_g = ("P", S.cnt["P"])
                gw_free[eb] = ev_g
                for ic in range(8):
                    for hf in range(2):
                        w = [ev_uw, ev_l2_last] + (
                            [pfree["B0"], pfree["B1"]] if (ic == 0 and hf == 0) else [])
                        S.emit("t", lambda h, ic=ic, hf=hf, eb=eb, G=G: h.matmul(
                            pB[:, hf * 512:(hf + 1) * 512],
                            lhsT=uw_s[eb][:, ic * 128:(ic + 1) * 128],
                            rhs=l2rhs(G, ic, hf), start=(ic == 0), stop=(ic == 7)),
                            waits=w, inc="P" if (ic == 7 and hf == 1) else None)
                ev_u = ("P", S.cnt["P"])
                uw_free[eb] = ev_u
                for hf in range(2):
                    w = [ev_wr, ev_c["onesb"]] + (
                        [pfree["C0"], pfree["C1"]] if hf == 0 else [])
                    S.emit("t", lambda h, hf=hf, eb=eb: h.matmul(
                        pC[:, hf * 512:(hf + 1) * 512], lhsT=ones_b[0:1, 0:128],
                        rhs=wrow[eb][0:1, hf * 512:(hf + 1) * 512],
                        start=True, stop=True), waits=w,
                        inc="P" if hf == 1 else None)
                ev_wbp = ("P", S.cnt["P"])
                wrow_free[eb] = ev_wbp
                ev_sgb = S.emit("a", lambda h: h.activation(
                    out=sg_b[:, :], in_=pA[:, :], func=AF.Silu),
                    waits=[ev_g, sg_free], inc="A")
                pfree["A0"] = pfree["A1"] = ev_sgb
                ev_wbe = S.emit("a", lambda h: h.activation(
                    out=wb_b[:, :], in_=pC[:, :], func=AF.Copy),
                    waits=[ev_wbp, wb_free], inc="A")
                pfree["C0"] = pfree["C1"] = ev_wbe
                ev_h1 = S.emit("v", lambda h: h.tensor_tensor(
                    out=h1_b[:, :], in0=sg_b[:, :], in1=pB[:, :], op=ALU.mult),
                    waits=[ev_sgb, ev_u, h1_free], inc="V")
                sg_free = ev_h1
                pfree["B0"] = pfree["B1"] = ev_h1
                ev_h = S.emit("v", lambda h, e8=e8: h.tensor_tensor(
                    out=h_blk[:, e8 * 1024:(e8 + 1) * 1024], in0=h1_b[:, :],
                    in1=wb_b[:, :], op=ALU.mult),
                    waits=[ev_h1, ev_wbe, h_blk_free[e8]], inc="V")
                h1_free = ev_h
                wb_free = ev_h
                ev_h_all = ev_h
            dn_regs = [("C0", "C1", pC), ("A0", "A1", pA)]
            for st in range(8):
                r0, r1, pp = dn_regs[st % 2]
                for e8 in range(8):
                    w = [ev_h_all, dw_free[e8]] + (
                        [pfree[r0], pfree[r1]] if e8 == 0 else [])
                    S.emit("t", lambda h, e8=e8, st=st, pp=pp: h.matmul(
                        pp[:, 0:512],
                        lhsT=h_blk[:, e8 * 1024 + st * 128: e8 * 1024 + (st + 1) * 128],
                        rhs=dw_s[e8][:, 0:512],
                        start=(e8 == 0), stop=False), waits=w)
                    S.emit("t", lambda h, e8=e8, st=st, pp=pp: h.matmul(
                        pp[:, 512:1024],
                        lhsT=h_blk[:, e8 * 1024 + st * 128: e8 * 1024 + (st + 1) * 128],
                        rhs=dw_s[e8][:, 512:1024],
                        start=(e8 == 0), stop=(e8 == 7)),
                        inc="P" if e8 == 7 else None)
                ev_dn = ("P", S.cnt["P"])
                if blk == 0:
                    ev_mo = S.emit("v", lambda h, st=st, pp=pp: h.tensor_copy(
                        moe[:, st * 1024:(st + 1) * 1024], pp[:, :]),
                        waits=[ev_dn, ev_moe[st]], inc="V")
                else:
                    ev_mo = S.emit("v", lambda h, st=st, pp=pp: h.tensor_tensor(
                        out=moe[:, st * 1024:(st + 1) * 1024],
                        in0=moe[:, st * 1024:(st + 1) * 1024], in1=pp[:, :],
                        op=ALU.add), waits=[ev_dn, ev_moe[st]], inc="V")
                ev_moe[st] = ev_mo
                pfree[r0] = pfree[r1] = ev_mo
            ev_dn_done = ("P", S.cnt["P"])
            for e8 in range(8):
                h_blk_free[e8] = ev_dn_done
                dw_free[e8] = ev_dn_done
            if G == 0:
                prep1(1, [2 * blk, 2 * blk + 1])
                prep2(1, [2 * blk, 2 * blk + 1])
                if blk == 3:
                    wstore(1)
        for hc in range(0 if DEBUG_NO_SHARED else 4):
            for ic in range(8):
                for hf in range(2):
                    w = [ev_sg, ev_l2_last] + (
                        [pfree["A0"], pfree["A1"]] if (ic == 0 and hf == 0) else [])
                    S.emit("t", lambda h, ic=ic, hf=hf, hc=hc, G=G: h.matmul(
                        pA[:, hf * 512:(hf + 1) * 512],
                        lhsT=sgw_s[:, ic * 512 + hc * 128: ic * 512 + (hc + 1) * 128],
                        rhs=l2rhs(G, ic, hf), start=(ic == 0), stop=(ic == 7)),
                        waits=w, inc="P" if (ic == 7 and hf == 1) else None)
            ev_g = ("P", S.cnt["P"])
            for ic in range(8):
                for hf in range(2):
                    w = [ev_su, ev_l2_last] + (
                        [pfree["B0"], pfree["B1"]] if (ic == 0 and hf == 0) else [])
                    S.emit("t", lambda h, ic=ic, hf=hf, hc=hc, G=G: h.matmul(
                        pB[:, hf * 512:(hf + 1) * 512],
                        lhsT=suw_s[:, ic * 512 + hc * 128: ic * 512 + (hc + 1) * 128],
                        rhs=l2rhs(G, ic, hf), start=(ic == 0), stop=(ic == 7)),
                        waits=w, inc="P" if (ic == 7 and hf == 1) else None)
            ev_u = ("P", S.cnt["P"])
            ev_sgb = S.emit("a", lambda h: h.activation(
                out=sg_b[:, :], in_=pA[:, :], func=AF.Silu),
                waits=[ev_g, sg_free], inc="A")
            pfree["A0"] = pfree["A1"] = ev_sgb
            ev_h = S.emit("v", lambda h, hc=hc: h.tensor_tensor(
                out=hsh[:, hc * 1024:(hc + 1) * 1024], in0=sg_b[:, :], in1=pB[:, :],
                op=ALU.mult), waits=[ev_sgb, ev_u], inc="V")
            sg_free = ev_h
            pfree["B0"] = pfree["B1"] = ev_h
        if DEBUG_NO_SHARED:
            for hc in []: pass
        ev_hsh = ("V", S.cnt["V"])
        dn_regs = [("C0", "C1", pC), ("A0", "A1", pA)]
        for st in range(0 if DEBUG_NO_SHARED else 8):
            r0, r1, pp = dn_regs[st % 2]
            for hc in range(4):
                w = [ev_hsh, ev_sd] + ([pfree[r0], pfree[r1]] if hc == 0 else [])
                S.emit("t", lambda h, hc=hc, st=st, pp=pp: h.matmul(
                    pp[:, 0:512],
                    lhsT=hsh[:, hc * 1024 + st * 128: hc * 1024 + (st + 1) * 128],
                    rhs=sdw_s[:, hc * 1024: hc * 1024 + 512],
                    start=(hc == 0), stop=False), waits=w)
                S.emit("t", lambda h, hc=hc, st=st, pp=pp: h.matmul(
                    pp[:, 512:1024],
                    lhsT=hsh[:, hc * 1024 + st * 128: hc * 1024 + (st + 1) * 128],
                    rhs=sdw_s[:, hc * 1024 + 512: (hc + 1) * 1024],
                    start=(hc == 0), stop=(hc == 3)),
                    inc="P" if hc == 3 else None)
            ev_dn = ("P", S.cnt["P"])
            ev_mo = S.emit("v", lambda h, st=st, pp=pp: h.tensor_tensor(
                out=moe[:, st * 1024:(st + 1) * 1024],
                in0=moe[:, st * 1024:(st + 1) * 1024], in1=pp[:, :], op=ALU.add),
                waits=[ev_dn, ev_moe[st]], inc="V")
            ev_moe[st] = ev_mo
            pfree[r0] = pfree[r1] = ev_mo
        for st in range(8):
            j = G * 8 + st
            pb = st % 2
            if DEBUG_FFN_RAW:
                out_free[pb] = S.dma("s", out_d[j * 128:(j + 1) * 128, :],
                                     moe[:, st * 1024:(st + 1) * 1024],
                                     waits=[ev_moe[st], out_free[pb]], sem="O")
                continue
            ev_x1l = S.dma("s", x1e[pb][:, :], x1_d[j], waits=[x1e_free[pb]],
                           sem="xq%d" % pb)
            ev_ln = emit_ln(moe, st * 1024, tmp_f[:, :], ev_moe[st])
            ev_moe[st] = ev_ln
            S.drain("v")
            ev_o = S.emit("v", lambda h, pb=pb: h.tensor_tensor(
                out=out_f[pb][:, :], in0=tmp_f[:, :], in1=x1e[pb][:, :], op=ALU.add),
                waits=[ev_x1l, out_free[pb]], inc="V")
            x1e_free[pb] = ev_o
            out_free[pb] = S.dma("s", out_d[j * 128:(j + 1) * 128, :], out_f[pb][:, :],
                                 waits=[ev_o], sem="O")
    S.emit("s", lambda h: None, waits=[("O", S.cnt["O"])])

    c_ctx.close()
    with octx, nc.Block() as block:
        S.replay(block)
    return nc


_NC_CACHE = {}


def _chunked(w, cols):
    return np.ascontiguousarray(
        w.reshape(8, 128, cols).transpose(1, 0, 2).reshape(128, 8 * cols))


def kernel(L0_attn_x, L0_attn_Wq, L0_attn_Aq, L0_attn_Bq, L0_attn_Wk, L0_attn_Ak,
           L0_attn_Bk, L0_attn_Wv, L0_attn_Av, L0_attn_Bv, L0_attn_Wp,
           L0_ffn_router_w, L0_ffn_router_bias, L0_ffn_gate_w, L0_ffn_up_w,
           L0_ffn_down_w, L0_ffn_sgate_w, L0_ffn_sup_w, L0_ffn_sdown_w):
    f = lambda a: np.ascontiguousarray(np.asarray(a, dtype=np.float32))
    X = f(L0_attn_x)
    wq = _chunked(f(L0_attn_Wq).T, 1024)
    wk = _chunked(f(L0_attn_Wk).T, 1024)
    wv = _chunked(f(L0_attn_Wv).T, 1024)
    wp = _chunked(f(L0_attn_Wp).T, 1024)
    Mkv = np.zeros((1024, 128), np.float32)
    Mkv[:, 32:40] = f(L0_attn_Bk).T
    Mkv[:, 64:72] = f(L0_attn_Bv).T
    bqkv = _chunked(Mkv, 128)
    bq = _chunked(f(L0_attn_Bq).T, 8)
    aT = np.zeros((128, 1024), np.float32)
    aT[0:8] = f(L0_attn_Aq).T
    aT[32:40] = f(L0_attn_Ak).T
    aT[64:72] = f(L0_attn_Av).T
    rw = _chunked(f(L0_ffn_router_w).T, 32)
    rb = f(L0_ffn_router_bias).reshape(1, 32)
    gw = np.ascontiguousarray(
        f(L0_ffn_gate_w).reshape(32, 128, 8, 128).transpose(0, 3, 2, 1).reshape(32, 128, 1024))
    uw = np.ascontiguousarray(
        f(L0_ffn_up_w).reshape(32, 128, 8, 128).transpose(0, 3, 2, 1).reshape(32, 128, 1024))
    dw = np.ascontiguousarray(f(L0_ffn_down_w).transpose(0, 2, 1))
    sg = _chunked(f(L0_ffn_sgate_w).T, 512)
    su = _chunked(f(L0_ffn_sup_w).T, 512)
    sd = np.ascontiguousarray(
        f(L0_ffn_sdown_w).T.reshape(4, 128, 1024).transpose(1, 0, 2).reshape(128, 4096))
    ident = np.eye(128, dtype=np.float32)
    ones = np.ones((128, 128), np.float32)
    scal = np.zeros((128, 8), np.float32)
    scal[:, 0] = EPS
    scal[:, 1] = 2.0
    scal[:, 2] = 0.125
    scal[:, 3] = -1.0
    scal[:, 4] = -0.5
    ishift = np.zeros((64, 256), np.float32)
    for d in range(64):
        ishift[d, d] = 1.0
        ishift[d, 128 + 64 + d] = 1.0
    slopes = np.array([2.0 ** (-(i + 1) / 2.0) for i in range(H)], np.float64)
    biastab = np.zeros((128, 512), np.float32)
    p = np.arange(128, dtype=np.float64)
    for hh in range(H):
        for kt in range(32):
            biastab[:, hh * 32 + kt] = (-slopes[hh] * (kt * 128 + p)).astype(np.float32)
    tri = np.where(np.arange(128)[:, None] > np.arange(128)[None, :],
                   NEGBIG, 0.0).astype(np.float32)
    full = np.full((128, 128), NEGBIG, np.float32)
    zero = np.zeros((128, 128), np.float32)
    maskr = [np.concatenate([tri, full], axis=1), np.concatenate([zero, tri], axis=1)]

    if "nc" not in _NC_CACHE:
        _NC_CACHE["nc"] = build_program()
    nc = _NC_CACHE["nc"]

    in_maps = []
    for c in range(8):
        b, r = c // 2, c % 2
        xs = np.ascontiguousarray(X[b])
        own_rows = np.concatenate(
            [np.arange((2 * j + r) * 128, (2 * j + r + 1) * 128) for j in range(NT)])
        xo = np.ascontiguousarray(xs[own_rows])
        in_maps.append(dict(
            xseq=xs, xown=xo, maskp=maskr[r], biastab=biastab, identc=ident,
            ishift=ishift, onesc=ones, scal=scal, wq=wq, wk=wk, wv=wv, wp=wp, bqkv=bqkv,
            bq=bq, at=aT, rw=rw, rb=rb, gw=gw, uw=uw, dw=dw, sg=sg, su=su, sd=sd))
    res = run_bass_kernel_spmd(nc, in_maps, core_ids=list(range(8)))
    out = np.empty((B, T, E), np.float32)
    for c in range(8):
        b, r = c // 2, c % 2
        oc = res.results[c]["out"]
        for j in range(NT):
            out[b, (2 * j + r) * 128:(2 * j + r + 1) * 128, :] = oc[j * 128:(j + 1) * 128, :]
    return out
